# version 02
# baseline (speedup 1.0000x reference)
import math
import numpy as np
import ml_dtypes

import concourse.bass as bass
import concourse.mybir as mybir
from concourse.bass_utils import run_bass_kernel_spmd

F32 = mybir.dt.float32
BF16 = mybir.dt.bfloat16
I16 = mybir.dt.int16
I8 = mybir.dt.int8
Act = mybir.ActivationFunctionType
Alu = mybir.AluOpType
AXX = mybir.AxisListType.X

MAGIC = 12582912.0
NCORES = 8
TC = 1024
D = 2048
HID = 8192
H = 16
CH = 128
NT = 8
NB = 2
LN3 = 1.0986122886681098


def _slopes(n):
    start = 2.0 ** (-2.0 ** (-(math.log2(n) - 3)))
    return np.array([start ** (i + 1) for i in range(n)], dtype=np.float64)


def host_constants():
    lam = np.exp(-_slopes(H))
    pos = np.arange(CH, dtype=np.float64)
    decay_q = lam[:, None] ** (pos + 1.0)
    decay_k = lam[:, None] ** (CH - 1.0 - pos)
    diff = pos[:, None] - pos[None, :]
    dmask = np.where(diff >= 0, lam[:, None, None] ** np.maximum(diff, 0), 0.0)
    lam_c = lam ** CH
    lam_sl = lam ** TC
    c = {}
    dmT = np.zeros((CH, H * CH), np.float64)
    for h in range(H):
        dmT[:, h * CH:(h + 1) * CH] = dmask[h].T
    c["dmaskT"] = dmT.astype(ml_dtypes.bfloat16)
    c["decayqcol"] = np.broadcast_to(decay_q.T, (CH, H)).astype(np.float32).copy()
    c["decaykb"] = np.broadcast_to(decay_k.T, (CH, H)).astype(np.float32).copy()
    c["lamcb"] = np.tile(lam_c.astype(np.float32)[None, :], (CH, 1))
    c["identf"] = np.eye(CH, dtype=np.float32)
    c["identb"] = np.eye(CH, dtype=np.float32).astype(ml_dtypes.bfloat16)
    c["ones_col"] = np.ones((CH, 1), np.float32)
    sinit = []
    for core in range(NCORES):
        w = np.zeros((7, H), np.float64)
        for j in range(core):
            w[j] = lam_sl ** (core - 1 - j)
        sinit.append(np.tile(w.reshape(1, 7 * H), (CH, 1)).astype(np.float32))
    c["sinit"] = sinit
    c["wnumel_inv"] = np.array(
        [[1.0 / (HID * D), 1.0 / (HID * D), 1.0 / (D * HID), 0.0]], np.float32)
    return c


class Sem:
    def __init__(self, handle, name):
        self.h = handle
        self.n = 0
        self.name = name


class Prog:

    ENGS = ("t", "v", "s", "g", "y")

    def __init__(self):
        self.q = {e: [] for e in self.ENGS}

    def op(self, eng, fn, inc=None, inck=1, flush=None):
        if flush is not None:
            assert inc is None
            flush.n += 1
            v = flush.n
            self.q[eng].append(lambda E: fn(E).then_inc(flush.h, 1))
            self.q[eng].append(lambda E: E.wait_ge(flush.h, v))
        elif inc is not None:
            inc.n += inck
            self.q[eng].append(lambda E: fn(E).then_inc(inc.h, inck))
        else:
            self.q[eng].append(lambda E: fn(E))

    def wait(self, eng, sem, val=None):
        v = sem.n if val is None else val
        self.q[eng].append(lambda E: E.wait_ge(sem.h, v))

    def barrier(self, sem, engs=ENGS):
        for e in engs:
            self.q[e].append(lambda E: E.nop().then_inc(sem.h, 1))
        sem.n += len(engs)
        tgt = sem.n
        for e in engs:
            self.q[e].append(lambda E: E.wait_ge(sem.h, tgt))


def build_nc(const):
    nc = bass.Bass("TRN2", target_bir_lowering=False, debug=False)

    d_x = nc.dram_tensor("x_slice", [TC, D], F32, kind="ExternalInput").ap()
    d_wqkvT = nc.dram_tensor("wqkvT", [D, 3 * D], F32, kind="ExternalInput").ap()
    d_bqkv_fm = nc.dram_tensor("bqkv_fm", [CH, 48], F32, kind="ExternalInput").ap()
    d_woutT = nc.dram_tensor("woutT", [D, D], F32, kind="ExternalInput").ap()
    d_boutb = nc.dram_tensor("bout_b", [CH, D], F32, kind="ExternalInput").ap()
    d_w1sh = nc.dram_tensor("w1T_shard", [D // 8, HID], F32, kind="ExternalInput").ap()
    d_w2sh = nc.dram_tensor("w2T_shard", [D // 8, HID], F32, kind="ExternalInput").ap()
    d_w3sh = nc.dram_tensor("w3T_shard", [HID // 8, D], F32, kind="ExternalInput").ap()
    d_g12b = nc.dram_tensor("g12_b", [CH, 2 * D], F32, kind="ExternalInput").ap()
    d_g3b = nc.dram_tensor("g3_b", [CH, HID], F32, kind="ExternalInput").ap()
    d_dmaskT = nc.dram_tensor("dmaskT", [CH, H * CH], BF16, kind="ExternalInput").ap()
    d_dqcol = nc.dram_tensor("decayqcol", [CH, H], F32, kind="ExternalInput").ap()
    d_dkcol = nc.dram_tensor("decaykb", [CH, H], F32, kind="ExternalInput").ap()
    d_lamcb = nc.dram_tensor("lamcb", [CH, H], F32, kind="ExternalInput").ap()
    d_identf = nc.dram_tensor("identf", [CH, CH], F32, kind="ExternalInput").ap()
    d_identb = nc.dram_tensor("identb", [CH, CH], BF16, kind="ExternalInput").ap()
    d_sinit = nc.dram_tensor("sinit_w", [CH, 7 * H], F32, kind="ExternalInput").ap()
    d_wninv = nc.dram_tensor("wnumel_inv", [1, 4], F32, kind="ExternalInput").ap()

    d_out = nc.dram_tensor("out_slice", [TC, D], F32, kind="ExternalOutput").ap()
    d_arin = nc.dram_tensor("ar_in", [1, 4], F32)
    d_arout = nc.dram_tensor("ar_out", [1, 4], F32)
    d_t1b = nc.dram_tensor("tern1_sh", [D // 8, HID], BF16)
    d_t2b = nc.dram_tensor("tern2_sh", [D // 8, HID], BF16)
    d_t3b = nc.dram_tensor("tern3_sh", [HID // 8, D], BF16)
    d_w1q = nc.dram_tensor("w1q", [D, HID], BF16)
    d_w2q = nc.dram_tensor("w2q", [D, HID], BF16)
    d_w3q = nc.dram_tensor("w3q", [HID, D], BF16)
    d_Aloc = nc.dram_tensor("A_loc", [CH, H * CH], F32)
    d_Aall = nc.dram_tensor("A_all", [NCORES * CH, H * CH], F32)
    d_y1 = nc.dram_tensor("y1_i8", [TC, D], I8)
    d_y2 = nc.dram_tensor("y2_i8", [TC, D], I8)
    d_intra = nc.dram_tensor("intra_d", [TC, D], BF16)
    d_outx = nc.dram_tensor("outx_d", [TC, D], BF16)
    d_t3 = nc.dram_tensor("t3_d", [TC, HID], BF16)

    A = nc.alloc_sbuf_tensor
    sb = {}

    def S(name, shape, dt):
        sb[name] = A(name, shape, dt)
        return sb[name]

    xnT0 = S("xnT0", [CH, 16 * 512], BF16)
    xnT1 = S("xnT1", [CH, 16 * 512], BF16)
    kvfm = S("kvfm", [CH, 2 * 8192], BF16)
    qfm = S("qfm", [CH, H * TC], BF16)
    cst = S("cst", [CH, 16384], BF16)
    istg = S("istg", [CH, D], BF16)
    wqp = S("wqp", [CH, 2 * 2048], BF16)
    dmk = S("dmk", [CH, H * CH], BF16)
    g12 = S("g12", [CH, 2 * D], BF16)
    attnt = S("attnt", [CH, D], BF16)
    afm = S("afm", [CH, D], BF16)
    smalls = S("smalls", [CH, 6 * CH], BF16)
    identb_t = S("identb_t", [CH, CH], BF16)
    f_sq = S("f_sq", [CH, D], F32)
    f_x = S("f_x", [CH, D], F32)
    f_xn = S("f_xn", [CH, D], F32)
    f_t1 = S("f_t1", [CH, D], F32)
    f_qa = S("f_qa", [CH, D], F32)
    f_vec = S("f_vec", [CH, 56], F32)
    f_st3 = S("f_st3", [CH, 3 * 128], F32)
    identf_t = S("identf_t", [CH, CH], F32)
    bq_t = S("bq_t", [CH, 48], F32)
    dq_t = S("dq_t", [CH, H], F32)
    dk_t = S("dk_t", [CH, H], F32)
    lc_t = S("lc_t", [CH, H], F32)
    si_t = S("si_t", [CH, 7 * H], F32)
    sc_t = S("sc_t", [CH, 16], F32)
    i8a = S("i8a", [CH, D], I8)
    i16a = S("i16a", [CH, 512], I16)

    rem = nc.sbuf_bytes_remaining() if callable(getattr(nc, "sbuf_bytes_remaining", None)) else None

    PF = [nc.alloc_psum_tensor if False else None]
    psum_guards = []
    PF = []
    for i in range(6):
        g = nc.psum_tensor(f"pf{i}", [CH, 512], F32)
        t = g.__enter__()
        psum_guards.append(g)
        PF.append(t)
    PBA_g = nc.psum_tensor("pba", [CH, 1024], BF16)
    PBA = PBA_g.__enter__()
    psum_guards.append(PBA_g)
    PBB_g = nc.psum_tensor("pbb", [CH, 1024], BF16)
    PBB = PBB_g.__enter__()
    psum_guards.append(PBB_g)

    sem_guards = []
    sems = {}

    def mksem(name):
        g = nc.semaphore(name)
        h = g.__enter__()
        sem_guards.append(g)
        sems[name] = Sem(h, name)
        return sems[name]

    dS = {n: mksem("d_" + n) for n in
          ("const", "x", "shard", "tern", "ar", "qkvpan", "y8", "intra",
           "aall", "outx", "woutpan", "ffnpan", "t3w", "t3r", "y8r",
           "w3pan", "outxr", "out", "g3b", "misc")}
    c_t = mksem("c_t")
    c_v = mksem("c_v")
    c_s = mksem("c_s")
    c_g = mksem("c_g")
    cc = mksem("cc")
    bar = mksem("bar")

    P = Prog()

    def dma(eng, dst, src, sem):
        P.op(eng, lambda E, d=dst, s=src: E.dma_start(out=d, in_=s), inc=sem, inck=16)

    dma("y", identf_t[:], d_identf, dS["const"])
    dma("y", identb_t[:], d_identb, dS["const"])
    dma("y", bq_t[:], d_bqkv_fm, dS["const"])
    dma("y", dq_t[:], d_dqcol, dS["const"])
    dma("y", dk_t[:], d_dkcol, dS["const"])
    dma("y", lc_t[:], d_lamcb, dS["const"])
    dma("y", si_t[:], d_sinit, dS["const"])
    dma("g", g12[:], d_g12b, dS["g3b"])
    const_done = dS["const"].n
    g12_done = dS["g3b"].n

    def shard_chunks(dram, wi):
        out = []
        if wi < 2:
            for pc in range(2):
                for cc_ in range(4):
                    out.append(dram[pc * CH:(pc + 1) * CH, cc_ * D:(cc_ + 1) * D])
        else:
            for pc in range(8):
                out.append(dram[pc * CH:(pc + 1) * CH, :])
        return out

    def tern_chunks(dram, wi):
        return shard_chunks(dram, wi)

    shard_in = [shard_chunks(d_w1sh, 0), shard_chunks(d_w2sh, 1), shard_chunks(d_w3sh, 2)]
    tern_out = [tern_chunks(d_t1b, 0), tern_chunks(d_t2b, 1), tern_chunks(d_t3b, 2)]

    ab_cols = f_st3[:, 0:32]
    for wi in range(3):
        for ci, chunk in enumerate(shard_in[wi]):
            dma("y", f_qa[:], chunk, dS["shard"])
            P.wait("s", dS["shard"], dS["shard"].n)
            P.op("s", lambda E, ci=ci, wi=wi: E.activation(
                f_sq[:], f_qa[:], Act.Abs,
                accum_out=ab_cols[:, wi * 8 + ci: wi * 8 + ci + 1]), inc=c_s)
            P.wait("y", c_s, c_s.n)
    P.wait("v", c_s, c_s.n)
    for wi in range(3):
        P.op("v", lambda E, wi=wi: E.reduce_sum(
            f_st3[:, 32 + wi:32 + wi + 1], ab_cols[:, wi * 8:wi * 8 + 8], axis=AXX),
            inc=c_v)
    P.op("v", lambda E: E.memset(sc_t[:, 15:16], 1.0), inc=c_v)
    P.wait("t", c_v, c_v.n)
    P.wait("t", dS["const"], const_done)
    for wi in range(3):
        P.op("t", lambda E, wi=wi: E.matmul(
            PF[0][0:1, wi:wi + 1], f_st3[:, 32 + wi:32 + wi + 1], sc_t[:, 15:16],
            start=True, stop=True), inc=c_t)
    P.wait("v", c_t, c_t.n)
    P.op("v", lambda E: E.memset(sc_t[0:1, 0:4], 0.0), flush=c_v)
    P.op("v", lambda E: E.tensor_copy(sc_t[0:1, 0:3], PF[0][0:1, 0:3]), inc=c_v)
    dma("y", d_arin.ap(), sc_t[0:1, 0:4], dS["ar"])
    P.wait("g", dS["ar"], dS["ar"].n)
    P.op("g", lambda E: E.collective_compute(
        "AllReduce", Alu.add, replica_groups=[list(range(NCORES))],
        ins=[d_arin.ap()], outs=[d_arout.ap()]), inc=cc)
    P.wait("g", cc, cc.n)
    P.op("g", lambda E: E.dma_start(out=sc_t[:, 0:4],
                                    in_=d_arout.ap().partition_broadcast(CH)),
         inc=dS["ar"], inck=16)
    P.op("g", lambda E: E.dma_start(out=sc_t[:, 4:8],
                                    in_=d_wninv.partition_broadcast(CH)),
         inc=dS["ar"], inck=16)
    P.wait("v", dS["ar"], dS["ar"].n)
    P.op("v", lambda E: E.tensor_tensor(sc_t[:, 0:4], sc_t[:, 0:4], sc_t[:, 4:8], op=Alu.mult), flush=c_v)
    P.op("v", lambda E: E.tensor_scalar(sc_t[:, 8:12], sc_t[:, 0:4], 1e-8, None, op0=Alu.add), flush=c_v)
    P.op("v", lambda E: E.reciprocal(sc_t[:, 8:12], sc_t[:, 8:12]), flush=c_v)
    P.op("v", lambda E: E.tensor_scalar(sc_t[:, 12:16], sc_t[:, 0:4], -1.0, 1.0,
                                        op0=Alu.mult, op1=Alu.add), flush=c_v)
    P.op("v", lambda E: E.reciprocal(sc_t[:, 12:16], sc_t[:, 12:16]), flush=c_v)
    P.op("v", lambda E: E.tensor_scalar(sc_t[:, 0:4], sc_t[:, 0:4], 1.0, None,
                                        op0=Alu.add), flush=c_v)
    P.op("v", lambda E: E.tensor_tensor(sc_t[:, 0:4], sc_t[:, 0:4], sc_t[:, 12:16],
                                        op=Alu.mult), inc=c_v)
    P.wait("s", c_v, c_v.n)
    P.op("s", lambda E: E.activation(sc_t[:, 12:16], sc_t[:, 0:4], Act.Ln, scale=1.0), flush=c_s)
    P.op("s", lambda E: E.mul(sc_t[:, 12:16], sc_t[:, 12:16], 0.5), inc=c_s)
    a_ready_s = c_s.n
    r0_ready_v = c_v.n

    for wi in range(3):
        for ci, chunk in enumerate(shard_in[wi]):
            P.wait("y", c_s, c_s.n)
            dma("y", f_qa[:], chunk, dS["shard"])
            P.wait("s", dS["shard"], dS["shard"].n)
            P.op("s", lambda E, wi=wi: E.activation(
                f_sq[:], f_qa[:], Act.Tanh, scale=sc_t[:, 8 + wi:9 + wi]), inc=c_s)
            P.wait("v", c_s, c_s.n)
            P.wait("v", dS["tern"], dS["tern"].n)
            P.op("v", lambda E: E.tensor_scalar(
                i8a[:], f_sq[:], MAGIC, -MAGIC, op0=Alu.add, op1=Alu.add), inc=c_v)
            P.op("v", lambda E: E.tensor_copy(istg[:], i8a[:]), inc=c_v)
            P.wait("g", c_v, c_v.n)
            P.op("g", lambda E, dst=tern_out[wi][ci]: E.dma_start(out=dst, in_=istg[:]),
                 inc=dS["tern"], inck=16)
    P.wait("g", dS["tern"], dS["tern"].n)
    for din, dout in ((d_t1b, d_w1q), (d_t2b, d_w2q), (d_t3b, d_w3q)):
        P.op("g", lambda E, a=din, b=dout: E.collective_compute(
            "AllGather", Alu.bypass, replica_groups=[list(range(NCORES))],
            ins=[a.ap()], outs=[b.ap()]), inc=cc)
    wq_ready = cc.n

    dma("y", dmk[:], d_dmaskT, dS["const"])
    dmask_done = dS["const"].n

    v_s = f_vec[:, 0:1]
    v_ss = f_vec[:, 1:2]
    v_m = f_vec[:, 2:3]
    v_var = f_vec[:, 3:4]
    v_vp = f_vec[:, 4:5]
    v_r = f_vec[:, 5:6]
    v_t0 = f_vec[:, 6:7]
    v_t1 = f_vec[:, 7:8]
    v_t2 = f_vec[:, 8:9]
    v_rr = f_vec[:, 9:10]
    v_mx = f_vec[:, 10:11]
    v_mn = f_vec[:, 11:12]
    v_m1 = f_vec[:, 12:13]
    v_d1 = f_vec[:, 13:14]
    v_c1 = f_vec[:, 14:15]
    invs1 = f_vec[:, 16:24]
    invs2 = f_vec[:, 24:32]
    cu_c = f_vec[:, 32:40]
    cv_c = f_vec[:, 40:48]
    cy3_c = f_vec[:, 16:24]
    c3_c = f_vec[:, 48:56]

    def newton_rsqrt(dst, src_plus):
        P.op("v", lambda E: E.reciprocal(v_t0[:], src_plus), inc=c_v)
        P.wait("s", c_v, c_v.n)
        P.op("s", lambda E: E.activation(dst, v_t0[:], Act.Sqrt), inc=c_s)
        P.wait("v", c_s, c_s.n)
        P.op("v", lambda E: E.tensor_tensor(v_t1[:], dst, dst, op=Alu.mult), flush=c_v)
        P.op("v", lambda E: E.tensor_tensor(v_t1[:], v_t1[:], src_plus, op=Alu.mult), flush=c_v)
        P.op("v", lambda E: E.tensor_scalar(v_t1[:], v_t1[:], -0.5, 1.5,
                                            op0=Alu.mult, op1=Alu.add), flush=c_v)
        P.op("v", lambda E: E.tensor_tensor(dst, dst, v_t1[:], op=Alu.mult), flush=c_v)

    xnT = [xnT0, xnT1]

    def phase1_tile(i):
        b = i // 4
        P.wait("y", c_s, c_s.n)
        P.wait("y", c_v, c_v.n)
        dma("y", f_x[:], d_x[i * CH:(i + 1) * CH, :], dS["x"])
        P.wait("s", dS["x"], dS["x"].n)
        P.wait("s", c_t, c_t.n)
        P.op("s", lambda E: E.activation(f_xn[:], f_x[:],
                                         Act.Square, accum_out=v_ss[:]), inc=c_s)
        P.wait("v", dS["x"], dS["x"].n)
        P.op("v", lambda E: E.reduce_sum(v_s[:], f_x[:], axis=AXX), inc=c_v)
        P.wait("v", c_s, c_s.n)
        P.op("v", lambda E: E.tensor_scalar(v_m[:], v_s[:], 1.0 / D, None, op0=Alu.mult),
             flush=c_v)
        P.op("v", lambda E: E.tensor_tensor(v_t2[:], v_m[:], v_m[:], op=Alu.mult),
             flush=c_v)
        P.op("v", lambda E: E.scalar_tensor_tensor(v_var[:], v_ss[:], 1.0 / D, v_t2[:],
                                                   op0=Alu.mult, op1=Alu.subtract),
             flush=c_v)
        P.op("v", lambda E: E.tensor_scalar(v_vp[:], v_var[:], 1e-5, None, op0=Alu.add),
             inc=c_v)
        P.wait("v", c_v, c_v.n)
        newton_rsqrt(v_r[:], v_vp[:])
        P.op("v", lambda E: E.scalar_tensor_tensor(v_t2[:], v_m[:], -1.0, v_r[:],
                                                   op0=Alu.mult, op1=Alu.mult), inc=c_v)
        P.wait("s", c_v, c_v.n)
        P.op("s", lambda E: E.activation(f_xn[:], f_x[:], Act.Identity,
                                         scale=v_r[:], bias=v_t2[:]), inc=c_s)
        xn_done = c_s.n
        P.wait("v", c_s, xn_done)
        P.op("v", lambda E: E.reciprocal(v_t0[:], v_vp[:]), flush=c_v)
        P.op("v", lambda E: E.tensor_tensor(v_t1[:], v_var[:], v_t0[:], op=Alu.mult),
             flush=c_v)
        P.op("v", lambda E: E.tensor_scalar(v_t1[:], v_t1[:], 1e-8, None, op0=Alu.add),
             flush=c_v)
        P.op("v", lambda E: E.tensor_copy(v_t2[:], v_t1[:]), flush=c_v)
        newton_rsqrt(v_rr[:], v_t2[:])
        P.wait("v", c_s, xn_done)
        P.wait("v", dS["g3b"], g12_done)
        for br in range(2):
            gsl = g12[:, br * D:(br + 1) * D]
            ydram = d_y1 if br == 0 else d_y2
            invsl = (invs1 if br == 0 else invs2)[:, i:i + 1]
            P.op("v", lambda E, gsl=gsl: E.tensor_tensor(f_t1[:], f_xn[:], gsl,
                                                         op=Alu.mult))
            P.op("v", lambda E: E.reduce_max(v_mx[:], f_t1[:], axis=AXX), flush=c_v)
            P.op("v", lambda E: E.tensor_reduce(v_mn[:], f_t1[:], axis=AXX, op=Alu.min),
                 flush=c_v)
            P.op("v", lambda E: E.tensor_scalar(v_mn[:], v_mn[:], -1.0, None,
                                                op0=Alu.mult), flush=c_v)
            P.op("v", lambda E: E.tensor_tensor(v_m1[:], v_mx[:], v_mn[:], op=Alu.max),
                 flush=c_v)
            P.op("v", lambda E: E.tensor_tensor(v_d1[:], v_rr[:], v_m1[:], op=Alu.mult),
                 flush=c_v)
            P.op("v", lambda E: E.tensor_scalar_max(v_d1[:], v_d1[:], 1e-4), flush=c_v)
            P.op("v", lambda E: E.reciprocal(v_t0[:], v_d1[:]), flush=c_v)
            P.op("v", lambda E: E.scalar_tensor_tensor(v_c1[:], v_rr[:], 127.0, v_t0[:],
                                                       op0=Alu.mult, op1=Alu.mult),
                 flush=c_v)
            P.op("v", lambda E, invsl=invsl: E.tensor_scalar(invsl, v_d1[:], 1.0 / 127.0,
                                                             None, op0=Alu.mult), inc=c_v)
            P.wait("v", c_v, c_v.n)
            P.op("v", lambda E: E.tensor_scalar(f_t1[:], f_t1[:], v_c1[:], MAGIC,
                                                op0=Alu.mult, op1=Alu.add))
            P.wait("v", dS["y8"], dS["y8"].n)
            P.op("v", lambda E: E.tensor_scalar(i8a[:], f_t1[:], -MAGIC, None,
                                                op0=Alu.add), inc=c_v)
            P.wait("g", c_v, c_v.n)
            P.op("g", lambda E, d=ydram, i=i: E.dma_start(
                out=d[i * CH:(i + 1) * CH, :], in_=i8a[:]), inc=dS["y8"], inck=16)
        P.wait("t", c_s, xn_done)
        P.wait("t", dS["const"], const_done)
        for gq in range(4):
            pf = PF[4 + (gq % 2)]
            P.wait("t", c_s, c_s.n)
            P.wait("t", c_v, c_v.n)
            for j in range(4):
                kc = gq * 4 + j
                P.op("t", lambda E, pf=pf, j=j, kc=kc: E.transpose(
                    pf[:, j * CH:(j + 1) * CH], f_xn[:, kc * CH:(kc + 1) * CH],
                    identf_t[:]), inc=(c_t if j == 3 else None))
            P.wait("s", c_t, c_t.n)
            dst = xnT[b][:].rearrange("p (kc t) -> p kc t", kc=16)
            P.op("s", lambda E, pf=pf, gq=gq, b=b, i=i, dst=dst: E.activation(
                dst[:, gq * 4:gq * 4 + 4, (i % 4) * CH:(i % 4 + 1) * CH],
                pf[:].rearrange("p (j t) -> p j t", j=4), Act.Copy), inc=c_s)

    def qkv_block(b, ocs):
        for oi, oc in enumerate(ocs):
            slot = oi % 2
            pan = wqp[:, slot * 2048:(slot + 1) * 2048]
            P.wait("g", c_t, c_t.n)
            src = d_wqkvT[:, oc * CH:(oc + 1) * CH].rearrange("(kc p) j -> p kc j", p=CH)
            P.op("g", lambda E, pan=pan, src=src: E.dma_start(
                out=pan.rearrange("p (kc j) -> p kc j", kc=16), in_=src),
                inc=dS["qkvpan"], inck=16)
            P.wait("t", dS["qkvpan"], dS["qkvpan"].n)
            pfq = PF[oi % 2]
            P.wait("t", c_s, c_s.n)
            P.wait("t", c_v, c_v.n)
            for kc in range(16):
                P.op("t", lambda E, pfq=pfq, pan=pan, kc=kc, b=b: E.matmul(
                    pfq[:], pan[:, kc * CH:(kc + 1) * CH],
                    xnT[b][:, kc * 512:(kc + 1) * 512],
                    start=(kc == 0), stop=(kc == 15)),
                    inc=(c_t if kc == 15 else None))
            if oc < 16:
                h = oc
                dst = qfm[:, h * TC + b * 512: h * TC + b * 512 + 512]
            elif oc < 32:
                h = oc - 16
                dst = kvfm[:, h * 512: h * 512 + 512]
            else:
                h = oc - 32
                dst = kvfm[:, 8192 + h * 512: 8192 + h * 512 + 512]
            P.wait("s", c_t, c_t.n)
            P.op("s", lambda E, dst=dst, pfq=pfq, oc=oc: E.activation(
                dst, pfq[:], Act.Identity, bias=bq_t[:, oc:oc + 1]), inc=c_s)

    KTS = smalls[:, 0:CH]
    VTM0 = smalls[:, CH:2 * CH]
    VTM1 = smalls[:, 2 * CH:3 * CH]
    SCT0 = smalls[:, 3 * CH:4 * CH]
    SCT1 = smalls[:, 4 * CH:5 * CH]
    SBF = smalls[:, 5 * CH:6 * CH]

    def p3a_block(b):
        for c_ in range(4):
            cg = b * 4 + c_
            for h in range(H):
                ksl = kvfm[:, h * 512 + c_ * CH: h * 512 + (c_ + 1) * CH]
                vsl = kvfm[:, 8192 + h * 512 + c_ * CH: 8192 + h * 512 + (c_ + 1) * CH]
                P.wait("t", c_s, c_s.n)
                P.op("t", lambda E, ksl=ksl: E.transpose(PBA[:, 0:CH], ksl, identb_t[:]),
                     inc=c_t)
                P.op("t", lambda E, vsl=vsl: E.transpose(PBB[:, 0:CH], vsl, identb_t[:]),
                     inc=c_t)
                P.wait("s", c_t, c_t.n)
                P.op("s", lambda E, h=h: E.activation(KTS[:], PBA[:, 0:CH], Act.Identity,
                                                      scale=dk_t[:, h:h + 1]), inc=c_s)
                P.op("s", lambda E: E.activation(VTM0[:], PBB[:, 0:CH], Act.Copy), inc=c_s)
                P.wait("t", c_s, c_s.n)
                P.wait("t", c_v, c_v.n)
                P.op("t", lambda E: E.matmul(PF[2][:, 0:CH], KTS[:], VTM0[:],
                                             start=True, stop=True), inc=c_t)
                P.wait("v", c_t, c_t.n)
                csl = cst[:, (cg * H + h) * CH: (cg * H + h + 1) * CH]
                P.op("v", lambda E, csl=csl: E.tensor_copy(csl, PF[2][:, 0:CH]))
                asl = f_qa[:, h * CH:(h + 1) * CH]
                if cg == 0:
                    P.op("v", lambda E, asl=asl: E.tensor_copy(asl, PF[2][:, 0:CH]),
                         inc=c_v)
                else:
                    P.op("v", lambda E, asl=asl, h=h: E.scalar_tensor_tensor(
                        asl, asl, lc_t[:, h:h + 1], PF[2][:, 0:CH],
                        op0=Alu.mult, op1=Alu.add), inc=c_v)

    def p3b_block(b):
        P.wait("t", dS["const"], dmask_done)
        P.wait("v", dS["const"], dmask_done)
        for c_ in range(4):
            cg = b * 4 + c_
            for h in range(H):
                ksl = kvfm[:, h * 512 + c_ * CH: h * 512 + (c_ + 1) * CH]
                vsl = kvfm[:, 8192 + h * 512 + c_ * CH: 8192 + h * 512 + (c_ + 1) * CH]
                qsl = qfm[:, h * TC + cg * CH: h * TC + (cg + 1) * CH]
                vtm = VTM0 if h % 2 == 0 else VTM1
                sct = SCT0 if h % 2 == 0 else SCT1
                P.wait("t", c_s, c_s.n)
                P.op("t", lambda E, vsl=vsl: E.transpose(PBB[:, 0:CH], vsl, identb_t[:]),
                     inc=c_t)
                P.wait("s", c_t, c_t.n)
                P.op("s", lambda E, vtm=vtm: E.activation(vtm, PBB[:, 0:CH], Act.Copy),
                     inc=c_s)
                P.wait("t", c_v, c_v.n)
                P.op("t", lambda E, ksl=ksl, qsl=qsl: E.matmul(
                    PF[3][:, 0:CH], ksl, qsl, start=True, stop=True), inc=c_t)
                P.wait("v", c_t, c_t.n)
                P.op("v", lambda E, h=h, sct=sct: E.tensor_tensor(
                    sct, PF[3][:, 0:CH], dmk[:, h * CH:(h + 1) * CH], op=Alu.mult),
                    inc=c_v)
                pfi = PF[4 + h % 2]
                P.wait("t", c_v, c_v.n)
                P.wait("t", c_s, c_s.n)
                P.op("t", lambda E, pfi=pfi, sct=sct, vtm=vtm: E.matmul(
                    pfi[:, 0:CH], sct, vtm, start=True, stop=True), inc=c_t)
                P.wait("v", c_t, c_t.n)
                P.wait("v", dS["intra"], dS["intra"].n)
                P.wait("v", dS["tern"], dS["tern"].n)
                P.op("v", lambda E, pfi=pfi, h=h: E.tensor_copy(
                    istg[:, h * CH:(h + 1) * CH], pfi[:, 0:CH]), inc=c_v)
            P.wait("g", c_v, c_v.n)
            P.op("g", lambda E, cg=cg: E.dma_start(
                out=d_intra.ap()[cg * CH:(cg + 1) * CH, :], in_=istg[:]),
                inc=dS["intra"], inck=16)

    for i in range(4):
        phase1_tile(i)
    qkv_block(0, list(range(16, 48)))
    p3a_block(0)
    qkv_block(0, list(range(0, 16)))
    p3b_block(0)
    for i in range(4, 8):
        phase1_tile(i)
    qkv_block(1, list(range(16, 48)))
    p3a_block(1)
    qkv_block(1, list(range(0, 16)))
    p3b_block(1)

    P.wait("g", c_v, c_v.n)
    P.op("g", lambda E: E.dma_start(out=d_Aloc.ap(), in_=f_qa[:]), inc=dS["aall"], inck=16)
    P.wait("g", dS["aall"], dS["aall"].n)
    P.op("g", lambda E: E.collective_compute(
        "AllGather", Alu.bypass, replica_groups=[list(range(NCORES))],
        ins=[d_Aloc.ap()], outs=[d_Aall.ap()]), inc=cc)
    aAG_done = cc.n

    P.wait("v", c_s, c_s.n)
    P.op("v", lambda E: E.memset(f_sq[:], 0.0), inc=c_v)
    P.wait("y", cc, aAG_done)
    for j in range(7):
        P.wait("y", c_v, c_v.n)
        if j == 0:
            P.wait("y", dS["aall"], dS["aall"].n)
        dma("y", f_qa[:], d_Aall.ap()[j * CH:(j + 1) * CH, :], dS["aall"])
        P.wait("v", dS["aall"], dS["aall"].n)
        for h in range(H):
            P.op("v", lambda E, j=j, h=h: E.scalar_tensor_tensor(
                f_sq[:, h * CH:(h + 1) * CH], f_qa[:, h * CH:(h + 1) * CH],
                si_t[:, j * H + h: j * H + h + 1], f_sq[:, h * CH:(h + 1) * CH],
                op0=Alu.mult, op1=Alu.add), inc=(c_v if h == H - 1 else None))

    P.wait("g", c_v, c_v.n)
    P.op("g", lambda E: E.dma_start(out=dmk[:], in_=d_boutb), inc=dS["misc"], inck=16)
    boutb_done = dS["misc"].n

    slot_load = [0, 0]

    def load_wout_panel(oc, slot):
        dstp = xnT[slot]
        src = d_woutT[:, oc * 512:(oc + 1) * 512].rearrange("(kc p) j -> p kc j", p=CH)
        P.op("g", lambda E, dstp=dstp, src=src: E.dma_start(
            out=dstp[:].rearrange("p (kc j) -> p kc j", kc=16), in_=src),
            inc=dS["woutpan"], inck=16)
        slot_load[slot] = dS["woutpan"].n

    P.wait("g", c_t, c_t.n)
    load_wout_panel(0, 0)
    load_wout_panel(1, 1)

    for j in range(NT):
        P.wait("y", c_v, c_v.n)
        if j == 0:
            P.wait("y", dS["intra"], dS["intra"].n)
        dma("y", istg[:], d_intra.ap()[j * CH:(j + 1) * CH, :], dS["t3r"])
        P.wait("v", dS["t3r"], dS["t3r"].n)
        P.wait("v", dS["outx"], dS["outx"].n)
        for h in range(H):
            P.wait("v", c_t, c_t.n)
            P.op("v", lambda E, h=h: E.tensor_copy(SBF[:], f_sq[:, h * CH:(h + 1) * CH]),
                 inc=c_v)
            pfi = PF[h % 2]
            P.wait("t", c_v, c_v.n)
            P.op("t", lambda E, pfi=pfi, h=h, j=j: E.matmul(
                pfi[:, 0:CH], qfm[:, h * TC + j * CH: h * TC + (j + 1) * CH], SBF[:],
                start=True, stop=True), inc=c_t)
            P.wait("v", c_t, c_t.n)
            P.op("v", lambda E, pfi=pfi, h=h: E.scalar_tensor_tensor(
                attnt[:, h * CH:(h + 1) * CH], pfi[:, 0:CH], dq_t[:, h:h + 1],
                istg[:, h * CH:(h + 1) * CH], op0=Alu.mult, op1=Alu.add), inc=c_v)
            P.op("v", lambda E, h=h, j=j: E.scalar_tensor_tensor(
                f_sq[:, h * CH:(h + 1) * CH], f_sq[:, h * CH:(h + 1) * CH],
                lc_t[:, h:h + 1], cst[:, (j * H + h) * CH:(j * H + h + 1) * CH],
                op0=Alu.mult, op1=Alu.add), inc=c_v)
        attn_done = c_v.n
        P.wait("t", c_v, attn_done)
        for gq in range(4):
            P.wait("t", c_s, c_s.n)
            for kk in range(4):
                kc = gq * 4 + kk
                P.op("t", lambda E, kc=kc, kk=kk: E.transpose(
                    PBA[:, kk * CH:(kk + 1) * CH], attnt[:, kc * CH:(kc + 1) * CH],
                    identb_t[:]), inc=(c_t if kk == 3 else None))
            P.wait("s", c_t, c_t.n)
            P.op("s", lambda E, gq=gq: E.activation(
                afm[:].rearrange("p (g t) -> p g t", g=16)[:, gq * 4:gq * 4 + 4, :],
                PBA[:, 0:512].rearrange("p (g t) -> p g t", g=4), Act.Copy), inc=c_s)
        afm_done = c_s.n
        P.wait("t", c_s, afm_done)
        for oc in range(4):
            pfo = PF[2 + oc % 2]
            P.wait("t", c_v, c_v.n)
            P.wait("t", dS["woutpan"], slot_load[oc % 2])
            wp_slot = xnT[oc % 2]
            for kc in range(16):
                P.op("t", lambda E, pfo=pfo, wp_slot=wp_slot, kc=kc: E.matmul(
                    pfo[:], afm[:, kc * CH:(kc + 1) * CH],
                    wp_slot[:, kc * 512:(kc + 1) * 512],
                    start=(kc == 0), stop=(kc == 15)), inc=(c_t if kc == 15 else None))
            if oc == 0:
                P.wait("y", c_v, c_v.n)
                dma("y", f_x[:], d_x[j * CH:(j + 1) * CH, :], dS["x"])
                P.wait("v", dS["x"], dS["x"].n)
                P.wait("v", dS["misc"], boutb_done)
            P.wait("v", c_t, c_t.n)
            P.wait("v", dS["outx"], dS["outx"].n)
            P.op("v", lambda E, pfo=pfo, oc=oc: E.scalar_tensor_tensor(
                f_t1[:, oc * 512:(oc + 1) * 512], pfo[:], 1.0,
                f_x[:, oc * 512:(oc + 1) * 512], op0=Alu.mult, op1=Alu.add))
            P.op("v", lambda E, pfo=pfo, oc=oc: E.tensor_tensor(
                attnt[:, oc * 512:(oc + 1) * 512], f_t1[:, oc * 512:(oc + 1) * 512],
                dmk[:, oc * 512:(oc + 1) * 512], op=Alu.add), inc=c_v)
            nxt = (j * 4 + oc + 2)
            if nxt < NT * 4:
                P.wait("g", c_t, c_t.n)
                load_wout_panel(nxt % 4, oc % 2)
        P.wait("g", c_v, c_v.n)
        P.op("g", lambda E, j=j: E.dma_start(
            out=d_outx.ap()[j * CH:(j + 1) * CH, :], in_=attnt[:]),
            inc=dS["outx"], inck=16)

    P.barrier(bar)
    g3t = kvfm[:, 8192:16384]
    P.op("g", lambda E: E.dma_start(out=g3t, in_=d_g3b), inc=dS["g3b"], inck=16)
    g3_done = dS["g3b"].n

    y1T = qfm
    y2T = cst
    for br in range(2):
        ydram = d_y1 if br == 0 else d_y2
        ytgt = y1T if br == 0 else y2T
        for i in range(NT):
            P.wait("y", c_v, c_v.n)
            P.wait("y", dS["y8"], dS["y8"].n)
            dma("y", i8a[:], ydram.ap()[i * CH:(i + 1) * CH, :], dS["y8r"])
            P.wait("v", dS["y8r"], dS["y8r"].n)
            P.wait("v", c_t, c_t.n)
            P.op("v", lambda E: E.tensor_copy(f_t1[:], i8a[:]), inc=c_v)
            P.wait("t", c_v, c_v.n)
            for gq in range(4):
                pf = PF[4 + gq % 2]
                P.wait("t", c_s, c_s.n)
                for kk in range(4):
                    kc = gq * 4 + kk
                    P.op("t", lambda E, pf=pf, kk=kk, kc=kc: E.transpose(
                        pf[:, kk * CH:(kk + 1) * CH], f_t1[:, kc * CH:(kc + 1) * CH],
                        identf_t[:]), inc=(c_t if kk == 3 else None))
                P.wait("s", c_t, c_t.n)
                dstr = ytgt[:].rearrange("p (kc t) -> p kc t", kc=16)
                P.op("s", lambda E, pf=pf, gq=gq, i=i, dstr=dstr: E.activation(
                    dstr[:, gq * 4:gq * 4 + 4, i * CH:(i + 1) * CH],
                    pf[:].rearrange("p (j t) -> p j t", j=4), Act.Copy), inc=c_s)

    P.wait("v", c_s, a_ready_s)
    P.op("v", lambda E: E.tensor_single_scalar(cu_c[:], invs1[:], sc_t[:, 12:13],
                                               op=Alu.mult), flush=c_v)
    P.op("v", lambda E: E.tensor_single_scalar(cv_c[:], invs2[:], sc_t[:, 13:14],
                                               op=Alu.mult), inc=c_v)

    P.op("v", lambda E: E.memset(f_st3[:], 0.0), inc=c_v)
    P.op("v", lambda E: E.memset(f_st3[:, 128:256], -3.0e38))
    P.op("v", lambda E: E.memset(f_st3[:, 256:384], 3.0e38), inc=c_v)

    pan_slots = [xnT0, xnT1, kvfm[:, 0:8192]]
    HCH = f_xn[:, 0:512]
    SILU = f_xn[:, 512:1024]
    T3S0 = attnt[:, 0:512]
    T3S1 = attnt[:, 512:1024]
    P.wait("y", cc, wq_ready)
    P.wait("t", bar, bar.n)
    P.wait("s", bar, bar.n)
    slot_sem_t = [0, 0, 0]

    pan_i = 0
    for oc in range(16):
        for w_i, dram in ((0, d_w1q), (1, d_w2q)):
            slot = pan_i % 3
            pan_i += 1
            P.wait("y", c_t, slot_sem_t[slot])
            src = dram.ap()[:, oc * 512:(oc + 1) * 512].rearrange(
                "(kc p) j -> p kc j", p=CH)
            P.op("y", lambda E, slot=slot, src=src: E.dma_start(
                out=pan_slots[slot][:].rearrange("p (kc j) -> p kc j", kc=16), in_=src),
                inc=dS["ffnpan"], inck=16)
        slot_u = (pan_i - 2) % 3
        slot_v = (pan_i - 1) % 3
        P.wait("t", dS["ffnpan"], dS["ffnpan"].n)
        if oc == 0:
            P.wait("s", dS["g3b"], g3_done)
            P.wait("v", dS["g3b"], g3_done)
            P.wait("v", dS["outx"], dS["outx"].n)
        for t_ in range(NT):
            pu = PF[t_ % 2]
            pv = PF[2 + t_ % 2]
            P.wait("t", c_s, c_s.n)
            for kc in range(16):
                P.op("t", lambda E, pu=pu, kc=kc, t_=t_, slot_u=slot_u: E.matmul(
                    pu[:], y1T[:, kc * TC + t_ * CH: kc * TC + (t_ + 1) * CH],
                    pan_slots[slot_u][:, kc * 512:(kc + 1) * 512],
                    start=(kc == 0), stop=(kc == 15)), inc=(c_t if kc == 15 else None))
            P.wait("t", c_v, c_v.n)
            for kc in range(16):
                P.op("t", lambda E, pv=pv, kc=kc, t_=t_, slot_v=slot_v: E.matmul(
                    pv[:], y2T[:, kc * TC + t_ * CH: kc * TC + (t_ + 1) * CH],
                    pan_slots[slot_v][:, kc * 512:(kc + 1) * 512],
                    start=(kc == 0), stop=(kc == 15)), inc=(c_t if kc == 15 else None))
            P.wait("s", c_t, c_t.n - 1)
            P.op("s", lambda E, pu=pu, t_=t_: E.activation(
                SILU, pu[:], Act.Silu, scale=cu_c[:, t_:t_ + 1]), inc=c_s)
            P.wait("v", c_t, c_t.n)
            P.wait("v", c_s, c_s.n)
            P.op("v", lambda E, pv=pv, t_=t_: E.scalar_tensor_tensor(
                HCH, pv[:], cv_c[:, t_:t_ + 1], SILU, op0=Alu.mult, op1=Alu.mult),
                inc=c_v)
            P.wait("s", c_v, c_v.n)
            P.op("s", lambda E: E.activation(f_xn[:, 1024:1536], HCH, Act.Square,
                                             accum_out=v_t0[:]), inc=c_s)
            P.wait("v", c_s, c_s.n)
            col = f_st3[:, t_ * 16 + oc: t_ * 16 + oc + 1]
            P.op("v", lambda E, col=col: E.tensor_copy(col, v_t0[:]), flush=c_v)
            t3s = T3S0 if oc % 2 == 0 else T3S1
            P.wait("v", dS["t3w"], dS["t3w"].n)
            P.op("v", lambda E, t3s=t3s, oc=oc: E.tensor_tensor(
                t3s, HCH, g3t[:, oc * 512:(oc + 1) * 512], op=Alu.mult))
            mxc = f_st3[:, 128 + t_ * 16 + oc: 128 + t_ * 16 + oc + 1]
            mnc = f_st3[:, 256 + t_ * 16 + oc: 256 + t_ * 16 + oc + 1]
            P.op("v", lambda E, t3s=t3s, mxc=mxc: E.reduce_max(mxc, t3s, axis=AXX),
                 flush=c_v)
            P.op("v", lambda E, t3s=t3s, mnc=mnc: E.tensor_reduce(mnc, t3s, axis=AXX,
                                                                  op=Alu.min), inc=c_v)
            P.wait("g", c_v, c_v.n)
            P.op("g", lambda E, t3s=t3s, t_=t_, oc=oc: E.dma_start(
                out=d_t3.ap()[t_ * CH:(t_ + 1) * CH, oc * 512:(oc + 1) * 512], in_=t3s),
                inc=dS["t3w"], inck=16)
        slot_sem_t[slot_u] = c_t.n
        slot_sem_t[slot_v] = c_t.n

    P.wait("v", c_s, c_s.n)
    for t_ in range(NT):
        P.op("v", lambda E, t_=t_: E.reduce_sum(v_ss[:], f_st3[:, t_ * 16:t_ * 16 + 16],
                                                axis=AXX), flush=c_v)
        P.op("v", lambda E, t_=t_: E.reduce_max(v_mx[:],
                                                f_st3[:, 128 + t_ * 16:128 + t_ * 16 + 16],
                                                axis=AXX), flush=c_v)
        P.op("v", lambda E, t_=t_: E.tensor_reduce(
            v_mn[:], f_st3[:, 256 + t_ * 16:256 + t_ * 16 + 16], axis=AXX, op=Alu.min),
            flush=c_v)
        P.op("v", lambda E: E.tensor_scalar(v_mn[:], v_mn[:], -1.0, None, op0=Alu.mult),
             flush=c_v)
        P.op("v", lambda E: E.tensor_tensor(v_m1[:], v_mx[:], v_mn[:], op=Alu.max),
             flush=c_v)
        P.op("v", lambda E: E.tensor_scalar(v_t1[:], v_ss[:], 1.0 / HID, 1e-8,
                                            op0=Alu.mult, op1=Alu.add), flush=c_v)
        P.op("v", lambda E: E.tensor_copy(v_t2[:], v_t1[:]), flush=c_v)
        newton_rsqrt(v_rr[:], v_t2[:])
        P.op("v", lambda E: E.tensor_tensor(v_d1[:], v_rr[:], v_m1[:], op=Alu.mult),
             flush=c_v)
        P.op("v", lambda E: E.tensor_scalar_max(v_d1[:], v_d1[:], 1e-4), flush=c_v)
        P.op("v", lambda E: E.reciprocal(v_t0[:], v_d1[:]), flush=c_v)
        P.op("v", lambda E, t_=t_: E.scalar_tensor_tensor(
            c3_c[:, t_:t_ + 1], v_rr[:], 127.0, v_t0[:], op0=Alu.mult, op1=Alu.mult),
            flush=c_v)
        P.op("v", lambda E, t_=t_: E.tensor_scalar(cy3_c[:, t_:t_ + 1], v_d1[:],
                                                   1.0 / 127.0, None, op0=Alu.mult),
             inc=c_v)
    P.wait("v", c_v, c_v.n)
    P.op("v", lambda E: E.tensor_single_scalar(cy3_c[:], cy3_c[:], sc_t[:, 14:15],
                                               op=Alu.mult), inc=c_v)

    P.barrier(bar)
    y3T = cst
    w3p = [wqp[:, 0:2048], wqp[:, 2048:4096]]
    OXB = dmk

    for g_ in range(4):
        tiles = (2 * g_, 2 * g_ + 1)
        for ti, t_ in enumerate(tiles):
            for oc in range(16):
                P.wait("y", c_v, c_v.n)
                dma("y", afm[:, 0:512],
                    d_t3.ap()[t_ * CH:(t_ + 1) * CH, oc * 512:(oc + 1) * 512],
                    dS["t3r"])
                P.wait("v", dS["t3r"], dS["t3r"].n)
                P.wait("v", c_t, c_t.n)
                P.op("v", lambda E, t_=t_: E.tensor_scalar(
                    f_t1[:, 0:512], afm[:, 0:512], c3_c[:, t_:t_ + 1], MAGIC,
                    op0=Alu.mult, op1=Alu.add), inc=c_v)
                pf = PF[4 + oc % 2]
                P.wait("t", c_v, c_v.n)
                for kk in range(4):
                    P.op("t", lambda E, pf=pf, kk=kk: E.transpose(
                        pf[:, kk * CH:(kk + 1) * CH], f_t1[:, kk * CH:(kk + 1) * CH],
                        identf_t[:]), inc=(c_t if kk == 3 else None))
                P.wait("v", c_t, c_t.n)
                P.op("v", lambda E, pf=pf: E.tensor_scalar(
                    i16a[:], pf[:], -MAGIC, None, op0=Alu.add), inc=c_v)
                dstr = y3T[:].rearrange("p (kc t) -> p kc t", kc=64)
                P.op("v", lambda E, oc=oc, ti=ti, dstr=dstr: E.tensor_copy(
                    dstr[:, oc * 4:oc * 4 + 4, ti * CH:(ti + 1) * CH],
                    i16a[:].rearrange("p (j t) -> p j t", j=4)), inc=c_v)
        y3r = y3T[:].rearrange("p (kc t) -> p kc t", kc=64)
        for sweep, ocs in ((0, (0, 1, 2)), (1, (3,))):
            for kc in range(64):
                slot = kc % 2
                P.wait("y", c_t, c_t.n)
                if sweep == 0:
                    src = d_w3q.ap()[kc * CH:(kc + 1) * CH, 0:1536]
                    P.op("y", lambda E, slot=slot, src=src: E.dma_start(
                        out=w3p[slot][:, 0:1536], in_=src), inc=dS["w3pan"], inck=16)
                else:
                    src = d_w3q.ap()[kc * CH:(kc + 1) * CH, 1536:2048]
                    P.op("y", lambda E, slot=slot, src=src: E.dma_start(
                        out=w3p[slot][:, 0:512], in_=src), inc=dS["w3pan"], inck=16)
                P.wait("t", dS["w3pan"], dS["w3pan"].n)
                if kc == 0:
                    P.wait("t", c_v, c_v.n)
                nmm = 2 * len(ocs)
                mi = 0
                for ti in range(2):
                    for oi, oc in enumerate(ocs):
                        mi += 1
                        pf = PF[ti * 3 + oi] if sweep == 0 else PF[ti]
                        off = oi * 512
                        P.op("t", lambda E, pf=pf, slot=slot, off=off, kc=kc, ti=ti:
                             E.matmul(pf[:], y3r[:, kc, ti * CH:(ti + 1) * CH],
                                      w3p[slot][:, off:off + 512],
                                      start=(kc == 0), stop=(kc == 63)),
                             inc=(c_t if mi == nmm else None))
            mm_done = c_t.n
            for ti, t_ in enumerate(tiles):
                P.wait("y", c_v, c_v.n)
                if g_ == 0 and sweep == 0 and ti == 0:
                    P.wait("y", dS["misc"], dS["misc"].n)
                dma("y", OXB[:], d_outx.ap()[t_ * CH:(t_ + 1) * CH, :], dS["outxr"])
                P.wait("v", c_t, mm_done)
                P.wait("v", dS["outxr"], dS["outxr"].n)
                P.wait("v", dS["out"], dS["out"].n)
                for oi, oc in enumerate(ocs):
                    pf = PF[ti * 3 + oi] if sweep == 0 else PF[ti]
                    P.op("v", lambda E, pf=pf, t_=t_, oc=oc: E.scalar_tensor_tensor(
                        f_sq[:, oc * 512:(oc + 1) * 512], pf[:], cy3_c[:, t_:t_ + 1],
                        OXB[:, oc * 512:(oc + 1) * 512], op0=Alu.mult, op1=Alu.add),
                        inc=c_v)
                P.wait("y", c_v, c_v.n)
                for oi, oc in enumerate(ocs):
                    P.op("y", lambda E, t_=t_, oc=oc: E.dma_start(
                        out=d_out[t_ * CH:(t_ + 1) * CH, oc * 512:(oc + 1) * 512],
                        in_=f_sq[:, oc * 512:(oc + 1) * 512]), inc=dS["out"], inck=16)

    P.wait("y", dS["out"], dS["out"].n)
    P.wait("g", dS["out"], dS["out"].n)

    with nc.Block() as block:
        @block.tensor
        def _(E):
            for f in P.q["t"]:
                f(E)

        @block.vector
        def _(E):
            for f in P.q["v"]:
                f(E)

        @block.scalar
        def _(E):
            for f in P.q["s"]:
                f(E)

        @block.gpsimd
        def _(E):
            for f in P.q["g"]:
                f(E)

        @block.sync
        def _(E):
            for f in P.q["y"]:
                f(E)

    return nc


_BUILT = {}


def kernel(L0_attn_x, L0_attn_w_qkv, L0_attn_b_qkv, L0_attn_w_out, L0_attn_b_out,
           L0_ffn_w1, L0_ffn_g1, L0_ffn_w2, L0_ffn_g2, L0_ffn_w3, L0_ffn_g3):
    const = host_constants()
    if "nc" not in _BUILT:
        _BUILT["nc"] = build_nc(const)
    nc = _BUILT["nc"]

    x = np.ascontiguousarray(np.asarray(L0_attn_x, np.float32).reshape(8192, D))
    wqkvT = np.ascontiguousarray(np.asarray(L0_attn_w_qkv, np.float32).T)
    bqkv = np.asarray(L0_attn_b_qkv, np.float32)
    bqkv_fm = np.ascontiguousarray(bqkv.reshape(48, CH).T)
    woutT = np.ascontiguousarray(np.asarray(L0_attn_w_out, np.float32).T)
    boutb = np.ascontiguousarray(
        np.broadcast_to(np.asarray(L0_attn_b_out, np.float32)[None, :], (CH, D)))
    w1T = np.ascontiguousarray(np.asarray(L0_ffn_w1, np.float32).T)
    w2T = np.ascontiguousarray(np.asarray(L0_ffn_w2, np.float32).T)
    w3T = np.ascontiguousarray(np.asarray(L0_ffn_w3, np.float32).T)
    g12b = np.ascontiguousarray(np.broadcast_to(
        np.concatenate([np.asarray(L0_ffn_g1, np.float32),
                        np.asarray(L0_ffn_g2, np.float32)])[None, :], (CH, 2 * D)))
    g3b = np.ascontiguousarray(
        np.broadcast_to(np.asarray(L0_ffn_g3, np.float32)[None, :], (CH, HID)))

    in_maps = []
    for c in range(NCORES):
        m = {
            "x_slice": np.ascontiguousarray(x[c * TC:(c + 1) * TC]),
            "wqkvT": wqkvT,
            "bqkv_fm": bqkv_fm,
            "woutT": woutT,
            "bout_b": boutb,
            "w1T_shard": np.ascontiguousarray(w1T[c * 256:(c + 1) * 256]),
            "w2T_shard": np.ascontiguousarray(w2T[c * 256:(c + 1) * 256]),
            "w3T_shard": np.ascontiguousarray(w3T[c * 1024:(c + 1) * 1024]),
            "g12_b": g12b,
            "g3_b": g3b,
            "dmaskT": const["dmaskT"],
            "decayqcol": const["decayqcol"],
            "decaykb": const["decaykb"],
            "lamcb": const["lamcb"],
            "identf": const["identf"],
            "identb": const["identb"],
            "sinit_w": const["sinit"][c],
            "wnumel_inv": const["wnumel_inv"],
        }
        in_maps.append(m)

    res = run_bass_kernel_spmd(nc, in_maps, core_ids=list(range(NCORES)))
    out = np.concatenate([res.results[c]["out_slice"] for c in range(NCORES)], axis=0)
    return out.reshape(1, 8192, D)


# version 07
# speedup vs baseline: 1.4463x; 1.4463x over previous
import math
import numpy as np
import ml_dtypes

import concourse.bass as bass
import concourse.mybir as mybir
from concourse.bass_utils import run_bass_kernel_spmd

F32 = mybir.dt.float32
BF16 = mybir.dt.bfloat16
I16 = mybir.dt.int16
I8 = mybir.dt.int8
Act = mybir.ActivationFunctionType
Alu = mybir.AluOpType
AXX = mybir.AxisListType.X

MAGIC = 12582912.0
NCORES = 8
TC = 1024
D = 2048
HID = 8192
H = 16
CH = 128
NT = 8
NB = 2
LN3 = 1.0986122886681098


def _slopes(n):
    start = 2.0 ** (-2.0 ** (-(math.log2(n) - 3)))
    return np.array([start ** (i + 1) for i in range(n)], dtype=np.float64)


def host_constants():
    lam = np.exp(-_slopes(H))
    pos = np.arange(CH, dtype=np.float64)
    decay_q = lam[:, None] ** (pos + 1.0)
    decay_k = lam[:, None] ** (CH - 1.0 - pos)
    diff = pos[:, None] - pos[None, :]
    dmask = np.where(diff >= 0, lam[:, None, None] ** np.maximum(diff, 0), 0.0)
    lam_c = lam ** CH
    lam_sl = lam ** TC
    c = {}
    dmT = np.zeros((CH, H * CH), np.float64)
    for h in range(H):
        dmT[:, h * CH:(h + 1) * CH] = dmask[h].T
    c["dmaskT"] = dmT.astype(ml_dtypes.bfloat16)
    c["decayqcol"] = np.broadcast_to(decay_q.T, (CH, H)).astype(np.float32).copy()
    c["decaykb"] = np.broadcast_to(decay_k.T, (CH, H)).astype(np.float32).copy()
    c["lamcb"] = np.tile(lam_c.astype(np.float32)[None, :], (CH, 1))
    c["identf"] = np.eye(CH, dtype=np.float32)
    c["identb"] = np.eye(CH, dtype=np.float32).astype(ml_dtypes.bfloat16)
    c["ones_col"] = np.ones((CH, 1), np.float32)
    sinit = []
    for core in range(NCORES):
        w = np.zeros((7, H), np.float64)
        for j in range(core):
            w[j] = lam_sl ** (core - 1 - j)
        sinit.append(np.tile(w.reshape(1, 7 * H), (CH, 1)).astype(np.float32))
    c["sinit"] = sinit
    c["wnumel_inv"] = np.array(
        [[1.0 / (HID * D), 1.0 / (HID * D), 1.0 / (D * HID), 0.0]], np.float32)
    return c


class Sem:
    def __init__(self, handle, name):
        self.h = handle
        self.n = 0
        self.name = name


class Prog:

    ENGS = ("t", "v", "s", "g", "y")

    def __init__(self):
        self.q = {e: [] for e in self.ENGS}

    def op(self, eng, fn, inc=None, inck=1, flush=None):
        if flush is not None:
            assert inc is None
            flush.n += 1
            v = flush.n
            self.q[eng].append(lambda E: fn(E).then_inc(flush.h, 1))
            self.q[eng].append(lambda E: E.wait_ge(flush.h, v))
        elif inc is not None:
            inc.n += inck
            self.q[eng].append(lambda E: fn(E).then_inc(inc.h, inck))
        else:
            self.q[eng].append(lambda E: fn(E))

    def wait(self, eng, sem, val=None):
        v = sem.n if val is None else val
        self.q[eng].append(lambda E: E.wait_ge(sem.h, v))

    def barrier(self, sem, engs=ENGS):
        for e in engs:
            self.q[e].append(lambda E: E.nop().then_inc(sem.h, 1))
        sem.n += len(engs)
        tgt = sem.n
        for e in engs:
            self.q[e].append(lambda E: E.wait_ge(sem.h, tgt))


def build_nc(const):
    nc = bass.Bass("TRN2", target_bir_lowering=False, debug=False)

    d_x = nc.dram_tensor("x_slice", [TC, D], F32, kind="ExternalInput").ap()
    d_wqkvT = nc.dram_tensor("wqkvT", [D, 3 * D], F32, kind="ExternalInput").ap()
    d_bqkv_fm = nc.dram_tensor("bqkv_fm", [CH, 48], F32, kind="ExternalInput").ap()
    d_woutT = nc.dram_tensor("woutT", [D, D], F32, kind="ExternalInput").ap()
    d_boutb = nc.dram_tensor("bout_b", [CH, D], F32, kind="ExternalInput").ap()
    d_w1sh = nc.dram_tensor("w1T_shard", [D // 8, HID], F32, kind="ExternalInput").ap()
    d_w2sh = nc.dram_tensor("w2T_shard", [D // 8, HID], F32, kind="ExternalInput").ap()
    d_w3sh = nc.dram_tensor("w3T_shard", [HID // 8, D], F32, kind="ExternalInput").ap()
    d_g12b = nc.dram_tensor("g12_b", [CH, 2 * D], F32, kind="ExternalInput").ap()
    d_g3b = nc.dram_tensor("g3_b", [CH, HID], F32, kind="ExternalInput").ap()
    d_dmaskT = nc.dram_tensor("dmaskT", [CH, H * CH], BF16, kind="ExternalInput").ap()
    d_dqcol = nc.dram_tensor("decayqcol", [CH, H], F32, kind="ExternalInput").ap()
    d_dkcol = nc.dram_tensor("decaykb", [CH, H], F32, kind="ExternalInput").ap()
    d_lamcb = nc.dram_tensor("lamcb", [CH, H], F32, kind="ExternalInput").ap()
    d_identf = nc.dram_tensor("identf", [CH, CH], F32, kind="ExternalInput").ap()
    d_identb = nc.dram_tensor("identb", [CH, CH], BF16, kind="ExternalInput").ap()
    d_sinit = nc.dram_tensor("sinit_w", [CH, 7 * H], F32, kind="ExternalInput").ap()
    d_wninv = nc.dram_tensor("wnumel_inv", [1, 4], F32, kind="ExternalInput").ap()

    d_out = nc.dram_tensor("out_slice", [TC, D], F32, kind="ExternalOutput").ap()
    d_arin = nc.dram_tensor("ar_in", [1, 4], F32)
    d_arout = nc.dram_tensor("ar_out", [1, 4], F32)
    d_t1b = nc.dram_tensor("tern1_sh", [D // 8, HID], BF16)
    d_t2b = nc.dram_tensor("tern2_sh", [D // 8, HID], BF16)
    d_t3b = nc.dram_tensor("tern3_sh", [HID // 8, D], BF16)
    d_w1q = nc.dram_tensor("w1q", [D, HID], BF16)
    d_w2q = nc.dram_tensor("w2q", [D, HID], BF16)
    d_w3q = nc.dram_tensor("w3q", [HID, D], BF16)
    d_Aloc = nc.dram_tensor("A_loc", [CH, H * CH], F32)
    d_Aall = nc.dram_tensor("A_all", [NCORES * CH, H * CH], F32)
    d_y1 = nc.dram_tensor("y1_i8", [TC, D], I8)
    d_y2 = nc.dram_tensor("y2_i8", [TC, D], I8)
    d_intra = nc.dram_tensor("intra_d", [TC, D], BF16)
    d_outx = nc.dram_tensor("outx_d", [TC, D], BF16)
    d_t3 = nc.dram_tensor("t3_d", [TC, HID], BF16)

    A = nc.alloc_sbuf_tensor
    sb = {}

    def S(name, shape, dt):
        sb[name] = A(name, shape, dt)
        return sb[name]

    xnT0 = S("xnT0", [CH, 16 * 512], BF16)
    xnT1 = S("xnT1", [CH, 16 * 512], BF16)
    kvfm = S("kvfm", [CH, 2 * 8192], BF16)
    qfm = S("qfm", [CH, H * TC], BF16)
    cst = S("cst", [CH, 16384], BF16)
    istg = S("istg", [CH, D], BF16)
    wqp = S("wqp", [CH, 2 * 2048], BF16)
    dmk = S("dmk", [CH, H * CH], BF16)
    g12 = S("g12", [CH, 2 * D], BF16)
    attnt = S("attnt", [CH, D], BF16)
    afm = S("afm", [CH, D], BF16)
    smalls = S("smalls", [CH, 6 * CH], BF16)
    identb_t = S("identb_t", [CH, CH], BF16)
    f_sq = S("f_sq", [CH, D], F32)
    f_x = S("f_x", [CH, D], F32)
    f_xn = S("f_xn", [CH, D], F32)
    f_t1 = S("f_t1", [CH, D], F32)
    f_qa = S("f_qa", [CH, D], F32)
    f_vec = S("f_vec", [CH, 56], F32)
    f_st3 = S("f_st3", [CH, 3 * 128], F32)
    identf_t = S("identf_t", [CH, CH], F32)
    bq_t = S("bq_t", [CH, 48], F32)
    dq_t = S("dq_t", [CH, H], F32)
    dk_t = S("dk_t", [CH, H], F32)
    lc_t = S("lc_t", [CH, H], F32)
    si_t = S("si_t", [CH, 7 * H], F32)
    sc_t = S("sc_t", [CH, 16], F32)
    i8a = S("i8a", [CH, D], I8)
    i16a = S("i16a", [CH, 512], I16)

    rem = nc.sbuf_bytes_remaining() if callable(getattr(nc, "sbuf_bytes_remaining", None)) else None

    PF = [nc.alloc_psum_tensor if False else None]
    psum_guards = []
    PF = []
    for i in range(6):
        g = nc.psum_tensor(f"pf{i}", [CH, 512], F32)
        t = g.__enter__()
        psum_guards.append(g)
        PF.append(t)
    PBA_g = nc.psum_tensor("pba", [CH, 1024], BF16)
    PBA = PBA_g.__enter__()
    psum_guards.append(PBA_g)
    PBB_g = nc.psum_tensor("pbb", [CH, 1024], BF16)
    PBB = PBB_g.__enter__()
    psum_guards.append(PBB_g)

    sem_guards = []
    sems = {}

    def mksem(name):
        g = nc.semaphore(name)
        h = g.__enter__()
        sem_guards.append(g)
        sems[name] = Sem(h, name)
        return sems[name]

    dS = {n: mksem("d_" + n) for n in
          ("const", "x", "shard", "tern", "ar", "y8", "intra",
           "aall", "outx", "ffnpan", "t3w", "y8r",
           "outxr", "out", "g3b", "misc",
           "qkvpan0", "qkvpan1", "woutp0", "woutp1", "w3p0", "w3p1",
           "t3r0", "t3r1", "t3r2", "t3r3")}
    c_t = mksem("c_t")
    c_v = mksem("c_v")
    c_s = mksem("c_s")
    c_g = mksem("c_g")
    cc = mksem("cc")
    bar = mksem("bar")

    P = Prog()

    def dma(eng, dst, src, sem):
        P.op(eng, lambda E, d=dst, s=src: E.dma_start(out=d, in_=s), inc=sem, inck=16)

    dma("y", identf_t[:], d_identf, dS["const"])
    dma("y", identb_t[:], d_identb, dS["const"])
    dma("y", bq_t[:], d_bqkv_fm, dS["const"])
    dma("y", dq_t[:], d_dqcol, dS["const"])
    dma("y", dk_t[:], d_dkcol, dS["const"])
    dma("y", lc_t[:], d_lamcb, dS["const"])
    dma("y", si_t[:], d_sinit, dS["const"])
    dma("g", g12[:], d_g12b, dS["g3b"])
    const_done = dS["const"].n
    g12_done = dS["g3b"].n

    def shard_chunks(dram, wi):
        out = []
        if wi < 2:
            for pc in range(2):
                for cc_ in range(4):
                    out.append(dram[pc * CH:(pc + 1) * CH, cc_ * D:(cc_ + 1) * D])
        else:
            for pc in range(8):
                out.append(dram[pc * CH:(pc + 1) * CH, :])
        return out

    def tern_chunks(dram, wi):
        return shard_chunks(dram, wi)

    shard_in = [shard_chunks(d_w1sh, 0), shard_chunks(d_w2sh, 1), shard_chunks(d_w3sh, 2)]
    tern_out = [tern_chunks(d_t1b, 0), tern_chunks(d_t2b, 1), tern_chunks(d_t3b, 2)]

    ab_cols = f_st3[:, 0:32]
    meta = {}

    def absum_chunks(wi):
        for ci, chunk in enumerate(shard_in[wi]):
            dma("y", f_qa[:], chunk, dS["shard"])
            P.wait("s", dS["shard"], dS["shard"].n)
            P.op("s", lambda E, ci=ci, wi=wi: E.activation(
                f_sq[:], f_qa[:], Act.Abs,
                accum_out=ab_cols[:, wi * 8 + ci: wi * 8 + ci + 1]), inc=c_s)
            P.wait("y", c_s, c_s.n)

    def absum_finish():
        P.wait("v", c_s, c_s.n)
        for wi in range(3):
            P.op("v", lambda E, wi=wi: E.reduce_sum(
                f_st3[:, 32 + wi:32 + wi + 1], ab_cols[:, wi * 8:wi * 8 + 8], axis=AXX),
                inc=c_v)
        P.op("v", lambda E: E.memset(sc_t[:, 15:16], 1.0), inc=c_v)
        P.wait("t", c_v, c_v.n)
        P.wait("t", dS["const"], const_done)
        for wi in range(3):
            P.op("t", lambda E, wi=wi: E.matmul(
                PF[0][0:1, wi:wi + 1], f_st3[:, 32 + wi:32 + wi + 1], sc_t[:, 15:16],
                start=True, stop=True), inc=c_t)
        P.wait("v", c_t, c_t.n)
        P.op("v", lambda E: E.memset(sc_t[0:1, 0:4], 0.0), flush=c_v)
        P.op("v", lambda E: E.tensor_copy(sc_t[0:1, 0:3], PF[0][0:1, 0:3]), inc=c_v)
        P.wait("y", c_v, c_v.n)
        dma("y", d_arin.ap(), sc_t[0:1, 0:4], dS["ar"])
        P.wait("g", dS["ar"], dS["ar"].n)
        P.op("g", lambda E: E.collective_compute(
            "AllReduce", Alu.add, replica_groups=[list(range(NCORES))],
            ins=[d_arin.ap()], outs=[d_arout.ap()]), inc=cc)
        P.wait("t", c_v, c_v.n)

    def quant_pass2():
        P.wait("g", cc, cc.n)
        P.op("g", lambda E: E.dma_start(out=sc_t[:, 0:4],
                                        in_=d_arout.ap().partition_broadcast(CH)),
             inc=dS["ar"], inck=16)
        P.op("g", lambda E: E.dma_start(out=sc_t[:, 4:8],
                                        in_=d_wninv.partition_broadcast(CH)),
             inc=dS["ar"], inck=16)
        P.wait("v", dS["ar"], dS["ar"].n)
        P.op("v", lambda E: E.tensor_tensor(sc_t[:, 0:4], sc_t[:, 0:4], sc_t[:, 4:8], op=Alu.mult), flush=c_v)
        P.op("v", lambda E: E.tensor_scalar(sc_t[:, 8:12], sc_t[:, 0:4], 1e-8, None, op0=Alu.add), flush=c_v)
        P.op("v", lambda E: E.reciprocal(sc_t[:, 8:12], sc_t[:, 8:12]), flush=c_v)
        P.op("v", lambda E: E.tensor_scalar(sc_t[:, 12:16], sc_t[:, 0:4], -1.0, 1.0,
                                            op0=Alu.mult, op1=Alu.add), flush=c_v)
        P.op("v", lambda E: E.reciprocal(sc_t[:, 12:16], sc_t[:, 12:16]), flush=c_v)
        P.op("v", lambda E: E.tensor_scalar(sc_t[:, 0:4], sc_t[:, 0:4], 1.0, None,
                                            op0=Alu.add), flush=c_v)
        P.op("v", lambda E: E.tensor_tensor(sc_t[:, 0:4], sc_t[:, 0:4], sc_t[:, 12:16],
                                            op=Alu.mult), inc=c_v)
        P.wait("s", c_v, c_v.n)
        P.op("s", lambda E: E.activation(sc_t[:, 12:16], sc_t[:, 0:4], Act.Ln, scale=1.0), flush=c_s)
        P.op("s", lambda E: E.mul(sc_t[:, 12:16], sc_t[:, 12:16], 0.5), inc=c_s)
        meta["a_ready_s"] = c_s.n
        for wi in range(3):
            for ci, chunk in enumerate(shard_in[wi]):
                P.wait("y", c_s, c_s.n)
                dma("y", f_qa[:], chunk, dS["shard"])
                P.wait("s", dS["shard"], dS["shard"].n)
                P.op("s", lambda E, wi=wi: E.activation(
                    f_sq[:], f_qa[:], Act.Tanh, scale=sc_t[:, 8 + wi:9 + wi]), inc=c_s)
                P.wait("v", c_s, c_s.n)
                P.wait("v", dS["tern"], dS["tern"].n)
                P.wait("v", dS["y8"], dS["y8"].n)
                P.op("v", lambda E: E.tensor_scalar(
                    i8a[:], f_sq[:], MAGIC, -MAGIC, op0=Alu.add, op1=Alu.add), inc=c_v)
                P.op("v", lambda E: E.tensor_copy(istg[:], i8a[:]), inc=c_v)
                P.wait("g", c_v, c_v.n)
                P.op("g", lambda E, dst=tern_out[wi][ci]: E.dma_start(out=dst, in_=istg[:]),
                     inc=dS["tern"], inck=16)
        P.wait("g", dS["tern"], dS["tern"].n)
        for din, dout in ((d_t1b, d_w1q), (d_t2b, d_w2q), (d_t3b, d_w3q)):
            P.op("g", lambda E, a=din, b=dout: E.collective_compute(
                "AllGather", Alu.bypass, replica_groups=[list(range(NCORES))],
                ins=[a.ap()], outs=[b.ap()]), inc=cc)
        meta["wq_ready"] = cc.n

    dma("y", dmk[:], d_dmaskT, dS["const"])
    dmask_done = dS["const"].n

    v_s = f_vec[:, 0:1]
    v_ss = f_vec[:, 1:2]
    v_m = f_vec[:, 2:3]
    v_var = f_vec[:, 3:4]
    v_vp = f_vec[:, 4:5]
    v_r = f_vec[:, 5:6]
    v_t0 = f_vec[:, 6:7]
    v_t1 = f_vec[:, 7:8]
    v_t2 = f_vec[:, 8:9]
    v_rr = f_vec[:, 9:10]
    v_mx = f_vec[:, 10:11]
    v_mn = f_vec[:, 11:12]
    v_m1 = f_vec[:, 12:13]
    v_d1 = f_vec[:, 13:14]
    v_c1 = f_vec[:, 14:15]
    invs1 = f_vec[:, 16:24]
    invs2 = f_vec[:, 24:32]
    cu_c = f_vec[:, 32:40]
    cv_c = f_vec[:, 40:48]
    cy3_c = f_vec[:, 16:24]
    c3_c = f_vec[:, 48:56]

    def newton_rsqrt(dst, src_plus):
        P.op("v", lambda E: E.reciprocal(v_t0[:], src_plus), inc=c_v)
        P.wait("s", c_v, c_v.n)
        P.op("s", lambda E: E.activation(dst, v_t0[:], Act.Sqrt), inc=c_s)
        P.wait("v", c_s, c_s.n)
        P.op("v", lambda E: E.tensor_tensor(v_t1[:], dst, dst, op=Alu.mult), flush=c_v)
        P.op("v", lambda E: E.tensor_tensor(v_t1[:], v_t1[:], src_plus, op=Alu.mult), flush=c_v)
        P.op("v", lambda E: E.tensor_scalar(v_t1[:], v_t1[:], -0.5, 1.5,
                                            op0=Alu.mult, op1=Alu.add), flush=c_v)
        P.op("v", lambda E: E.tensor_tensor(dst, dst, v_t1[:], op=Alu.mult), flush=c_v)

    xnT = [xnT0, xnT1]

    def phase1_tile(i):
        b = i // 4
        P.wait("y", c_s, c_s.n)
        P.wait("y", c_v, c_v.n)
        dma("y", f_x[:], d_x[i * CH:(i + 1) * CH, :], dS["x"])
        P.wait("s", dS["x"], dS["x"].n)
        P.wait("s", c_t, c_t.n)
        P.wait("s", c_v, c_v.n)
        P.op("s", lambda E: E.activation(f_xn[:], f_x[:],
                                         Act.Square, accum_out=v_ss[:]), inc=c_s)
        P.wait("v", dS["x"], dS["x"].n)
        P.op("v", lambda E: E.reduce_sum(v_s[:], f_x[:], axis=AXX), inc=c_v)
        P.wait("v", c_s, c_s.n)
        P.op("v", lambda E: E.tensor_scalar(v_m[:], v_s[:], 1.0 / D, None, op0=Alu.mult),
             flush=c_v)
        P.op("v", lambda E: E.tensor_tensor(v_t2[:], v_m[:], v_m[:], op=Alu.mult),
             flush=c_v)
        P.op("v", lambda E: E.scalar_tensor_tensor(v_var[:], v_ss[:], 1.0 / D, v_t2[:],
                                                   op0=Alu.mult, op1=Alu.subtract),
             flush=c_v)
        P.op("v", lambda E: E.tensor_scalar(v_vp[:], v_var[:], 1e-5, None, op0=Alu.add),
             inc=c_v)
        P.wait("v", c_v, c_v.n)
        newton_rsqrt(v_r[:], v_vp[:])
        P.op("v", lambda E: E.scalar_tensor_tensor(v_t2[:], v_m[:], -1.0, v_r[:],
                                                   op0=Alu.mult, op1=Alu.mult), inc=c_v)
        P.wait("s", c_v, c_v.n)
        P.op("s", lambda E: E.activation(f_xn[:], f_x[:], Act.Identity,
                                         scale=v_r[:], bias=v_t2[:]), inc=c_s)
        xn_done = c_s.n
        P.wait("v", c_s, xn_done)
        P.op("v", lambda E: E.reciprocal(v_t0[:], v_vp[:]), flush=c_v)
        P.op("v", lambda E: E.tensor_tensor(v_t1[:], v_var[:], v_t0[:], op=Alu.mult),
             flush=c_v)
        P.op("v", lambda E: E.tensor_scalar(v_t1[:], v_t1[:], 1e-8, None, op0=Alu.add),
             flush=c_v)
        P.op("v", lambda E: E.tensor_copy(v_t2[:], v_t1[:]), flush=c_v)
        newton_rsqrt(v_rr[:], v_t2[:])
        P.wait("v", c_s, xn_done)
        P.wait("v", dS["g3b"], g12_done)
        for br in range(2):
            gsl = g12[:, br * D:(br + 1) * D]
            ydram = d_y1 if br == 0 else d_y2
            invsl = (invs1 if br == 0 else invs2)[:, i:i + 1]
            P.op("v", lambda E, gsl=gsl: E.tensor_tensor(f_t1[:], f_xn[:], gsl,
                                                         op=Alu.mult))
            P.op("v", lambda E: E.reduce_max(v_mx[:], f_t1[:], axis=AXX), flush=c_v)
            P.op("v", lambda E: E.tensor_reduce(v_mn[:], f_t1[:], axis=AXX, op=Alu.min),
                 flush=c_v)
            P.op("v", lambda E: E.tensor_scalar(v_mn[:], v_mn[:], -1.0, None,
                                                op0=Alu.mult), flush=c_v)
            P.op("v", lambda E: E.tensor_tensor(v_m1[:], v_mx[:], v_mn[:], op=Alu.max),
                 flush=c_v)
            P.op("v", lambda E: E.tensor_tensor(v_d1[:], v_rr[:], v_m1[:], op=Alu.mult),
                 flush=c_v)
            P.op("v", lambda E: E.tensor_scalar_max(v_d1[:], v_d1[:], 1e-4), flush=c_v)
            P.op("v", lambda E: E.reciprocal(v_t0[:], v_d1[:]), flush=c_v)
            P.op("v", lambda E: E.scalar_tensor_tensor(v_c1[:], v_rr[:], 127.0, v_t0[:],
                                                       op0=Alu.mult, op1=Alu.mult),
                 flush=c_v)
            P.op("v", lambda E, invsl=invsl: E.tensor_scalar(invsl, v_d1[:], 1.0 / 127.0,
                                                             None, op0=Alu.mult), inc=c_v)
            P.wait("v", c_v, c_v.n)
            P.op("v", lambda E: E.tensor_scalar(f_t1[:], f_t1[:], v_c1[:], MAGIC,
                                                op0=Alu.mult, op1=Alu.add))
            P.wait("v", dS["y8"], dS["y8"].n)
            P.op("v", lambda E: E.tensor_scalar(i8a[:], f_t1[:], -MAGIC, None,
                                                op0=Alu.add), inc=c_v)
            P.wait("g", c_v, c_v.n)
            P.op("g", lambda E, d=ydram, i=i: E.dma_start(
                out=d[i * CH:(i + 1) * CH, :], in_=i8a[:]), inc=dS["y8"], inck=16)
        P.wait("t", c_s, xn_done)
        P.wait("t", dS["const"], const_done)
        for gq in range(4):
            pf = PF[4 + (gq % 2)]
            P.wait("t", c_s, c_s.n)
            P.wait("t", c_v, c_v.n)
            for j in range(4):
                kc = gq * 4 + j
                P.op("t", lambda E, pf=pf, j=j, kc=kc: E.transpose(
                    pf[:, j * CH:(j + 1) * CH], f_xn[:, kc * CH:(kc + 1) * CH],
                    identf_t[:]), inc=(c_t if j == 3 else None))
            P.wait("s", c_t, c_t.n)
            dst = xnT[b][:].rearrange("p (kc t) -> p kc t", kc=16)
            P.op("s", lambda E, pf=pf, gq=gq, b=b, i=i, dst=dst: E.activation(
                dst[:, gq * 4:gq * 4 + 4, (i % 4) * CH:(i % 4 + 1) * CH],
                pf[:].rearrange("p (j t) -> p j t", j=4), Act.Copy), inc=c_s)

    qkv_state = {"slot_mm": [0, 0], "bank_ev": [0, 0], "pan_ld": [0, 0]}

    def qkv_block(b, ocs):
        st = qkv_state
        P.wait("t", c_s, c_s.n)
        for oi, oc in enumerate(ocs):
            slot = oi % 2
            pan = wqp[:, slot * 2048:(slot + 1) * 2048]
            P.wait("g", c_t, st["slot_mm"][slot])
            psem = dS["qkvpan0" if slot == 0 else "qkvpan1"]
            src = d_wqkvT[:, oc * CH:(oc + 1) * CH].rearrange("(kc p) j -> p kc j", p=CH)
            P.op("g", lambda E, pan=pan, src=src: E.dma_start(
                out=pan.rearrange("p (kc j) -> p kc j", kc=16), in_=src),
                inc=psem, inck=16)
            pfq = PF[oi % 2]
            P.wait("t", psem, psem.n)
            P.wait("t", c_s, st["bank_ev"][oi % 2])
            for kc in range(16):
                P.op("t", lambda E, pfq=pfq, pan=pan, kc=kc, b=b: E.matmul(
                    pfq[:], pan[:, kc * CH:(kc + 1) * CH],
                    xnT[b][:, kc * 512:(kc + 1) * 512],
                    start=(kc == 0), stop=(kc == 15)),
                    inc=(c_t if kc == 15 else None))
            st["slot_mm"][slot] = c_t.n
            if oc < 16:
                h = oc
                dst = qfm[:, h * TC + b * 512: h * TC + b * 512 + 512]
            elif oc < 32:
                h = oc - 16
                dst = kvfm[:, h * 512: h * 512 + 512]
            else:
                h = oc - 32
                dst = kvfm[:, 8192 + h * 512: 8192 + h * 512 + 512]
            P.wait("s", c_t, c_t.n)
            P.op("s", lambda E, dst=dst, pfq=pfq, oc=oc: E.activation(
                dst, pfq[:], Act.Identity, bias=bq_t[:, oc:oc + 1]), inc=c_s)
            st["bank_ev"][oi % 2] = c_s.n

    KTS = smalls[:, 0:CH]
    VTM0 = smalls[:, CH:2 * CH]
    VTM1 = smalls[:, 2 * CH:3 * CH]
    SCT0 = smalls[:, 3 * CH:4 * CH]
    SCT1 = smalls[:, 4 * CH:5 * CH]
    SBF = smalls[:, 5 * CH:6 * CH]

    def p3a_block(b):
        for c_ in range(4):
            cg = b * 4 + c_
            for h in range(H):
                ksl = kvfm[:, h * 512 + c_ * CH: h * 512 + (c_ + 1) * CH]
                vsl = kvfm[:, 8192 + h * 512 + c_ * CH: 8192 + h * 512 + (c_ + 1) * CH]
                P.wait("t", c_s, c_s.n)
                P.op("t", lambda E, ksl=ksl: E.transpose(PBA[:, 0:CH], ksl, identb_t[:]),
                     inc=c_t)
                P.op("t", lambda E, vsl=vsl: E.transpose(PBB[:, 0:CH], vsl, identb_t[:]),
                     inc=c_t)
                P.wait("s", c_t, c_t.n)
                P.op("s", lambda E, h=h: E.activation(KTS[:], PBA[:, 0:CH], Act.Identity,
                                                      scale=dk_t[:, h:h + 1]), inc=c_s)
                P.op("s", lambda E: E.activation(VTM0[:], PBB[:, 0:CH], Act.Copy), inc=c_s)
                P.wait("t", c_s, c_s.n)
                P.wait("t", c_v, c_v.n)
                P.op("t", lambda E: E.matmul(PF[2][:, 0:CH], KTS[:], VTM0[:],
                                             start=True, stop=True), inc=c_t)
                P.wait("v", c_t, c_t.n)
                csl = cst[:, (cg * H + h) * CH: (cg * H + h + 1) * CH]
                P.op("v", lambda E, csl=csl: E.tensor_copy(csl, PF[2][:, 0:CH]))
                asl = f_qa[:, h * CH:(h + 1) * CH]
                if cg == 0:
                    P.op("v", lambda E, asl=asl: E.tensor_copy(asl, PF[2][:, 0:CH]),
                         inc=c_v)
                else:
                    P.op("v", lambda E, asl=asl, h=h: E.scalar_tensor_tensor(
                        asl, asl, lc_t[:, h:h + 1], PF[2][:, 0:CH],
                        op0=Alu.mult, op1=Alu.add), inc=c_v)

    def p3b_block(b):
        P.wait("t", dS["const"], dmask_done)
        P.wait("v", dS["const"], dmask_done)
        for c_ in range(4):
            cg = b * 4 + c_
            for h in range(H):
                ksl = kvfm[:, h * 512 + c_ * CH: h * 512 + (c_ + 1) * CH]
                vsl = kvfm[:, 8192 + h * 512 + c_ * CH: 8192 + h * 512 + (c_ + 1) * CH]
                qsl = qfm[:, h * TC + cg * CH: h * TC + (cg + 1) * CH]
                vtm = VTM0 if h % 2 == 0 else VTM1
                sct = SCT0 if h % 2 == 0 else SCT1
                P.wait("t", c_s, c_s.n)
                P.op("t", lambda E, vsl=vsl: E.transpose(PBB[:, 0:CH], vsl, identb_t[:]),
                     inc=c_t)
                P.wait("s", c_t, c_t.n)
                P.op("s", lambda E, vtm=vtm: E.activation(vtm, PBB[:, 0:CH], Act.Copy),
                     inc=c_s)
                P.wait("t", c_v, c_v.n)
                P.op("t", lambda E, ksl=ksl, qsl=qsl: E.matmul(
                    PF[3][:, 0:CH], ksl, qsl, start=True, stop=True), inc=c_t)
                P.wait("v", c_t, c_t.n)
                P.op("v", lambda E, h=h, sct=sct: E.tensor_tensor(
                    sct, PF[3][:, 0:CH], dmk[:, h * CH:(h + 1) * CH], op=Alu.mult),
                    inc=c_v)
                pfi = PF[4 + h % 2]
                P.wait("t", c_v, c_v.n)
                P.wait("t", c_s, c_s.n)
                P.op("t", lambda E, pfi=pfi, sct=sct, vtm=vtm: E.matmul(
                    pfi[:, 0:CH], sct, vtm, start=True, stop=True), inc=c_t)
                P.wait("v", c_t, c_t.n)
                P.wait("v", dS["intra"], dS["intra"].n)
                P.wait("v", dS["tern"], dS["tern"].n)
                P.op("v", lambda E, pfi=pfi, h=h: E.tensor_copy(
                    istg[:, h * CH:(h + 1) * CH], pfi[:, 0:CH]), inc=c_v)
            P.wait("g", c_v, c_v.n)
            P.op("g", lambda E, cg=cg: E.dma_start(
                out=d_intra.ap()[cg * CH:(cg + 1) * CH, :], in_=istg[:]),
                inc=dS["intra"], inck=16)

    for i in range(4):
        phase1_tile(i)
        if i < 3:
            absum_chunks(i)
    absum_finish()
    qkv_block(0, list(range(16, 48)))
    quant_pass2()
    p3a_block(0)
    qkv_block(0, list(range(0, 16)))
    p3b_block(0)
    for i in range(4, 8):
        phase1_tile(i)
    qkv_block(1, list(range(16, 48)))
    p3a_block(1)
    qkv_block(1, list(range(0, 16)))
    p3b_block(1)

    P.wait("g", c_v, c_v.n)
    P.op("g", lambda E: E.dma_start(out=d_Aloc.ap(), in_=f_qa[:]), inc=dS["aall"], inck=16)
    P.wait("g", dS["aall"], dS["aall"].n)
    P.op("g", lambda E: E.collective_compute(
        "AllGather", Alu.bypass, replica_groups=[list(range(NCORES))],
        ins=[d_Aloc.ap()], outs=[d_Aall.ap()]), inc=cc)
    aAG_done = cc.n

    P.wait("v", c_s, c_s.n)
    P.op("v", lambda E: E.memset(f_sq[:], 0.0), inc=c_v)
    P.wait("y", cc, aAG_done)
    for j in range(7):
        P.wait("y", c_v, c_v.n)
        if j == 0:
            P.wait("y", dS["aall"], dS["aall"].n)
        dma("y", f_qa[:], d_Aall.ap()[j * CH:(j + 1) * CH, :], dS["aall"])
        P.wait("v", dS["aall"], dS["aall"].n)
        for h in range(H):
            P.op("v", lambda E, j=j, h=h: E.scalar_tensor_tensor(
                f_sq[:, h * CH:(h + 1) * CH], f_qa[:, h * CH:(h + 1) * CH],
                si_t[:, j * H + h: j * H + h + 1], f_sq[:, h * CH:(h + 1) * CH],
                op0=Alu.mult, op1=Alu.add), inc=(c_v if h == H - 1 else None))

    P.wait("g", c_v, c_v.n)
    P.op("g", lambda E: E.dma_start(out=dmk[:], in_=d_boutb), inc=dS["misc"], inck=16)
    boutb_done = dS["misc"].n

    slot_load = [None, None]

    def load_wout_panel(oc, slot):
        dstp = xnT[slot]
        psem = dS["woutp0" if slot == 0 else "woutp1"]
        src = d_woutT[:, oc * 512:(oc + 1) * 512].rearrange("(kc p) j -> p kc j", p=CH)
        P.op("g", lambda E, dstp=dstp, src=src: E.dma_start(
            out=dstp[:].rearrange("p (kc j) -> p kc j", kc=16), in_=src),
            inc=psem, inck=16)
        slot_load[slot] = (psem, psem.n)

    P.wait("g", c_t, c_t.n)
    load_wout_panel(0, 0)
    load_wout_panel(1, 1)

    for j in range(NT):
        P.wait("y", c_v, c_v.n)
        if j == 0:
            P.wait("y", dS["intra"], dS["intra"].n)
        dma("y", istg[:], d_intra.ap()[j * CH:(j + 1) * CH, :], dS["t3r0"])
        P.wait("v", dS["t3r0"], dS["t3r0"].n)
        P.wait("v", dS["outx"], dS["outx"].n)
        for h in range(H):
            P.wait("v", c_t, c_t.n)
            P.op("v", lambda E, h=h: E.tensor_copy(SBF[:], f_sq[:, h * CH:(h + 1) * CH]),
                 inc=c_v)
            pfi = PF[h % 2]
            P.wait("t", c_v, c_v.n)
            P.op("t", lambda E, pfi=pfi, h=h, j=j: E.matmul(
                pfi[:, 0:CH], qfm[:, h * TC + j * CH: h * TC + (j + 1) * CH], SBF[:],
                start=True, stop=True), inc=c_t)
            P.wait("v", c_t, c_t.n)
            P.op("v", lambda E, pfi=pfi, h=h: E.scalar_tensor_tensor(
                attnt[:, h * CH:(h + 1) * CH], pfi[:, 0:CH], dq_t[:, h:h + 1],
                istg[:, h * CH:(h + 1) * CH], op0=Alu.mult, op1=Alu.add), inc=c_v)
            P.op("v", lambda E, h=h, j=j: E.scalar_tensor_tensor(
                f_sq[:, h * CH:(h + 1) * CH], f_sq[:, h * CH:(h + 1) * CH],
                lc_t[:, h:h + 1], cst[:, (j * H + h) * CH:(j * H + h + 1) * CH],
                op0=Alu.mult, op1=Alu.add), inc=c_v)
        attn_done = c_v.n
        P.wait("t", c_v, attn_done)
        for gq in range(4):
            P.wait("t", c_s, c_s.n)
            for kk in range(4):
                kc = gq * 4 + kk
                P.op("t", lambda E, kc=kc, kk=kk: E.transpose(
                    PBA[:, kk * CH:(kk + 1) * CH], attnt[:, kc * CH:(kc + 1) * CH],
                    identb_t[:]), inc=(c_t if kk == 3 else None))
            P.wait("s", c_t, c_t.n)
            P.op("s", lambda E, gq=gq: E.activation(
                afm[:].rearrange("p (g t) -> p g t", g=16)[:, gq * 4:gq * 4 + 4, :],
                PBA[:, 0:512].rearrange("p (g t) -> p g t", g=4), Act.Copy), inc=c_s)
        afm_done = c_s.n
        P.wait("t", c_s, afm_done)
        for oc in range(4):
            pfo = PF[2 + oc % 2]
            P.wait("t", c_v, c_v.n)
            P.wait("t", slot_load[oc % 2][0], slot_load[oc % 2][1])
            wp_slot = xnT[oc % 2]
            for kc in range(16):
                P.op("t", lambda E, pfo=pfo, wp_slot=wp_slot, kc=kc: E.matmul(
                    pfo[:], afm[:, kc * CH:(kc + 1) * CH],
                    wp_slot[:, kc * 512:(kc + 1) * 512],
                    start=(kc == 0), stop=(kc == 15)), inc=(c_t if kc == 15 else None))
            if oc == 0:
                P.wait("y", c_v, c_v.n)
                dma("y", f_x[:], d_x[j * CH:(j + 1) * CH, :], dS["x"])
                P.wait("v", dS["x"], dS["x"].n)
                P.wait("v", dS["misc"], boutb_done)
            P.wait("v", c_t, c_t.n)
            P.wait("v", dS["outx"], dS["outx"].n)
            P.op("v", lambda E, pfo=pfo, oc=oc: E.scalar_tensor_tensor(
                f_t1[:, oc * 512:(oc + 1) * 512], pfo[:], 1.0,
                f_x[:, oc * 512:(oc + 1) * 512], op0=Alu.mult, op1=Alu.add))
            P.op("v", lambda E, pfo=pfo, oc=oc: E.tensor_tensor(
                attnt[:, oc * 512:(oc + 1) * 512], f_t1[:, oc * 512:(oc + 1) * 512],
                dmk[:, oc * 512:(oc + 1) * 512], op=Alu.add), inc=c_v)
            nxt = (j * 4 + oc + 2)
            if nxt < NT * 4:
                P.wait("g", c_t, c_t.n)
                load_wout_panel(nxt % 4, oc % 2)
        P.wait("g", c_v, c_v.n)
        P.op("g", lambda E, j=j: E.dma_start(
            out=d_outx.ap()[j * CH:(j + 1) * CH, :], in_=attnt[:]),
            inc=dS["outx"], inck=16)

    P.barrier(bar)
    g3t = kvfm[:, 8192:16384]
    P.op("g", lambda E: E.dma_start(out=g3t, in_=d_g3b), inc=dS["g3b"], inck=16)
    g3_done = dS["g3b"].n

    y1T = qfm
    y2T = cst
    for br in range(2):
        ydram = d_y1 if br == 0 else d_y2
        ytgt = y1T if br == 0 else y2T
        for i in range(NT):
            P.wait("y", c_v, c_v.n)
            P.wait("y", dS["y8"], dS["y8"].n)
            dma("y", i8a[:], ydram.ap()[i * CH:(i + 1) * CH, :], dS["y8r"])
            P.wait("v", dS["y8r"], dS["y8r"].n)
            P.wait("v", c_t, c_t.n)
            P.op("v", lambda E: E.tensor_copy(f_t1[:], i8a[:]), inc=c_v)
            P.wait("t", c_v, c_v.n)
            for gq in range(4):
                pf = PF[4 + gq % 2]
                P.wait("t", c_s, c_s.n)
                for kk in range(4):
                    kc = gq * 4 + kk
                    P.op("t", lambda E, pf=pf, kk=kk, kc=kc: E.transpose(
                        pf[:, kk * CH:(kk + 1) * CH], f_t1[:, kc * CH:(kc + 1) * CH],
                        identf_t[:]), inc=(c_t if kk == 3 else None))
                P.wait("s", c_t, c_t.n)
                dstr = ytgt[:].rearrange("p (kc t) -> p kc t", kc=16)
                P.op("s", lambda E, pf=pf, gq=gq, i=i, dstr=dstr: E.activation(
                    dstr[:, gq * 4:gq * 4 + 4, i * CH:(i + 1) * CH],
                    pf[:].rearrange("p (j t) -> p j t", j=4), Act.Copy), inc=c_s)

    P.wait("v", c_s, meta["a_ready_s"])
    P.op("v", lambda E: E.tensor_single_scalar(cu_c[:], invs1[:], sc_t[:, 12:13],
                                               op=Alu.mult), flush=c_v)
    P.op("v", lambda E: E.tensor_single_scalar(cv_c[:], invs2[:], sc_t[:, 13:14],
                                               op=Alu.mult), inc=c_v)

    P.op("v", lambda E: E.memset(f_st3[:], 0.0), inc=c_v)
    P.op("v", lambda E: E.memset(f_st3[:, 128:256], -3.0e38))
    P.op("v", lambda E: E.memset(f_st3[:, 256:384], 3.0e38), inc=c_v)

    pan_slots = [xnT0, xnT1, kvfm[:, 0:8192]]
    HCH = f_xn[:, 0:512]
    SILUS = [f_xn[:, 512:1024], f_xn[:, 1536:2048]]
    T3S0 = attnt[:, 0:512]
    T3S1 = attnt[:, 512:1024]
    P.wait("y", cc, meta["wq_ready"])
    P.wait("t", bar, bar.n)
    P.wait("s", bar, bar.n)
    slot_sem_t = [0, 0, 0]
    ubank_ev = [c_s.n, c_s.n]
    vbank_ev = [c_v.n, c_v.n]
    silu_free = [c_v.n, c_v.n]

    pan_i = 0
    for oc in range(16):
        for w_i, dram in ((0, d_w1q), (1, d_w2q)):
            slot = pan_i % 3
            pan_i += 1
            P.wait("y", c_t, slot_sem_t[slot])
            src = dram.ap()[:, oc * 512:(oc + 1) * 512].rearrange(
                "(kc p) j -> p kc j", p=CH)
            P.op("y", lambda E, slot=slot, src=src: E.dma_start(
                out=pan_slots[slot][:].rearrange("p (kc j) -> p kc j", kc=16), in_=src),
                inc=dS["ffnpan"], inck=16)
        slot_u = (pan_i - 2) % 3
        slot_v = (pan_i - 1) % 3
        P.wait("t", dS["ffnpan"], dS["ffnpan"].n)
        if oc == 0:
            P.wait("s", dS["g3b"], g3_done)
            P.wait("v", dS["g3b"], g3_done)
            P.wait("v", dS["outx"], dS["outx"].n)
        for t_ in range(NT):
            pu = PF[t_ % 2]
            pv = PF[2 + t_ % 2]
            P.wait("t", c_s, ubank_ev[t_ % 2])
            for kc in range(16):
                P.op("t", lambda E, pu=pu, kc=kc, t_=t_, slot_u=slot_u: E.matmul(
                    pu[:], y1T[:, kc * TC + t_ * CH: kc * TC + (t_ + 1) * CH],
                    pan_slots[slot_u][:, kc * 512:(kc + 1) * 512],
                    start=(kc == 0), stop=(kc == 15)), inc=(c_t if kc == 15 else None))
            u_mm = c_t.n
            P.wait("t", c_v, vbank_ev[t_ % 2])
            for kc in range(16):
                P.op("t", lambda E, pv=pv, kc=kc, t_=t_, slot_v=slot_v: E.matmul(
                    pv[:], y2T[:, kc * TC + t_ * CH: kc * TC + (t_ + 1) * CH],
                    pan_slots[slot_v][:, kc * 512:(kc + 1) * 512],
                    start=(kc == 0), stop=(kc == 15)), inc=(c_t if kc == 15 else None))
            v_mm = c_t.n
            SILU = SILUS[t_ % 2]
            P.wait("s", c_t, u_mm)
            P.wait("s", c_v, silu_free[t_ % 2])
            P.op("s", lambda E, pu=pu, t_=t_, SILU=SILU: E.activation(
                SILU, pu[:], Act.Silu, scale=cu_c[:, t_:t_ + 1]), inc=c_s)
            ubank_ev[t_ % 2] = c_s.n
            P.wait("v", c_t, v_mm)
            P.wait("v", c_s, c_s.n)
            P.op("v", lambda E, pv=pv, t_=t_, SILU=SILU: E.scalar_tensor_tensor(
                HCH, pv[:], cv_c[:, t_:t_ + 1], SILU, op0=Alu.mult, op1=Alu.mult),
                inc=c_v)
            vbank_ev[t_ % 2] = c_v.n
            silu_free[t_ % 2] = c_v.n
            P.wait("s", c_v, c_v.n)
            P.op("s", lambda E: E.activation(f_xn[:, 1024:1536], HCH, Act.Square,
                                             accum_out=v_t0[:]), inc=c_s)
            P.wait("v", c_s, c_s.n)
            col = f_st3[:, t_ * 16 + oc: t_ * 16 + oc + 1]
            P.op("v", lambda E, col=col: E.tensor_copy(col, v_t0[:]), flush=c_v)
            t3s = T3S0 if oc % 2 == 0 else T3S1
            P.wait("v", dS["t3w"], dS["t3w"].n)
            P.op("v", lambda E, t3s=t3s, oc=oc: E.tensor_tensor(
                t3s, HCH, g3t[:, oc * 512:(oc + 1) * 512], op=Alu.mult))
            mxc = f_st3[:, 128 + t_ * 16 + oc: 128 + t_ * 16 + oc + 1]
            mnc = f_st3[:, 256 + t_ * 16 + oc: 256 + t_ * 16 + oc + 1]
            P.op("v", lambda E, t3s=t3s, mxc=mxc: E.reduce_max(mxc, t3s, axis=AXX),
                 flush=c_v)
            P.op("v", lambda E, t3s=t3s, mnc=mnc: E.tensor_reduce(mnc, t3s, axis=AXX,
                                                                  op=Alu.min), inc=c_v)
            P.wait("g", c_v, c_v.n)
            P.op("g", lambda E, t3s=t3s, t_=t_, oc=oc: E.dma_start(
                out=d_t3.ap()[t_ * CH:(t_ + 1) * CH, oc * 512:(oc + 1) * 512], in_=t3s),
                inc=dS["t3w"], inck=16)
        slot_sem_t[slot_u] = c_t.n
        slot_sem_t[slot_v] = c_t.n

    P.wait("v", c_s, c_s.n)
    for t_ in range(NT):
        P.op("v", lambda E, t_=t_: E.reduce_sum(v_ss[:], f_st3[:, t_ * 16:t_ * 16 + 16],
                                                axis=AXX), flush=c_v)
        P.op("v", lambda E, t_=t_: E.reduce_max(v_mx[:],
                                                f_st3[:, 128 + t_ * 16:128 + t_ * 16 + 16],
                                                axis=AXX), flush=c_v)
        P.op("v", lambda E, t_=t_: E.tensor_reduce(
            v_mn[:], f_st3[:, 256 + t_ * 16:256 + t_ * 16 + 16], axis=AXX, op=Alu.min),
            flush=c_v)
        P.op("v", lambda E: E.tensor_scalar(v_mn[:], v_mn[:], -1.0, None, op0=Alu.mult),
             flush=c_v)
        P.op("v", lambda E: E.tensor_tensor(v_m1[:], v_mx[:], v_mn[:], op=Alu.max),
             flush=c_v)
        P.op("v", lambda E: E.tensor_scalar(v_t1[:], v_ss[:], 1.0 / HID, 1e-8,
                                            op0=Alu.mult, op1=Alu.add), flush=c_v)
        P.op("v", lambda E: E.tensor_copy(v_t2[:], v_t1[:]), flush=c_v)
        newton_rsqrt(v_rr[:], v_t2[:])
        P.op("v", lambda E: E.tensor_tensor(v_d1[:], v_rr[:], v_m1[:], op=Alu.mult),
             flush=c_v)
        P.op("v", lambda E: E.tensor_scalar_max(v_d1[:], v_d1[:], 1e-4), flush=c_v)
        P.op("v", lambda E: E.reciprocal(v_t0[:], v_d1[:]), flush=c_v)
        P.op("v", lambda E, t_=t_: E.scalar_tensor_tensor(
            c3_c[:, t_:t_ + 1], v_rr[:], 127.0, v_t0[:], op0=Alu.mult, op1=Alu.mult),
            flush=c_v)
        P.op("v", lambda E, t_=t_: E.tensor_scalar(cy3_c[:, t_:t_ + 1], v_d1[:],
                                                   1.0 / 127.0, None, op0=Alu.mult),
             inc=c_v)
    P.wait("v", c_v, c_v.n)
    P.op("v", lambda E: E.tensor_single_scalar(cy3_c[:], cy3_c[:], sc_t[:, 14:15],
                                               op=Alu.mult), inc=c_v)

    P.barrier(bar)
    y3T = cst
    w3p = [wqp[:, 0:2048], wqp[:, 2048:4096]]
    OXB = dmk

    afm_use = [0, 0, 0, 0]
    ft1_use = [0, 0, 0, 0]
    pfq_ev = [0, 0]
    t3r_ld = [0, 0, 0, 0]
    for g_ in range(4):
        tiles = (2 * g_, 2 * g_ + 1)
        for ti, t_ in enumerate(tiles):
            for oc in range(16):
                it = (ti * 16 + oc) % 4
                asl = afm[:, it * 512:(it + 1) * 512]
                fsl = f_t1[:, it * 512:(it + 1) * 512]
                P.wait("y", c_v, afm_use[it])
                psem = dS["t3r%d" % it]
                dma("y", asl,
                    d_t3.ap()[t_ * CH:(t_ + 1) * CH, oc * 512:(oc + 1) * 512],
                    psem)
                P.wait("v", psem, psem.n)
                P.wait("v", c_t, ft1_use[it])
                P.op("v", lambda E, t_=t_, asl=asl, fsl=fsl: E.tensor_scalar(
                    fsl, asl, c3_c[:, t_:t_ + 1], MAGIC,
                    op0=Alu.mult, op1=Alu.add), inc=c_v)
                afm_use[it] = c_v.n
                q_ready = c_v.n
                pf = PF[4 + oc % 2]
                P.wait("t", c_v, q_ready)
                P.wait("t", c_v, pfq_ev[oc % 2])
                for kk in range(4):
                    P.op("t", lambda E, pf=pf, kk=kk, fsl=fsl: E.transpose(
                        pf[:, kk * CH:(kk + 1) * CH], fsl[:, kk * CH:(kk + 1) * CH],
                        identf_t[:]), inc=(c_t if kk == 3 else None))
                ft1_use[it] = c_t.n
                P.wait("v", c_t, ft1_use[it])
                P.op("v", lambda E, pf=pf: E.tensor_scalar(
                    i16a[:], pf[:], -MAGIC, None, op0=Alu.add))
                dstr = y3T[:].rearrange("p (kc t) -> p kc t", kc=64)
                P.op("v", lambda E, oc=oc, ti=ti, dstr=dstr: E.tensor_copy(
                    dstr[:, oc * 4:oc * 4 + 4, ti * CH:(ti + 1) * CH],
                    i16a[:].rearrange("p (j t) -> p j t", j=4)), inc=c_v)
                pfq_ev[oc % 2] = c_v.n
        y3r = y3T[:].rearrange("p (kc t) -> p kc t", kc=64)
        w3_slot_mm = [0, 0]
        w3_slot_ld = [0, 0]
        for sweep, ocs in ((0, (0, 1, 2)), (1, (3,))):
            for kc in range(64):
                slot = kc % 2
                P.wait("y", c_t, w3_slot_mm[slot])
                psem = dS["w3p0" if slot == 0 else "w3p1"]
                if sweep == 0:
                    src = d_w3q.ap()[kc * CH:(kc + 1) * CH, 0:1536]
                    P.op("y", lambda E, slot=slot, src=src: E.dma_start(
                        out=w3p[slot][:, 0:1536], in_=src), inc=psem, inck=16)
                else:
                    src = d_w3q.ap()[kc * CH:(kc + 1) * CH, 1536:2048]
                    P.op("y", lambda E, slot=slot, src=src: E.dma_start(
                        out=w3p[slot][:, 0:512], in_=src), inc=psem, inck=16)
                P.wait("t", psem, psem.n)
                if kc == 0:
                    P.wait("t", c_v, c_v.n)
                nmm = 2 * len(ocs)
                mi = 0
                for ti in range(2):
                    for oi, oc in enumerate(ocs):
                        mi += 1
                        pf = PF[ti * 3 + oi] if sweep == 0 else PF[ti]
                        off = oi * 512
                        P.op("t", lambda E, pf=pf, slot=slot, off=off, kc=kc, ti=ti:
                             E.matmul(pf[:], y3r[:, kc, ti * CH:(ti + 1) * CH],
                                      w3p[slot][:, off:off + 512],
                                      start=(kc == 0), stop=(kc == 63)),
                             inc=(c_t if mi == nmm else None))
                w3_slot_mm[slot] = c_t.n
            mm_done = c_t.n
            for ti, t_ in enumerate(tiles):
                P.wait("y", c_v, c_v.n)
                if g_ == 0 and sweep == 0 and ti == 0:
                    P.wait("y", dS["misc"], dS["misc"].n)
                dma("y", OXB[:], d_outx.ap()[t_ * CH:(t_ + 1) * CH, :], dS["outxr"])
                P.wait("v", c_t, mm_done)
                P.wait("v", dS["outxr"], dS["outxr"].n)
                P.wait("v", dS["out"], dS["out"].n)
                for oi, oc in enumerate(ocs):
                    pf = PF[ti * 3 + oi] if sweep == 0 else PF[ti]
                    P.op("v", lambda E, pf=pf, t_=t_, oc=oc: E.scalar_tensor_tensor(
                        f_sq[:, oc * 512:(oc + 1) * 512], pf[:], cy3_c[:, t_:t_ + 1],
                        OXB[:, oc * 512:(oc + 1) * 512], op0=Alu.mult, op1=Alu.add),
                        inc=c_v)
                P.wait("y", c_v, c_v.n)
                for oi, oc in enumerate(ocs):
                    P.op("y", lambda E, t_=t_, oc=oc: E.dma_start(
                        out=d_out[t_ * CH:(t_ + 1) * CH, oc * 512:(oc + 1) * 512],
                        in_=f_sq[:, oc * 512:(oc + 1) * 512]), inc=dS["out"], inck=16)

    P.wait("y", dS["out"], dS["out"].n)
    P.wait("g", dS["out"], dS["out"].n)

    with nc.Block() as block:
        @block.tensor
        def _(E):
            for f in P.q["t"]:
                f(E)

        @block.vector
        def _(E):
            for f in P.q["v"]:
                f(E)

        @block.scalar
        def _(E):
            for f in P.q["s"]:
                f(E)

        @block.gpsimd
        def _(E):
            for f in P.q["g"]:
                f(E)

        @block.sync
        def _(E):
            for f in P.q["y"]:
                f(E)

    return nc


_BUILT = {}


def kernel(L0_attn_x, L0_attn_w_qkv, L0_attn_b_qkv, L0_attn_w_out, L0_attn_b_out,
           L0_ffn_w1, L0_ffn_g1, L0_ffn_w2, L0_ffn_g2, L0_ffn_w3, L0_ffn_g3):
    const = host_constants()
    if "nc" not in _BUILT:
        _BUILT["nc"] = build_nc(const)
    nc = _BUILT["nc"]

    x = np.ascontiguousarray(np.asarray(L0_attn_x, np.float32).reshape(8192, D))
    wqkvT = np.ascontiguousarray(np.asarray(L0_attn_w_qkv, np.float32).T)
    bqkv = np.asarray(L0_attn_b_qkv, np.float32)
    bqkv_fm = np.ascontiguousarray(bqkv.reshape(48, CH).T)
    woutT = np.ascontiguousarray(np.asarray(L0_attn_w_out, np.float32).T)
    boutb = np.ascontiguousarray(
        np.broadcast_to(np.asarray(L0_attn_b_out, np.float32)[None, :], (CH, D)))
    w1T = np.ascontiguousarray(np.asarray(L0_ffn_w1, np.float32).T)
    w2T = np.ascontiguousarray(np.asarray(L0_ffn_w2, np.float32).T)
    w3T = np.ascontiguousarray(np.asarray(L0_ffn_w3, np.float32).T)
    g12b = np.ascontiguousarray(np.broadcast_to(
        np.concatenate([np.asarray(L0_ffn_g1, np.float32),
                        np.asarray(L0_ffn_g2, np.float32)])[None, :], (CH, 2 * D)))
    g3b = np.ascontiguousarray(
        np.broadcast_to(np.asarray(L0_ffn_g3, np.float32)[None, :], (CH, HID)))

    in_maps = []
    for c in range(NCORES):
        m = {
            "x_slice": np.ascontiguousarray(x[c * TC:(c + 1) * TC]),
            "wqkvT": wqkvT,
            "bqkv_fm": bqkv_fm,
            "woutT": woutT,
            "bout_b": boutb,
            "w1T_shard": np.ascontiguousarray(w1T[c * 256:(c + 1) * 256]),
            "w2T_shard": np.ascontiguousarray(w2T[c * 256:(c + 1) * 256]),
            "w3T_shard": np.ascontiguousarray(w3T[c * 1024:(c + 1) * 1024]),
            "g12_b": g12b,
            "g3_b": g3b,
            "dmaskT": const["dmaskT"],
            "decayqcol": const["decayqcol"],
            "decaykb": const["decaykb"],
            "lamcb": const["lamcb"],
            "identf": const["identf"],
            "identb": const["identb"],
            "sinit_w": const["sinit"][c],
            "wnumel_inv": const["wnumel_inv"],
        }
        in_maps.append(m)

    res = run_bass_kernel_spmd(nc, in_maps, core_ids=list(range(NCORES)))
    out = np.concatenate([res.results[c]["out_slice"] for c in range(NCORES)], axis=0)
    return out.reshape(1, 8192, D)


# version 08
# speedup vs baseline: 1.5306x; 1.0583x over previous
import math
import numpy as np
import ml_dtypes

import concourse.bass as bass
import concourse.mybir as mybir
from concourse.bass_utils import run_bass_kernel_spmd

F32 = mybir.dt.float32
BF16 = mybir.dt.bfloat16
I16 = mybir.dt.int16
I8 = mybir.dt.int8
Act = mybir.ActivationFunctionType
Alu = mybir.AluOpType
AXX = mybir.AxisListType.X

MAGIC = 12582912.0
NCORES = 8
TC = 1024
D = 2048
HID = 8192
H = 16
CH = 128
NT = 8
NB = 2
LN3 = 1.0986122886681098


def _slopes(n):
    start = 2.0 ** (-2.0 ** (-(math.log2(n) - 3)))
    return np.array([start ** (i + 1) for i in range(n)], dtype=np.float64)


def host_constants():
    lam = np.exp(-_slopes(H))
    pos = np.arange(CH, dtype=np.float64)
    decay_q = lam[:, None] ** (pos + 1.0)
    decay_k = lam[:, None] ** (CH - 1.0 - pos)
    diff = pos[:, None] - pos[None, :]
    dmask = np.where(diff >= 0, lam[:, None, None] ** np.maximum(diff, 0), 0.0)
    lam_c = lam ** CH
    lam_sl = lam ** TC
    c = {}
    dmT = np.zeros((CH, H * CH), np.float64)
    for h in range(H):
        dmT[:, h * CH:(h + 1) * CH] = dmask[h].T
    c["dmaskT"] = dmT.astype(ml_dtypes.bfloat16)
    c["decayqcol"] = np.broadcast_to(decay_q.T, (CH, H)).astype(np.float32).copy()
    c["decaykb"] = np.broadcast_to(decay_k.T, (CH, H)).astype(np.float32).copy()
    c["lamcb"] = np.tile(lam_c.astype(np.float32)[None, :], (CH, 1))
    c["identf"] = np.eye(CH, dtype=np.float32)
    c["identb"] = np.eye(CH, dtype=np.float32).astype(ml_dtypes.bfloat16)
    c["ones_col"] = np.ones((CH, 1), np.float32)
    sinit = []
    for core in range(NCORES):
        w = np.zeros((7, H), np.float64)
        for j in range(core):
            w[j] = lam_sl ** (core - 1 - j)
        sinit.append(np.tile(w.reshape(1, 7 * H), (CH, 1)).astype(np.float32))
    c["sinit"] = sinit
    c["wnumel_inv"] = np.array(
        [[1.0 / (HID * D), 1.0 / (HID * D), 1.0 / (D * HID), 0.0]], np.float32)
    return c


class Sem:
    def __init__(self, handle, name):
        self.h = handle
        self.n = 0
        self.name = name


class Prog:

    ENGS = ("t", "v", "s", "g", "y")

    def __init__(self):
        self.q = {e: [] for e in self.ENGS}

    def op(self, eng, fn, inc=None, inck=1, flush=None):
        if flush is not None:
            assert inc is None
            flush.n += 1
            v = flush.n
            self.q[eng].append(lambda E: fn(E).then_inc(flush.h, 1))
            self.q[eng].append(lambda E: E.wait_ge(flush.h, v))
        elif inc is not None:
            inc.n += inck
            self.q[eng].append(lambda E: fn(E).then_inc(inc.h, inck))
        else:
            self.q[eng].append(lambda E: fn(E))

    def wait(self, eng, sem, val=None):
        v = sem.n if val is None else val
        self.q[eng].append(lambda E: E.wait_ge(sem.h, v))

    def barrier(self, sem, engs=ENGS):
        for e in engs:
            self.q[e].append(lambda E: E.nop().then_inc(sem.h, 1))
        sem.n += len(engs)
        tgt = sem.n
        for e in engs:
            self.q[e].append(lambda E: E.wait_ge(sem.h, tgt))


def build_nc(const):
    nc = bass.Bass("TRN2", target_bir_lowering=False, debug=False)

    d_x = nc.dram_tensor("x_slice", [TC, D], F32, kind="ExternalInput").ap()
    d_wqkvT = nc.dram_tensor("wqkvT", [D, 3 * D], F32, kind="ExternalInput").ap()
    d_bqkv_fm = nc.dram_tensor("bqkv_fm", [CH, 48], F32, kind="ExternalInput").ap()
    d_woutT = nc.dram_tensor("woutT", [D, D], F32, kind="ExternalInput").ap()
    d_boutb = nc.dram_tensor("bout_b", [CH, D], F32, kind="ExternalInput").ap()
    d_w1sh = nc.dram_tensor("w1T_shard", [D // 8, HID], F32, kind="ExternalInput").ap()
    d_w2sh = nc.dram_tensor("w2T_shard", [D // 8, HID], F32, kind="ExternalInput").ap()
    d_w3sh = nc.dram_tensor("w3T_shard", [HID // 8, D], F32, kind="ExternalInput").ap()
    d_g12b = nc.dram_tensor("g12_b", [CH, 2 * D], F32, kind="ExternalInput").ap()
    d_g3b = nc.dram_tensor("g3_b", [CH, HID], F32, kind="ExternalInput").ap()
    d_dmaskT = nc.dram_tensor("dmaskT", [CH, H * CH], BF16, kind="ExternalInput").ap()
    d_dqcol = nc.dram_tensor("decayqcol", [CH, H], F32, kind="ExternalInput").ap()
    d_dkcol = nc.dram_tensor("decaykb", [CH, H], F32, kind="ExternalInput").ap()
    d_lamcb = nc.dram_tensor("lamcb", [CH, H], F32, kind="ExternalInput").ap()
    d_identf = nc.dram_tensor("identf", [CH, CH], F32, kind="ExternalInput").ap()
    d_identb = nc.dram_tensor("identb", [CH, CH], BF16, kind="ExternalInput").ap()
    d_sinit = nc.dram_tensor("sinit_w", [CH, 7 * H], F32, kind="ExternalInput").ap()
    d_wninv = nc.dram_tensor("wnumel_inv", [1, 4], F32, kind="ExternalInput").ap()

    d_out = nc.dram_tensor("out_slice", [TC, D], F32, kind="ExternalOutput").ap()
    d_arin = nc.dram_tensor("ar_in", [1, 4], F32)
    d_arout = nc.dram_tensor("ar_out", [1, 4], F32)
    d_t1b = nc.dram_tensor("tern1_sh", [D // 8, HID], BF16)
    d_t2b = nc.dram_tensor("tern2_sh", [D // 8, HID], BF16)
    d_t3b = nc.dram_tensor("tern3_sh", [HID // 8, D], BF16)
    d_w1q = nc.dram_tensor("w1q", [D, HID], BF16)
    d_w2q = nc.dram_tensor("w2q", [D, HID], BF16)
    d_w3q = nc.dram_tensor("w3q", [HID, D], BF16)
    d_Aloc = nc.dram_tensor("A_loc", [CH, H * CH], F32)
    d_Aall = nc.dram_tensor("A_all", [NCORES * CH, H * CH], F32)
    d_y1 = nc.dram_tensor("y1_i8", [TC, D], I8)
    d_y2 = nc.dram_tensor("y2_i8", [TC, D], I8)
    d_intra = nc.dram_tensor("intra_d", [TC, D], BF16)
    d_outx = nc.dram_tensor("outx_d", [TC, D], BF16)
    d_t3 = nc.dram_tensor("t3_d", [TC, HID], BF16)

    A = nc.alloc_sbuf_tensor
    sb = {}

    def S(name, shape, dt):
        sb[name] = A(name, shape, dt)
        return sb[name]

    xnT0 = S("xnT0", [CH, 16 * 512], BF16)
    xnT1 = S("xnT1", [CH, 16 * 512], BF16)
    kvfm = S("kvfm", [CH, 2 * 8192], BF16)
    qfm = S("qfm", [CH, H * TC], BF16)
    cst = S("cst", [CH, 16384], BF16)
    istg = S("istg", [CH, D], BF16)
    wqp = S("wqp", [CH, 2 * 2048], BF16)
    dmk = S("dmk", [CH, H * CH], BF16)
    g12 = S("g12", [CH, 2 * D], BF16)
    attnt = S("attnt", [CH, D], BF16)
    afm = S("afm", [CH, D], BF16)
    smalls = S("smalls", [CH, 6 * CH], BF16)
    identb_t = S("identb_t", [CH, CH], BF16)
    f_sq = S("f_sq", [CH, D], F32)
    f_x = S("f_x", [CH, D], F32)
    f_xn = S("f_xn", [CH, D], F32)
    f_t1 = S("f_t1", [CH, D], F32)
    f_qa = S("f_qa", [CH, D], F32)
    f_vec = S("f_vec", [CH, 56], F32)
    f_st3 = S("f_st3", [CH, 3 * 128], F32)
    identf_t = S("identf_t", [CH, CH], F32)
    bq_t = S("bq_t", [CH, 48], F32)
    dq_t = S("dq_t", [CH, H], F32)
    dk_t = S("dk_t", [CH, H], F32)
    lc_t = S("lc_t", [CH, H], F32)
    si_t = S("si_t", [CH, 7 * H], F32)
    sc_t = S("sc_t", [CH, 16], F32)
    i8a = S("i8a", [CH, D], I8)
    i16a = S("i16a", [CH, 512], I16)

    rem = nc.sbuf_bytes_remaining() if callable(getattr(nc, "sbuf_bytes_remaining", None)) else None

    PF = [nc.alloc_psum_tensor if False else None]
    psum_guards = []
    PF = []
    for i in range(6):
        g = nc.psum_tensor(f"pf{i}", [CH, 512], F32)
        t = g.__enter__()
        psum_guards.append(g)
        PF.append(t)
    PBA_g = nc.psum_tensor("pba", [CH, 1024], BF16)
    PBA = PBA_g.__enter__()
    psum_guards.append(PBA_g)
    PBB_g = nc.psum_tensor("pbb", [CH, 1024], BF16)
    PBB = PBB_g.__enter__()
    psum_guards.append(PBB_g)

    sem_guards = []
    sems = {}

    def mksem(name):
        g = nc.semaphore(name)
        h = g.__enter__()
        sem_guards.append(g)
        sems[name] = Sem(h, name)
        return sems[name]

    dS = {n: mksem("d_" + n) for n in
          ("const", "x", "shard", "tern", "ar", "y8", "intra",
           "aall", "outx", "ffnpan", "t3w", "y8r",
           "outxr", "out", "g3b", "misc",
           "qkvpan0", "qkvpan1", "woutp0", "woutp1", "w3p0", "w3p1",
           "t3r0", "t3r1", "t3r2", "t3r3")}
    c_t = mksem("c_t")
    c_v = mksem("c_v")
    c_s = mksem("c_s")
    c_g = mksem("c_g")
    cc = mksem("cc")
    bar = mksem("bar")

    P = Prog()

    def dma(eng, dst, src, sem):
        P.op(eng, lambda E, d=dst, s=src: E.dma_start(out=d, in_=s), inc=sem, inck=16)

    dma("y", identf_t[:], d_identf, dS["const"])
    dma("y", identb_t[:], d_identb, dS["const"])
    dma("y", bq_t[:], d_bqkv_fm, dS["const"])
    dma("y", dq_t[:], d_dqcol, dS["const"])
    dma("y", dk_t[:], d_dkcol, dS["const"])
    dma("y", lc_t[:], d_lamcb, dS["const"])
    dma("y", si_t[:], d_sinit, dS["const"])
    dma("g", g12[:], d_g12b, dS["g3b"])
    const_done = dS["const"].n
    g12_done = dS["g3b"].n

    def shard_chunks(dram, wi):
        out = []
        if wi < 2:
            for pc in range(2):
                for cc_ in range(4):
                    out.append(dram[pc * CH:(pc + 1) * CH, cc_ * D:(cc_ + 1) * D])
        else:
            for pc in range(8):
                out.append(dram[pc * CH:(pc + 1) * CH, :])
        return out

    def tern_chunks(dram, wi):
        return shard_chunks(dram, wi)

    shard_in = [shard_chunks(d_w1sh, 0), shard_chunks(d_w2sh, 1), shard_chunks(d_w3sh, 2)]
    tern_out = [tern_chunks(d_t1b, 0), tern_chunks(d_t2b, 1), tern_chunks(d_t3b, 2)]

    ab_cols = f_st3[:, 0:32]
    meta = {}

    def absum_chunks(wi):
        for ci, chunk in enumerate(shard_in[wi]):
            dma("y", f_qa[:], chunk, dS["shard"])
            P.wait("s", dS["shard"], dS["shard"].n)
            P.op("s", lambda E, ci=ci, wi=wi: E.activation(
                f_sq[:], f_qa[:], Act.Abs,
                accum_out=ab_cols[:, wi * 8 + ci: wi * 8 + ci + 1]), inc=c_s)
            P.wait("y", c_s, c_s.n)

    def absum_finish():
        P.wait("v", c_s, c_s.n)
        for wi in range(3):
            P.op("v", lambda E, wi=wi: E.reduce_sum(
                f_st3[:, 32 + wi:32 + wi + 1], ab_cols[:, wi * 8:wi * 8 + 8], axis=AXX),
                inc=c_v)
        P.op("v", lambda E: E.memset(sc_t[:, 15:16], 1.0), inc=c_v)
        P.wait("t", c_v, c_v.n)
        P.wait("t", dS["const"], const_done)
        for wi in range(3):
            P.op("t", lambda E, wi=wi: E.matmul(
                PF[0][0:1, wi:wi + 1], f_st3[:, 32 + wi:32 + wi + 1], sc_t[:, 15:16],
                start=True, stop=True), inc=c_t)
        P.wait("v", c_t, c_t.n)
        P.op("v", lambda E: E.memset(sc_t[0:1, 0:4], 0.0), flush=c_v)
        P.op("v", lambda E: E.tensor_copy(sc_t[0:1, 0:3], PF[0][0:1, 0:3]), inc=c_v)
        P.wait("y", c_v, c_v.n)
        dma("y", d_arin.ap(), sc_t[0:1, 0:4], dS["ar"])
        P.wait("g", dS["ar"], dS["ar"].n)
        P.op("g", lambda E: E.collective_compute(
            "AllReduce", Alu.add, replica_groups=[list(range(NCORES))],
            ins=[d_arin.ap()], outs=[d_arout.ap()]), inc=cc)
        P.wait("t", c_v, c_v.n)

    def quant_pass2():
        P.wait("g", cc, cc.n)
        P.op("g", lambda E: E.dma_start(out=sc_t[:, 0:4],
                                        in_=d_arout.ap().partition_broadcast(CH)),
             inc=dS["ar"], inck=16)
        P.op("g", lambda E: E.dma_start(out=sc_t[:, 4:8],
                                        in_=d_wninv.partition_broadcast(CH)),
             inc=dS["ar"], inck=16)
        P.wait("v", dS["ar"], dS["ar"].n)
        P.op("v", lambda E: E.tensor_tensor(sc_t[:, 0:4], sc_t[:, 0:4], sc_t[:, 4:8], op=Alu.mult), flush=c_v)
        P.op("v", lambda E: E.tensor_scalar(sc_t[:, 8:12], sc_t[:, 0:4], 1e-8, None, op0=Alu.add), flush=c_v)
        P.op("v", lambda E: E.reciprocal(sc_t[:, 8:12], sc_t[:, 8:12]), flush=c_v)
        P.op("v", lambda E: E.tensor_scalar(sc_t[:, 12:16], sc_t[:, 0:4], -1.0, 1.0,
                                            op0=Alu.mult, op1=Alu.add), flush=c_v)
        P.op("v", lambda E: E.reciprocal(sc_t[:, 12:16], sc_t[:, 12:16]), flush=c_v)
        P.op("v", lambda E: E.tensor_scalar(sc_t[:, 0:4], sc_t[:, 0:4], 1.0, None,
                                            op0=Alu.add), flush=c_v)
        P.op("v", lambda E: E.tensor_tensor(sc_t[:, 0:4], sc_t[:, 0:4], sc_t[:, 12:16],
                                            op=Alu.mult), inc=c_v)
        P.wait("s", c_v, c_v.n)
        P.op("s", lambda E: E.activation(sc_t[:, 12:16], sc_t[:, 0:4], Act.Ln, scale=1.0), flush=c_s)
        P.op("s", lambda E: E.mul(sc_t[:, 12:16], sc_t[:, 12:16], 0.5), inc=c_s)
        meta["a_ready_s"] = c_s.n
        for wi in range(3):
            for ci, chunk in enumerate(shard_in[wi]):
                P.wait("y", c_s, c_s.n)
                dma("y", f_qa[:], chunk, dS["shard"])
                P.wait("s", dS["shard"], dS["shard"].n)
                P.op("s", lambda E, wi=wi: E.activation(
                    f_sq[:], f_qa[:], Act.Tanh, scale=sc_t[:, 8 + wi:9 + wi]), inc=c_s)
                P.wait("v", c_s, c_s.n)
                P.wait("v", dS["tern"], dS["tern"].n)
                P.wait("v", dS["y8"], dS["y8"].n)
                P.op("v", lambda E: E.tensor_scalar(
                    i8a[:], f_sq[:], MAGIC, -MAGIC, op0=Alu.add, op1=Alu.add), inc=c_v)
                P.op("v", lambda E: E.tensor_copy(istg[:], i8a[:]), inc=c_v)
                P.wait("g", c_v, c_v.n)
                P.op("g", lambda E, dst=tern_out[wi][ci]: E.dma_start(out=dst, in_=istg[:]),
                     inc=dS["tern"], inck=16)
        meta["tern_done"] = dS["tern"].n

    dma("y", dmk[:], d_dmaskT, dS["const"])
    dmask_done = dS["const"].n

    v_s = f_vec[:, 0:1]
    v_ss = f_vec[:, 1:2]
    v_m = f_vec[:, 2:3]
    v_var = f_vec[:, 3:4]
    v_vp = f_vec[:, 4:5]
    v_r = f_vec[:, 5:6]
    v_t0 = f_vec[:, 6:7]
    v_t1 = f_vec[:, 7:8]
    v_t2 = f_vec[:, 8:9]
    v_rr = f_vec[:, 9:10]
    v_mx = f_vec[:, 10:11]
    v_mn = f_vec[:, 11:12]
    v_m1 = f_vec[:, 12:13]
    v_d1 = f_vec[:, 13:14]
    v_c1 = f_vec[:, 14:15]
    invs1 = f_vec[:, 16:24]
    invs2 = f_vec[:, 24:32]
    cu_c = f_vec[:, 32:40]
    cv_c = f_vec[:, 40:48]
    cy3_c = f_vec[:, 16:24]
    c3_c = f_vec[:, 48:56]

    def newton_rsqrt(dst, src_plus):
        P.op("v", lambda E: E.reciprocal(v_t0[:], src_plus), inc=c_v)
        P.wait("s", c_v, c_v.n)
        P.op("s", lambda E: E.activation(dst, v_t0[:], Act.Sqrt), inc=c_s)
        P.wait("v", c_s, c_s.n)
        P.op("v", lambda E: E.tensor_tensor(v_t1[:], dst, dst, op=Alu.mult), flush=c_v)
        P.op("v", lambda E: E.tensor_tensor(v_t1[:], v_t1[:], src_plus, op=Alu.mult), flush=c_v)
        P.op("v", lambda E: E.tensor_scalar(v_t1[:], v_t1[:], -0.5, 1.5,
                                            op0=Alu.mult, op1=Alu.add), flush=c_v)
        P.op("v", lambda E: E.tensor_tensor(dst, dst, v_t1[:], op=Alu.mult), flush=c_v)

    xnT = [xnT0, xnT1]

    def phase1_tile(i):
        b = i // 4
        P.wait("y", c_s, c_s.n)
        P.wait("y", c_v, c_v.n)
        dma("y", f_x[:], d_x[i * CH:(i + 1) * CH, :], dS["x"])
        P.wait("s", dS["x"], dS["x"].n)
        P.wait("s", c_t, c_t.n)
        P.wait("s", c_v, c_v.n)
        P.op("s", lambda E: E.activation(f_xn[:], f_x[:],
                                         Act.Square, accum_out=v_ss[:]), inc=c_s)
        P.wait("v", dS["x"], dS["x"].n)
        P.op("v", lambda E: E.reduce_sum(v_s[:], f_x[:], axis=AXX), inc=c_v)
        P.wait("v", c_s, c_s.n)
        P.op("v", lambda E: E.tensor_scalar(v_m[:], v_s[:], 1.0 / D, None, op0=Alu.mult),
             flush=c_v)
        P.op("v", lambda E: E.tensor_tensor(v_t2[:], v_m[:], v_m[:], op=Alu.mult),
             flush=c_v)
        P.op("v", lambda E: E.scalar_tensor_tensor(v_var[:], v_ss[:], 1.0 / D, v_t2[:],
                                                   op0=Alu.mult, op1=Alu.subtract),
             flush=c_v)
        P.op("v", lambda E: E.tensor_scalar(v_vp[:], v_var[:], 1e-5, None, op0=Alu.add),
             inc=c_v)
        P.wait("v", c_v, c_v.n)
        newton_rsqrt(v_r[:], v_vp[:])
        P.op("v", lambda E: E.scalar_tensor_tensor(v_t2[:], v_m[:], -1.0, v_r[:],
                                                   op0=Alu.mult, op1=Alu.mult), inc=c_v)
        P.wait("s", c_v, c_v.n)
        P.op("s", lambda E: E.activation(f_xn[:], f_x[:], Act.Identity,
                                         scale=v_r[:], bias=v_t2[:]), inc=c_s)
        xn_done = c_s.n
        P.wait("v", c_s, xn_done)
        P.op("v", lambda E: E.reciprocal(v_t0[:], v_vp[:]), flush=c_v)
        P.op("v", lambda E: E.tensor_tensor(v_t1[:], v_var[:], v_t0[:], op=Alu.mult),
             flush=c_v)
        P.op("v", lambda E: E.tensor_scalar(v_t1[:], v_t1[:], 1e-8, None, op0=Alu.add),
             flush=c_v)
        P.op("v", lambda E: E.tensor_copy(v_t2[:], v_t1[:]), flush=c_v)
        newton_rsqrt(v_rr[:], v_t2[:])
        P.wait("v", c_s, xn_done)
        P.wait("v", dS["g3b"], g12_done)
        for br in range(2):
            gsl = g12[:, br * D:(br + 1) * D]
            ydram = d_y1 if br == 0 else d_y2
            invsl = (invs1 if br == 0 else invs2)[:, i:i + 1]
            P.op("v", lambda E, gsl=gsl: E.tensor_tensor(f_t1[:], f_xn[:], gsl,
                                                         op=Alu.mult))
            P.op("v", lambda E: E.reduce_max(v_mx[:], f_t1[:], axis=AXX), flush=c_v)
            P.op("v", lambda E: E.tensor_reduce(v_mn[:], f_t1[:], axis=AXX, op=Alu.min),
                 flush=c_v)
            P.op("v", lambda E: E.tensor_scalar(v_mn[:], v_mn[:], -1.0, None,
                                                op0=Alu.mult), flush=c_v)
            P.op("v", lambda E: E.tensor_tensor(v_m1[:], v_mx[:], v_mn[:], op=Alu.max),
                 flush=c_v)
            P.op("v", lambda E: E.tensor_tensor(v_d1[:], v_rr[:], v_m1[:], op=Alu.mult),
                 flush=c_v)
            P.op("v", lambda E: E.tensor_scalar_max(v_d1[:], v_d1[:], 1e-4), flush=c_v)
            P.op("v", lambda E: E.reciprocal(v_t0[:], v_d1[:]), flush=c_v)
            P.op("v", lambda E: E.scalar_tensor_tensor(v_c1[:], v_rr[:], 127.0, v_t0[:],
                                                       op0=Alu.mult, op1=Alu.mult),
                 flush=c_v)
            P.op("v", lambda E, invsl=invsl: E.tensor_scalar(invsl, v_d1[:], 1.0 / 127.0,
                                                             None, op0=Alu.mult), inc=c_v)
            P.wait("v", c_v, c_v.n)
            P.op("v", lambda E: E.tensor_scalar(f_t1[:], f_t1[:], v_c1[:], MAGIC,
                                                op0=Alu.mult, op1=Alu.add))
            P.wait("v", dS["y8"], dS["y8"].n)
            P.op("v", lambda E: E.tensor_scalar(i8a[:], f_t1[:], -MAGIC, None,
                                                op0=Alu.add), inc=c_v)
            P.wait("g", c_v, c_v.n)
            P.op("g", lambda E, d=ydram, i=i: E.dma_start(
                out=d[i * CH:(i + 1) * CH, :], in_=i8a[:]), inc=dS["y8"], inck=16)
        P.wait("t", c_s, xn_done)
        P.wait("t", dS["const"], const_done)
        for gq in range(4):
            pf = PF[4 + (gq % 2)]
            P.wait("t", c_s, c_s.n)
            P.wait("t", c_v, c_v.n)
            for j in range(4):
                kc = gq * 4 + j
                P.op("t", lambda E, pf=pf, j=j, kc=kc: E.transpose(
                    pf[:, j * CH:(j + 1) * CH], f_xn[:, kc * CH:(kc + 1) * CH],
                    identf_t[:]), inc=(c_t if j == 3 else None))
            P.wait("s", c_t, c_t.n)
            dst = xnT[b][:].rearrange("p (kc t) -> p kc t", kc=16)
            P.op("s", lambda E, pf=pf, gq=gq, b=b, i=i, dst=dst: E.activation(
                dst[:, gq * 4:gq * 4 + 4, (i % 4) * CH:(i % 4 + 1) * CH],
                pf[:].rearrange("p (j t) -> p j t", j=4), Act.Copy), inc=c_s)

    qkv_state = {"slot_mm": [0, 0], "bank_ev": [0, 0], "pan_ld": [0, 0]}

    def qkv_block(b, ocs):
        st = qkv_state
        P.wait("t", c_s, c_s.n)
        for oi, oc in enumerate(ocs):
            slot = oi % 2
            pan = wqp[:, slot * 2048:(slot + 1) * 2048]
            P.wait("g", c_t, st["slot_mm"][slot])
            psem = dS["qkvpan0" if slot == 0 else "qkvpan1"]
            src = d_wqkvT[:, oc * CH:(oc + 1) * CH].rearrange("(kc p) j -> p kc j", p=CH)
            P.op("g", lambda E, pan=pan, src=src: E.dma_start(
                out=pan.rearrange("p (kc j) -> p kc j", kc=16), in_=src),
                inc=psem, inck=16)
            pfq = PF[oi % 2]
            P.wait("t", psem, psem.n)
            P.wait("t", c_s, st["bank_ev"][oi % 2])
            for kc in range(16):
                P.op("t", lambda E, pfq=pfq, pan=pan, kc=kc, b=b: E.matmul(
                    pfq[:], pan[:, kc * CH:(kc + 1) * CH],
                    xnT[b][:, kc * 512:(kc + 1) * 512],
                    start=(kc == 0), stop=(kc == 15)),
                    inc=(c_t if kc == 15 else None))
            st["slot_mm"][slot] = c_t.n
            if oc < 16:
                h = oc
                dst = qfm[:, h * TC + b * 512: h * TC + b * 512 + 512]
            elif oc < 32:
                h = oc - 16
                dst = kvfm[:, h * 512: h * 512 + 512]
            else:
                h = oc - 32
                dst = kvfm[:, 8192 + h * 512: 8192 + h * 512 + 512]
            P.wait("s", c_t, c_t.n)
            P.op("s", lambda E, dst=dst, pfq=pfq, oc=oc: E.activation(
                dst, pfq[:], Act.Identity, bias=bq_t[:, oc:oc + 1]), inc=c_s)
            st["bank_ev"][oi % 2] = c_s.n

    KTS = smalls[:, 0:CH]
    VTM0 = smalls[:, CH:2 * CH]
    VTM1 = smalls[:, 2 * CH:3 * CH]
    SCT0 = smalls[:, 3 * CH:4 * CH]
    SCT1 = smalls[:, 4 * CH:5 * CH]
    SBF = smalls[:, 5 * CH:6 * CH]

    def p3a_block(b):
        for c_ in range(4):
            cg = b * 4 + c_
            for h in range(H):
                ksl = kvfm[:, h * 512 + c_ * CH: h * 512 + (c_ + 1) * CH]
                vsl = kvfm[:, 8192 + h * 512 + c_ * CH: 8192 + h * 512 + (c_ + 1) * CH]
                P.wait("t", c_s, c_s.n)
                P.op("t", lambda E, ksl=ksl: E.transpose(PBA[:, 0:CH], ksl, identb_t[:]),
                     inc=c_t)
                P.op("t", lambda E, vsl=vsl: E.transpose(PBB[:, 0:CH], vsl, identb_t[:]),
                     inc=c_t)
                P.wait("s", c_t, c_t.n)
                P.op("s", lambda E, h=h: E.activation(KTS[:], PBA[:, 0:CH], Act.Identity,
                                                      scale=dk_t[:, h:h + 1]), inc=c_s)
                P.op("s", lambda E: E.activation(VTM0[:], PBB[:, 0:CH], Act.Copy), inc=c_s)
                P.wait("t", c_s, c_s.n)
                P.wait("t", c_v, c_v.n)
                P.op("t", lambda E: E.matmul(PF[2][:, 0:CH], KTS[:], VTM0[:],
                                             start=True, stop=True), inc=c_t)
                P.wait("v", c_t, c_t.n)
                csl = cst[:, (cg * H + h) * CH: (cg * H + h + 1) * CH]
                P.op("v", lambda E, csl=csl: E.tensor_copy(csl, PF[2][:, 0:CH]))
                asl = f_qa[:, h * CH:(h + 1) * CH]
                if cg == 0:
                    P.op("v", lambda E, asl=asl: E.tensor_copy(asl, PF[2][:, 0:CH]),
                         inc=c_v)
                else:
                    P.op("v", lambda E, asl=asl, h=h: E.scalar_tensor_tensor(
                        asl, asl, lc_t[:, h:h + 1], PF[2][:, 0:CH],
                        op0=Alu.mult, op1=Alu.add), inc=c_v)

    def p3b_block(b):
        P.wait("t", dS["const"], dmask_done)
        P.wait("v", dS["const"], dmask_done)
        for c_ in range(4):
            cg = b * 4 + c_
            for h in range(H):
                ksl = kvfm[:, h * 512 + c_ * CH: h * 512 + (c_ + 1) * CH]
                vsl = kvfm[:, 8192 + h * 512 + c_ * CH: 8192 + h * 512 + (c_ + 1) * CH]
                qsl = qfm[:, h * TC + cg * CH: h * TC + (cg + 1) * CH]
                vtm = VTM0 if h % 2 == 0 else VTM1
                sct = SCT0 if h % 2 == 0 else SCT1
                P.wait("t", c_s, c_s.n)
                P.op("t", lambda E, vsl=vsl: E.transpose(PBB[:, 0:CH], vsl, identb_t[:]),
                     inc=c_t)
                P.wait("s", c_t, c_t.n)
                P.op("s", lambda E, vtm=vtm: E.activation(vtm, PBB[:, 0:CH], Act.Copy),
                     inc=c_s)
                P.wait("t", c_v, c_v.n)
                P.op("t", lambda E, ksl=ksl, qsl=qsl: E.matmul(
                    PF[3][:, 0:CH], ksl, qsl, start=True, stop=True), inc=c_t)
                P.wait("v", c_t, c_t.n)
                P.op("v", lambda E, h=h, sct=sct: E.tensor_tensor(
                    sct, PF[3][:, 0:CH], dmk[:, h * CH:(h + 1) * CH], op=Alu.mult),
                    inc=c_v)
                pfi = PF[4 + h % 2]
                P.wait("t", c_v, c_v.n)
                P.wait("t", c_s, c_s.n)
                P.op("t", lambda E, pfi=pfi, sct=sct, vtm=vtm: E.matmul(
                    pfi[:, 0:CH], sct, vtm, start=True, stop=True), inc=c_t)
                P.wait("v", c_t, c_t.n)
                P.wait("v", dS["intra"], dS["intra"].n)
                P.wait("v", dS["tern"], dS["tern"].n)
                P.op("v", lambda E, pfi=pfi, h=h: E.tensor_copy(
                    istg[:, h * CH:(h + 1) * CH], pfi[:, 0:CH]), inc=c_v)
            P.wait("g", c_v, c_v.n)
            P.op("g", lambda E, cg=cg: E.dma_start(
                out=d_intra.ap()[cg * CH:(cg + 1) * CH, :], in_=istg[:]),
                inc=dS["intra"], inck=16)

    for i in range(4):
        phase1_tile(i)
        if i < 3:
            absum_chunks(i)
    absum_finish()
    qkv_block(0, list(range(16, 48)))
    quant_pass2()
    p3a_block(0)
    qkv_block(0, list(range(0, 16)))
    p3b_block(0)
    for i in range(4, 8):
        phase1_tile(i)
    qkv_block(1, list(range(16, 48)))
    p3a_block(1)
    P.wait("g", c_v, c_v.n)
    P.op("g", lambda E: E.dma_start(out=d_Aloc.ap(), in_=f_qa[:]), inc=dS["aall"], inck=16)
    P.wait("g", dS["aall"], dS["aall"].n)
    P.op("g", lambda E: E.collective_compute(
        "AllGather", Alu.bypass, replica_groups=[list(range(NCORES))],
        ins=[d_Aloc.ap()], outs=[d_Aall.ap()]), inc=cc)
    aAG_done = cc.n
    P.wait("g", dS["tern"], meta["tern_done"])
    for din, dout in ((d_t1b, d_w1q), (d_t2b, d_w2q), (d_t3b, d_w3q)):
        P.op("g", lambda E, a=din, b=dout: E.collective_compute(
            "AllGather", Alu.bypass, replica_groups=[list(range(NCORES))],
            ins=[a.ap()], outs=[b.ap()]), inc=cc)
    meta["wq_ready"] = cc.n
    qkv_block(1, list(range(0, 16)))
    p3b_block(1)

    P.wait("v", c_s, c_s.n)
    P.op("v", lambda E: E.memset(f_sq[:], 0.0), inc=c_v)
    P.wait("y", cc, aAG_done)
    for j in range(7):
        P.wait("y", c_v, c_v.n)
        if j == 0:
            P.wait("y", dS["aall"], dS["aall"].n)
        dma("y", f_qa[:], d_Aall.ap()[j * CH:(j + 1) * CH, :], dS["aall"])
        P.wait("v", dS["aall"], dS["aall"].n)
        for h in range(H):
            P.op("v", lambda E, j=j, h=h: E.scalar_tensor_tensor(
                f_sq[:, h * CH:(h + 1) * CH], f_qa[:, h * CH:(h + 1) * CH],
                si_t[:, j * H + h: j * H + h + 1], f_sq[:, h * CH:(h + 1) * CH],
                op0=Alu.mult, op1=Alu.add), inc=(c_v if h == H - 1 else None))

    P.wait("g", c_v, c_v.n)
    P.op("g", lambda E: E.dma_start(out=dmk[:], in_=d_boutb), inc=dS["misc"], inck=16)
    boutb_done = dS["misc"].n

    slot_load = [None, None]

    def load_wout_panel(oc, slot):
        dstp = xnT[slot]
        psem = dS["woutp0" if slot == 0 else "woutp1"]
        src = d_woutT[:, oc * 512:(oc + 1) * 512].rearrange("(kc p) j -> p kc j", p=CH)
        P.op("g", lambda E, dstp=dstp, src=src: E.dma_start(
            out=dstp[:].rearrange("p (kc j) -> p kc j", kc=16), in_=src),
            inc=psem, inck=16)
        slot_load[slot] = (psem, psem.n)

    P.wait("g", c_t, c_t.n)
    load_wout_panel(0, 0)
    load_wout_panel(1, 1)

    for j in range(NT):
        P.wait("y", c_v, c_v.n)
        if j == 0:
            P.wait("y", dS["intra"], dS["intra"].n)
        dma("y", istg[:], d_intra.ap()[j * CH:(j + 1) * CH, :], dS["t3r0"])
        P.wait("v", dS["t3r0"], dS["t3r0"].n)
        P.wait("v", dS["outx"], dS["outx"].n)
        for h in range(H):
            P.wait("v", c_t, c_t.n)
            P.op("v", lambda E, h=h: E.tensor_copy(SBF[:], f_sq[:, h * CH:(h + 1) * CH]),
                 inc=c_v)
            pfi = PF[h % 2]
            P.wait("t", c_v, c_v.n)
            P.op("t", lambda E, pfi=pfi, h=h, j=j: E.matmul(
                pfi[:, 0:CH], qfm[:, h * TC + j * CH: h * TC + (j + 1) * CH], SBF[:],
                start=True, stop=True), inc=c_t)
            P.wait("v", c_t, c_t.n)
            P.op("v", lambda E, pfi=pfi, h=h: E.scalar_tensor_tensor(
                attnt[:, h * CH:(h + 1) * CH], pfi[:, 0:CH], dq_t[:, h:h + 1],
                istg[:, h * CH:(h + 1) * CH], op0=Alu.mult, op1=Alu.add), inc=c_v)
            P.op("v", lambda E, h=h, j=j: E.scalar_tensor_tensor(
                f_sq[:, h * CH:(h + 1) * CH], f_sq[:, h * CH:(h + 1) * CH],
                lc_t[:, h:h + 1], cst[:, (j * H + h) * CH:(j * H + h + 1) * CH],
                op0=Alu.mult, op1=Alu.add), inc=c_v)
        attn_done = c_v.n
        P.wait("t", c_v, attn_done)
        for gq in range(4):
            P.wait("t", c_s, c_s.n)
            for kk in range(4):
                kc = gq * 4 + kk
                P.op("t", lambda E, kc=kc, kk=kk: E.transpose(
                    PBA[:, kk * CH:(kk + 1) * CH], attnt[:, kc * CH:(kc + 1) * CH],
                    identb_t[:]), inc=(c_t if kk == 3 else None))
            P.wait("s", c_t, c_t.n)
            P.op("s", lambda E, gq=gq: E.activation(
                afm[:].rearrange("p (g t) -> p g t", g=16)[:, gq * 4:gq * 4 + 4, :],
                PBA[:, 0:512].rearrange("p (g t) -> p g t", g=4), Act.Copy), inc=c_s)
        afm_done = c_s.n
        P.wait("t", c_s, afm_done)
        for oc in range(4):
            pfo = PF[2 + oc % 2]
            P.wait("t", c_v, c_v.n)
            P.wait("t", slot_load[oc % 2][0], slot_load[oc % 2][1])
            wp_slot = xnT[oc % 2]
            for kc in range(16):
                P.op("t", lambda E, pfo=pfo, wp_slot=wp_slot, kc=kc: E.matmul(
                    pfo[:], afm[:, kc * CH:(kc + 1) * CH],
                    wp_slot[:, kc * 512:(kc + 1) * 512],
                    start=(kc == 0), stop=(kc == 15)), inc=(c_t if kc == 15 else None))
            if oc == 0:
                P.wait("y", c_v, c_v.n)
                dma("y", f_x[:], d_x[j * CH:(j + 1) * CH, :], dS["x"])
                P.wait("v", dS["x"], dS["x"].n)
                P.wait("v", dS["misc"], boutb_done)
            P.wait("v", c_t, c_t.n)
            P.wait("v", dS["outx"], dS["outx"].n)
            P.op("v", lambda E, pfo=pfo, oc=oc: E.scalar_tensor_tensor(
                f_t1[:, oc * 512:(oc + 1) * 512], pfo[:], 1.0,
                f_x[:, oc * 512:(oc + 1) * 512], op0=Alu.mult, op1=Alu.add))
            P.op("v", lambda E, pfo=pfo, oc=oc: E.tensor_tensor(
                attnt[:, oc * 512:(oc + 1) * 512], f_t1[:, oc * 512:(oc + 1) * 512],
                dmk[:, oc * 512:(oc + 1) * 512], op=Alu.add), inc=c_v)
            nxt = (j * 4 + oc + 2)
            if nxt < NT * 4:
                P.wait("g", c_t, c_t.n)
                load_wout_panel(nxt % 4, oc % 2)
        P.wait("g", c_v, c_v.n)
        P.op("g", lambda E, j=j: E.dma_start(
            out=d_outx.ap()[j * CH:(j + 1) * CH, :], in_=attnt[:]),
            inc=dS["outx"], inck=16)

    P.barrier(bar)
    g3t = kvfm[:, 8192:16384]
    P.op("g", lambda E: E.dma_start(out=g3t, in_=d_g3b), inc=dS["g3b"], inck=16)
    g3_done = dS["g3b"].n

    y1T = qfm
    y2T = cst
    for br in range(2):
        ydram = d_y1 if br == 0 else d_y2
        ytgt = y1T if br == 0 else y2T
        for i in range(NT):
            P.wait("y", c_v, c_v.n)
            P.wait("y", dS["y8"], dS["y8"].n)
            dma("y", i8a[:], ydram.ap()[i * CH:(i + 1) * CH, :], dS["y8r"])
            P.wait("v", dS["y8r"], dS["y8r"].n)
            P.wait("v", c_t, c_t.n)
            P.op("v", lambda E: E.tensor_copy(f_t1[:], i8a[:]), inc=c_v)
            P.wait("t", c_v, c_v.n)
            for gq in range(4):
                pf = PF[4 + gq % 2]
                P.wait("t", c_s, c_s.n)
                for kk in range(4):
                    kc = gq * 4 + kk
                    P.op("t", lambda E, pf=pf, kk=kk, kc=kc: E.transpose(
                        pf[:, kk * CH:(kk + 1) * CH], f_t1[:, kc * CH:(kc + 1) * CH],
                        identf_t[:]), inc=(c_t if kk == 3 else None))
                P.wait("s", c_t, c_t.n)
                dstr = ytgt[:].rearrange("p (kc t) -> p kc t", kc=16)
                P.op("s", lambda E, pf=pf, gq=gq, i=i, dstr=dstr: E.activation(
                    dstr[:, gq * 4:gq * 4 + 4, i * CH:(i + 1) * CH],
                    pf[:].rearrange("p (j t) -> p j t", j=4), Act.Copy), inc=c_s)

    P.wait("v", c_s, meta["a_ready_s"])
    P.op("v", lambda E: E.tensor_single_scalar(cu_c[:], invs1[:], sc_t[:, 12:13],
                                               op=Alu.mult), flush=c_v)
    P.op("v", lambda E: E.tensor_single_scalar(cv_c[:], invs2[:], sc_t[:, 13:14],
                                               op=Alu.mult), inc=c_v)

    P.op("v", lambda E: E.memset(f_st3[:], 0.0), inc=c_v)
    P.op("v", lambda E: E.memset(f_st3[:, 128:256], -3.0e38))
    P.op("v", lambda E: E.memset(f_st3[:, 256:384], 3.0e38), inc=c_v)

    pan_slots = [xnT0, xnT1, kvfm[:, 0:8192]]
    HCH = f_xn[:, 0:512]
    SILUS = [f_xn[:, 512:1024], f_xn[:, 1536:2048]]
    T3S0 = attnt[:, 0:512]
    T3S1 = attnt[:, 512:1024]
    P.wait("y", cc, meta["wq_ready"])
    P.wait("t", bar, bar.n)
    P.wait("s", bar, bar.n)
    slot_sem_t = [0, 0, 0]
    ubank_ev = [c_s.n, c_s.n]
    vbank_ev = [c_v.n, c_v.n]
    silu_free = [c_v.n, c_v.n]

    pan_i = 0
    for oc in range(16):
        for w_i, dram in ((0, d_w1q), (1, d_w2q)):
            slot = pan_i % 3
            pan_i += 1
            P.wait("y", c_t, slot_sem_t[slot])
            src = dram.ap()[:, oc * 512:(oc + 1) * 512].rearrange(
                "(kc p) j -> p kc j", p=CH)
            P.op("y", lambda E, slot=slot, src=src: E.dma_start(
                out=pan_slots[slot][:].rearrange("p (kc j) -> p kc j", kc=16), in_=src),
                inc=dS["ffnpan"], inck=16)
        slot_u = (pan_i - 2) % 3
        slot_v = (pan_i - 1) % 3
        P.wait("t", dS["ffnpan"], dS["ffnpan"].n)
        if oc == 0:
            P.wait("s", dS["g3b"], g3_done)
            P.wait("v", dS["g3b"], g3_done)
            P.wait("v", dS["outx"], dS["outx"].n)
        for t_ in range(NT):
            pu = PF[t_ % 2]
            pv = PF[2 + t_ % 2]
            P.wait("t", c_s, ubank_ev[t_ % 2])
            for kc in range(16):
                P.op("t", lambda E, pu=pu, kc=kc, t_=t_, slot_u=slot_u: E.matmul(
                    pu[:], y1T[:, kc * TC + t_ * CH: kc * TC + (t_ + 1) * CH],
                    pan_slots[slot_u][:, kc * 512:(kc + 1) * 512],
                    start=(kc == 0), stop=(kc == 15)), inc=(c_t if kc == 15 else None))
            u_mm = c_t.n
            P.wait("t", c_v, vbank_ev[t_ % 2])
            for kc in range(16):
                P.op("t", lambda E, pv=pv, kc=kc, t_=t_, slot_v=slot_v: E.matmul(
                    pv[:], y2T[:, kc * TC + t_ * CH: kc * TC + (t_ + 1) * CH],
                    pan_slots[slot_v][:, kc * 512:(kc + 1) * 512],
                    start=(kc == 0), stop=(kc == 15)), inc=(c_t if kc == 15 else None))
            v_mm = c_t.n
            SILU = SILUS[t_ % 2]
            P.wait("s", c_t, u_mm)
            P.wait("s", c_v, silu_free[t_ % 2])
            P.op("s", lambda E, pu=pu, t_=t_, SILU=SILU: E.activation(
                SILU, pu[:], Act.Silu, scale=cu_c[:, t_:t_ + 1]), inc=c_s)
            ubank_ev[t_ % 2] = c_s.n
            P.wait("v", c_t, v_mm)
            P.wait("v", c_s, c_s.n)
            P.op("v", lambda E, pv=pv, t_=t_, SILU=SILU: E.scalar_tensor_tensor(
                HCH, pv[:], cv_c[:, t_:t_ + 1], SILU, op0=Alu.mult, op1=Alu.mult),
                inc=c_v)
            vbank_ev[t_ % 2] = c_v.n
            silu_free[t_ % 2] = c_v.n
            P.wait("s", c_v, c_v.n)
            P.op("s", lambda E: E.activation(f_xn[:, 1024:1536], HCH, Act.Square,
                                             accum_out=v_t0[:]), inc=c_s)
            P.wait("v", c_s, c_s.n)
            col = f_st3[:, t_ * 16 + oc: t_ * 16 + oc + 1]
            P.op("v", lambda E, col=col: E.tensor_copy(col, v_t0[:]), flush=c_v)
            t3s = T3S0 if oc % 2 == 0 else T3S1
            P.wait("v", dS["t3w"], dS["t3w"].n)
            P.op("v", lambda E, t3s=t3s, oc=oc: E.tensor_tensor(
                t3s, HCH, g3t[:, oc * 512:(oc + 1) * 512], op=Alu.mult))
            mxc = f_st3[:, 128 + t_ * 16 + oc: 128 + t_ * 16 + oc + 1]
            mnc = f_st3[:, 256 + t_ * 16 + oc: 256 + t_ * 16 + oc + 1]
            P.op("v", lambda E, t3s=t3s, mxc=mxc: E.reduce_max(mxc, t3s, axis=AXX),
                 flush=c_v)
            P.op("v", lambda E, t3s=t3s, mnc=mnc: E.tensor_reduce(mnc, t3s, axis=AXX,
                                                                  op=Alu.min), inc=c_v)
            P.wait("g", c_v, c_v.n)
            P.op("g", lambda E, t3s=t3s, t_=t_, oc=oc: E.dma_start(
                out=d_t3.ap()[t_ * CH:(t_ + 1) * CH, oc * 512:(oc + 1) * 512], in_=t3s),
                inc=dS["t3w"], inck=16)
        slot_sem_t[slot_u] = c_t.n
        slot_sem_t[slot_v] = c_t.n

    P.wait("v", c_s, c_s.n)
    for t_ in range(NT):
        P.op("v", lambda E, t_=t_: E.reduce_sum(v_ss[:], f_st3[:, t_ * 16:t_ * 16 + 16],
                                                axis=AXX), flush=c_v)
        P.op("v", lambda E, t_=t_: E.reduce_max(v_mx[:],
                                                f_st3[:, 128 + t_ * 16:128 + t_ * 16 + 16],
                                                axis=AXX), flush=c_v)
        P.op("v", lambda E, t_=t_: E.tensor_reduce(
            v_mn[:], f_st3[:, 256 + t_ * 16:256 + t_ * 16 + 16], axis=AXX, op=Alu.min),
            flush=c_v)
        P.op("v", lambda E: E.tensor_scalar(v_mn[:], v_mn[:], -1.0, None, op0=Alu.mult),
             flush=c_v)
        P.op("v", lambda E: E.tensor_tensor(v_m1[:], v_mx[:], v_mn[:], op=Alu.max),
             flush=c_v)
        P.op("v", lambda E: E.tensor_scalar(v_t1[:], v_ss[:], 1.0 / HID, 1e-8,
                                            op0=Alu.mult, op1=Alu.add), flush=c_v)
        P.op("v", lambda E: E.tensor_copy(v_t2[:], v_t1[:]), flush=c_v)
        newton_rsqrt(v_rr[:], v_t2[:])
        P.op("v", lambda E: E.tensor_tensor(v_d1[:], v_rr[:], v_m1[:], op=Alu.mult),
             flush=c_v)
        P.op("v", lambda E: E.tensor_scalar_max(v_d1[:], v_d1[:], 1e-4), flush=c_v)
        P.op("v", lambda E: E.reciprocal(v_t0[:], v_d1[:]), flush=c_v)
        P.op("v", lambda E, t_=t_: E.scalar_tensor_tensor(
            c3_c[:, t_:t_ + 1], v_rr[:], 127.0, v_t0[:], op0=Alu.mult, op1=Alu.mult),
            flush=c_v)
        P.op("v", lambda E, t_=t_: E.tensor_scalar(cy3_c[:, t_:t_ + 1], v_d1[:],
                                                   1.0 / 127.0, None, op0=Alu.mult),
             inc=c_v)
    P.wait("v", c_v, c_v.n)
    P.op("v", lambda E: E.tensor_single_scalar(cy3_c[:], cy3_c[:], sc_t[:, 14:15],
                                               op=Alu.mult), inc=c_v)

    P.barrier(bar)
    y3T = cst
    w3p = [wqp[:, 0:2048], wqp[:, 2048:4096]]
    OXB = dmk

    afm_use = [0, 0, 0, 0]
    ft1_use = [0, 0, 0, 0]
    pfq_ev = [0, 0]
    t3r_ld = [0, 0, 0, 0]
    for g_ in range(4):
        tiles = (2 * g_, 2 * g_ + 1)
        for ti, t_ in enumerate(tiles):
            for oc in range(16):
                it = (ti * 16 + oc) % 4
                asl = afm[:, it * 512:(it + 1) * 512]
                fsl = f_t1[:, it * 512:(it + 1) * 512]
                P.wait("y", c_v, afm_use[it])
                psem = dS["t3r%d" % it]
                dma("y", asl,
                    d_t3.ap()[t_ * CH:(t_ + 1) * CH, oc * 512:(oc + 1) * 512],
                    psem)
                P.wait("v", psem, psem.n)
                P.wait("v", c_t, ft1_use[it])
                P.op("v", lambda E, t_=t_, asl=asl, fsl=fsl: E.tensor_scalar(
                    fsl, asl, c3_c[:, t_:t_ + 1], MAGIC,
                    op0=Alu.mult, op1=Alu.add), inc=c_v)
                afm_use[it] = c_v.n
                q_ready = c_v.n
                pf = PF[4 + oc % 2]
                P.wait("t", c_v, q_ready)
                P.wait("t", c_s, pfq_ev[oc % 2])
                for kk in range(4):
                    P.op("t", lambda E, pf=pf, kk=kk, fsl=fsl: E.transpose(
                        pf[:, kk * CH:(kk + 1) * CH], fsl[:, kk * CH:(kk + 1) * CH],
                        identf_t[:]), inc=(c_t if kk == 3 else None))
                ft1_use[it] = c_t.n
                P.wait("s", c_t, ft1_use[it])
                P.op("s", lambda E, pf=pf: E.activation(
                    i16a[:], pf[:], Act.Copy, bias=-MAGIC))
                dstr = y3T[:].rearrange("p (kc t) -> p kc t", kc=64)
                P.op("s", lambda E, oc=oc, ti=ti, dstr=dstr: E.activation(
                    dstr[:, oc * 4:oc * 4 + 4, ti * CH:(ti + 1) * CH],
                    i16a[:].rearrange("p (j t) -> p j t", j=4), Act.Copy), inc=c_s)
                pfq_ev[oc % 2] = c_s.n
        y3r = y3T[:].rearrange("p (kc t) -> p kc t", kc=64)
        w3_slot_mm = [0, 0]
        w3_slot_ld = [0, 0]
        for sweep, ocs in ((0, (0, 1, 2)), (1, (3,))):
            for kc in range(64):
                slot = kc % 2
                P.wait("y", c_t, w3_slot_mm[slot])
                psem = dS["w3p0" if slot == 0 else "w3p1"]
                if sweep == 0:
                    P.wait("g", c_t, w3_slot_mm[slot])
                    src_a = d_w3q.ap()[kc * CH:(kc + 1) * CH, 0:768]
                    src_b = d_w3q.ap()[kc * CH:(kc + 1) * CH, 768:1536]
                    P.op("y", lambda E, slot=slot, src_a=src_a: E.dma_start(
                        out=w3p[slot][:, 0:768], in_=src_a), inc=psem, inck=16)
                    P.op("g", lambda E, slot=slot, src_b=src_b: E.dma_start(
                        out=w3p[slot][:, 768:1536], in_=src_b), inc=psem, inck=16)
                else:
                    src = d_w3q.ap()[kc * CH:(kc + 1) * CH, 1536:2048]
                    P.op("y", lambda E, slot=slot, src=src: E.dma_start(
                        out=w3p[slot][:, 0:512], in_=src), inc=psem, inck=16)
                P.wait("t", psem, psem.n)
                if kc == 0:
                    P.wait("t", c_v, c_v.n)
                    P.wait("t", c_s, c_s.n)
                nmm = 2 * len(ocs)
                mi = 0
                for ti in range(2):
                    for oi, oc in enumerate(ocs):
                        mi += 1
                        pf = PF[ti * 3 + oi] if sweep == 0 else PF[ti]
                        off = oi * 512
                        P.op("t", lambda E, pf=pf, slot=slot, off=off, kc=kc, ti=ti:
                             E.matmul(pf[:], y3r[:, kc, ti * CH:(ti + 1) * CH],
                                      w3p[slot][:, off:off + 512],
                                      start=(kc == 0), stop=(kc == 63)),
                             inc=(c_t if mi == nmm else None))
                w3_slot_mm[slot] = c_t.n
            mm_done = c_t.n
            for ti, t_ in enumerate(tiles):
                P.wait("y", c_v, c_v.n)
                if g_ == 0 and sweep == 0 and ti == 0:
                    P.wait("y", dS["misc"], dS["misc"].n)
                dma("y", OXB[:], d_outx.ap()[t_ * CH:(t_ + 1) * CH, :], dS["outxr"])
                P.wait("v", c_t, mm_done)
                P.wait("v", dS["outxr"], dS["outxr"].n)
                P.wait("v", dS["out"], dS["out"].n)
                for oi, oc in enumerate(ocs):
                    pf = PF[ti * 3 + oi] if sweep == 0 else PF[ti]
                    P.op("v", lambda E, pf=pf, t_=t_, oc=oc: E.scalar_tensor_tensor(
                        f_sq[:, oc * 512:(oc + 1) * 512], pf[:], cy3_c[:, t_:t_ + 1],
                        OXB[:, oc * 512:(oc + 1) * 512], op0=Alu.mult, op1=Alu.add),
                        inc=c_v)
                P.wait("y", c_v, c_v.n)
                for oi, oc in enumerate(ocs):
                    P.op("y", lambda E, t_=t_, oc=oc: E.dma_start(
                        out=d_out[t_ * CH:(t_ + 1) * CH, oc * 512:(oc + 1) * 512],
                        in_=f_sq[:, oc * 512:(oc + 1) * 512]), inc=dS["out"], inck=16)

    P.wait("y", dS["out"], dS["out"].n)
    P.wait("g", dS["out"], dS["out"].n)

    with nc.Block() as block:
        @block.tensor
        def _(E):
            for f in P.q["t"]:
                f(E)

        @block.vector
        def _(E):
            for f in P.q["v"]:
                f(E)

        @block.scalar
        def _(E):
            for f in P.q["s"]:
                f(E)

        @block.gpsimd
        def _(E):
            for f in P.q["g"]:
                f(E)

        @block.sync
        def _(E):
            for f in P.q["y"]:
                f(E)

    return nc


_BUILT = {}


def kernel(L0_attn_x, L0_attn_w_qkv, L0_attn_b_qkv, L0_attn_w_out, L0_attn_b_out,
           L0_ffn_w1, L0_ffn_g1, L0_ffn_w2, L0_ffn_g2, L0_ffn_w3, L0_ffn_g3):
    const = host_constants()
    if "nc" not in _BUILT:
        _BUILT["nc"] = build_nc(const)
    nc = _BUILT["nc"]

    x = np.ascontiguousarray(np.asarray(L0_attn_x, np.float32).reshape(8192, D))
    wqkvT = np.ascontiguousarray(np.asarray(L0_attn_w_qkv, np.float32).T)
    bqkv = np.asarray(L0_attn_b_qkv, np.float32)
    bqkv_fm = np.ascontiguousarray(bqkv.reshape(48, CH).T)
    woutT = np.ascontiguousarray(np.asarray(L0_attn_w_out, np.float32).T)
    boutb = np.ascontiguousarray(
        np.broadcast_to(np.asarray(L0_attn_b_out, np.float32)[None, :], (CH, D)))
    w1T = np.ascontiguousarray(np.asarray(L0_ffn_w1, np.float32).T)
    w2T = np.ascontiguousarray(np.asarray(L0_ffn_w2, np.float32).T)
    w3T = np.ascontiguousarray(np.asarray(L0_ffn_w3, np.float32).T)
    g12b = np.ascontiguousarray(np.broadcast_to(
        np.concatenate([np.asarray(L0_ffn_g1, np.float32),
                        np.asarray(L0_ffn_g2, np.float32)])[None, :], (CH, 2 * D)))
    g3b = np.ascontiguousarray(
        np.broadcast_to(np.asarray(L0_ffn_g3, np.float32)[None, :], (CH, HID)))

    in_maps = []
    for c in range(NCORES):
        m = {
            "x_slice": np.ascontiguousarray(x[c * TC:(c + 1) * TC]),
            "wqkvT": wqkvT,
            "bqkv_fm": bqkv_fm,
            "woutT": woutT,
            "bout_b": boutb,
            "w1T_shard": np.ascontiguousarray(w1T[c * 256:(c + 1) * 256]),
            "w2T_shard": np.ascontiguousarray(w2T[c * 256:(c + 1) * 256]),
            "w3T_shard": np.ascontiguousarray(w3T[c * 1024:(c + 1) * 1024]),
            "g12_b": g12b,
            "g3_b": g3b,
            "dmaskT": const["dmaskT"],
            "decayqcol": const["decayqcol"],
            "decaykb": const["decaykb"],
            "lamcb": const["lamcb"],
            "identf": const["identf"],
            "identb": const["identb"],
            "sinit_w": const["sinit"][c],
            "wnumel_inv": const["wnumel_inv"],
        }
        in_maps.append(m)

    res = run_bass_kernel_spmd(nc, in_maps, core_ids=list(range(NCORES)))
    out = np.concatenate([res.results[c]["out_slice"] for c in range(NCORES)], axis=0)
    return out.reshape(1, 8192, D)


# version 09
# speedup vs baseline: 1.5994x; 1.0449x over previous
import math
import numpy as np
import ml_dtypes

import concourse.bass as bass
import concourse.mybir as mybir
from concourse.bass_utils import run_bass_kernel_spmd

F32 = mybir.dt.float32
BF16 = mybir.dt.bfloat16
I16 = mybir.dt.int16
I8 = mybir.dt.int8
Act = mybir.ActivationFunctionType
Alu = mybir.AluOpType
AXX = mybir.AxisListType.X

MAGIC = 12582912.0
NCORES = 8
TC = 1024
D = 2048
HID = 8192
H = 16
CH = 128
NT = 8
NB = 2
LN3 = 1.0986122886681098


def _slopes(n):
    start = 2.0 ** (-2.0 ** (-(math.log2(n) - 3)))
    return np.array([start ** (i + 1) for i in range(n)], dtype=np.float64)


def host_constants():
    lam = np.exp(-_slopes(H))
    pos = np.arange(CH, dtype=np.float64)
    decay_q = lam[:, None] ** (pos + 1.0)
    decay_k = lam[:, None] ** (CH - 1.0 - pos)
    diff = pos[:, None] - pos[None, :]
    dmask = np.where(diff >= 0, lam[:, None, None] ** np.maximum(diff, 0), 0.0)
    lam_c = lam ** CH
    lam_sl = lam ** TC
    c = {}
    dmT = np.zeros((CH, H * CH), np.float64)
    for h in range(H):
        dmT[:, h * CH:(h + 1) * CH] = dmask[h].T
    c["dmaskT"] = dmT.astype(ml_dtypes.bfloat16)
    c["decayqcol"] = np.broadcast_to(decay_q.T, (CH, H)).astype(np.float32).copy()
    c["decaykb"] = np.broadcast_to(decay_k.T, (CH, H)).astype(np.float32).copy()
    c["lamcb"] = np.tile(lam_c.astype(np.float32)[None, :], (CH, 1))
    c["identf"] = np.eye(CH, dtype=np.float32)
    c["identb"] = np.eye(CH, dtype=np.float32).astype(ml_dtypes.bfloat16)
    c["ones_col"] = np.ones((CH, 1), np.float32)
    sinit = []
    for core in range(NCORES):
        w = np.zeros((7, H), np.float64)
        for j in range(core):
            w[j] = lam_sl ** (core - 1 - j)
        sinit.append(np.tile(w.reshape(1, 7 * H), (CH, 1)).astype(np.float32))
    c["sinit"] = sinit
    c["wnumel_inv"] = np.array(
        [[1.0 / (HID * D), 1.0 / (HID * D), 1.0 / (D * HID), 0.0]], np.float32)
    return c


class Sem:
    def __init__(self, handle, name):
        self.h = handle
        self.n = 0
        self.name = name


class Prog:

    ENGS = ("t", "v", "s", "g", "y")

    def __init__(self):
        self.q = {e: [] for e in self.ENGS}

    def op(self, eng, fn, inc=None, inck=1, flush=None):
        if flush is not None:
            assert inc is None
            flush.n += 1
            v = flush.n
            self.q[eng].append(lambda E: fn(E).then_inc(flush.h, 1))
            self.q[eng].append(lambda E: E.wait_ge(flush.h, v))
        elif inc is not None:
            inc.n += inck
            self.q[eng].append(lambda E: fn(E).then_inc(inc.h, inck))
        else:
            self.q[eng].append(lambda E: fn(E))

    def wait(self, eng, sem, val=None):
        v = sem.n if val is None else val
        self.q[eng].append(lambda E: E.wait_ge(sem.h, v))

    def barrier(self, sem, engs=ENGS):
        for e in engs:
            self.q[e].append(lambda E: E.nop().then_inc(sem.h, 1))
        sem.n += len(engs)
        tgt = sem.n
        for e in engs:
            self.q[e].append(lambda E: E.wait_ge(sem.h, tgt))


def build_nc(const):
    nc = bass.Bass("TRN2", target_bir_lowering=False, debug=False)

    d_x = nc.dram_tensor("x_slice", [TC, D], F32, kind="ExternalInput").ap()
    d_wqkvT = nc.dram_tensor("wqkvT", [D, 3 * D], F32, kind="ExternalInput").ap()
    d_bqkv_fm = nc.dram_tensor("bqkv_fm", [CH, 48], F32, kind="ExternalInput").ap()
    d_woutT = nc.dram_tensor("woutT", [D, D], F32, kind="ExternalInput").ap()
    d_boutb = nc.dram_tensor("bout_b", [CH, D], F32, kind="ExternalInput").ap()
    d_w1sh = nc.dram_tensor("w1T_shard", [D // 8, HID], F32, kind="ExternalInput").ap()
    d_w2sh = nc.dram_tensor("w2T_shard", [D // 8, HID], F32, kind="ExternalInput").ap()
    d_w3sh = nc.dram_tensor("w3T_shard", [HID // 8, D], F32, kind="ExternalInput").ap()
    d_g12b = nc.dram_tensor("g12_b", [CH, 2 * D], F32, kind="ExternalInput").ap()
    d_g3b = nc.dram_tensor("g3_b", [CH, HID], F32, kind="ExternalInput").ap()
    d_dmaskT = nc.dram_tensor("dmaskT", [CH, H * CH], BF16, kind="ExternalInput").ap()
    d_dqcol = nc.dram_tensor("decayqcol", [CH, H], F32, kind="ExternalInput").ap()
    d_dkcol = nc.dram_tensor("decaykb", [CH, H], F32, kind="ExternalInput").ap()
    d_lamcb = nc.dram_tensor("lamcb", [CH, H], F32, kind="ExternalInput").ap()
    d_identf = nc.dram_tensor("identf", [CH, CH], F32, kind="ExternalInput").ap()
    d_identb = nc.dram_tensor("identb", [CH, CH], BF16, kind="ExternalInput").ap()
    d_sinit = nc.dram_tensor("sinit_w", [CH, 7 * H], F32, kind="ExternalInput").ap()
    d_wninv = nc.dram_tensor("wnumel_inv", [1, 4], F32, kind="ExternalInput").ap()

    d_out = nc.dram_tensor("out_slice", [TC, D], F32, kind="ExternalOutput").ap()
    d_arin = nc.dram_tensor("ar_in", [1, 4], F32)
    d_arout = nc.dram_tensor("ar_out", [1, 4], F32)
    d_t1b = nc.dram_tensor("tern1_sh", [D // 8, HID], BF16)
    d_t2b = nc.dram_tensor("tern2_sh", [D // 8, HID], BF16)
    d_t3b = nc.dram_tensor("tern3_sh", [HID // 8, D], BF16)
    d_w1q = nc.dram_tensor("w1q", [D, HID], BF16)
    d_w2q = nc.dram_tensor("w2q", [D, HID], BF16)
    d_w3q = nc.dram_tensor("w3q", [HID, D], BF16)
    d_Aloc = nc.dram_tensor("A_loc", [CH, H * CH], F32)
    d_Aall = nc.dram_tensor("A_all", [NCORES * CH, H * CH], F32)
    d_y1 = nc.dram_tensor("y1_i8", [TC, D], I8)
    d_y2 = nc.dram_tensor("y2_i8", [TC, D], I8)
    d_intra = nc.dram_tensor("intra_d", [TC, D], BF16)
    d_outx = nc.dram_tensor("outx_d", [TC, D], BF16)
    d_t3 = nc.dram_tensor("t3_d", [TC, HID], BF16)

    A = nc.alloc_sbuf_tensor
    sb = {}

    def S(name, shape, dt):
        sb[name] = A(name, shape, dt)
        return sb[name]

    xnT0 = S("xnT0", [CH, 16 * 512], BF16)
    xnT1 = S("xnT1", [CH, 16 * 512], BF16)
    kvfm = S("kvfm", [CH, 2 * 8192], BF16)
    qfm = S("qfm", [CH, H * TC], BF16)
    cst = S("cst", [CH, 16384], BF16)
    istg = S("istg", [CH, D], BF16)
    wqp = S("wqp", [CH, 2 * 2048], BF16)
    dmk = S("dmk", [CH, H * CH], BF16)
    g12 = S("g12", [CH, 2 * D], BF16)
    attnt = S("attnt", [CH, D], BF16)
    afm = S("afm", [CH, D], BF16)
    smalls = S("smalls", [CH, 6 * CH], BF16)
    identb_t = S("identb_t", [CH, CH], BF16)
    f_sq = S("f_sq", [CH, D], F32)
    f_x = S("f_x", [CH, D], F32)
    f_xn = S("f_xn", [CH, D], F32)
    f_t1 = S("f_t1", [CH, D], F32)
    f_qa = S("f_qa", [CH, D], F32)
    f_vec = S("f_vec", [CH, 56], F32)
    f_st3 = S("f_st3", [CH, 3 * 128], F32)
    identf_t = S("identf_t", [CH, CH], F32)
    bq_t = S("bq_t", [CH, 48], F32)
    dq_t = S("dq_t", [CH, H], F32)
    dk_t = S("dk_t", [CH, H], F32)
    lc_t = S("lc_t", [CH, H], F32)
    si_t = S("si_t", [CH, 7 * H], F32)
    sc_t = S("sc_t", [CH, 16], F32)
    i8a = S("i8a", [CH, D], I8)
    i16a = S("i16a", [CH, 512], I16)

    rem = nc.sbuf_bytes_remaining() if callable(getattr(nc, "sbuf_bytes_remaining", None)) else None

    PF = [nc.alloc_psum_tensor if False else None]
    psum_guards = []
    PF = []
    for i in range(6):
        g = nc.psum_tensor(f"pf{i}", [CH, 512], F32)
        t = g.__enter__()
        psum_guards.append(g)
        PF.append(t)
    PBA_g = nc.psum_tensor("pba", [CH, 1024], BF16)
    PBA = PBA_g.__enter__()
    psum_guards.append(PBA_g)
    PBB_g = nc.psum_tensor("pbb", [CH, 1024], BF16)
    PBB = PBB_g.__enter__()
    psum_guards.append(PBB_g)

    sem_guards = []
    sems = {}

    def mksem(name):
        g = nc.semaphore(name)
        h = g.__enter__()
        sem_guards.append(g)
        sems[name] = Sem(h, name)
        return sems[name]

    dS = {n: mksem("d_" + n) for n in
          ("const", "x", "shard", "tern", "ar", "y8", "intra",
           "aall", "outx", "ffnpan", "t3w", "y8r",
           "outxr", "out", "g3b", "misc",
           "qkvpan0", "qkvpan1", "woutp0", "woutp1", "w3p0", "w3p1",
           "t3r0", "t3r1", "t3r2", "t3r3")}
    c_t = mksem("c_t")
    c_v = mksem("c_v")
    c_s = mksem("c_s")
    c_g = mksem("c_g")
    cc = mksem("cc")
    bar = mksem("bar")

    P = Prog()

    def dma(eng, dst, src, sem):
        P.op(eng, lambda E, d=dst, s=src: E.dma_start(out=d, in_=s), inc=sem, inck=16)

    dma("y", identf_t[:], d_identf, dS["const"])
    dma("y", identb_t[:], d_identb, dS["const"])
    dma("y", bq_t[:], d_bqkv_fm, dS["const"])
    dma("y", dq_t[:], d_dqcol, dS["const"])
    dma("y", dk_t[:], d_dkcol, dS["const"])
    dma("y", lc_t[:], d_lamcb, dS["const"])
    dma("y", si_t[:], d_sinit, dS["const"])
    dma("g", g12[:], d_g12b, dS["g3b"])
    const_done = dS["const"].n
    g12_done = dS["g3b"].n

    def shard_chunks(dram, wi):
        out = []
        if wi < 2:
            for pc in range(2):
                for cc_ in range(4):
                    out.append(dram[pc * CH:(pc + 1) * CH, cc_ * D:(cc_ + 1) * D])
        else:
            for pc in range(8):
                out.append(dram[pc * CH:(pc + 1) * CH, :])
        return out

    def tern_chunks(dram, wi):
        return shard_chunks(dram, wi)

    shard_in = [shard_chunks(d_w1sh, 0), shard_chunks(d_w2sh, 1), shard_chunks(d_w3sh, 2)]
    tern_out = [tern_chunks(d_t1b, 0), tern_chunks(d_t2b, 1), tern_chunks(d_t3b, 2)]

    ab_cols = f_st3[:, 0:32]
    meta = {}

    def absum_chunks(wi):
        for ci, chunk in enumerate(shard_in[wi]):
            dma("y", f_qa[:], chunk, dS["shard"])
            P.wait("s", dS["shard"], dS["shard"].n)
            P.op("s", lambda E, ci=ci, wi=wi: E.activation(
                f_sq[:], f_qa[:], Act.Abs,
                accum_out=ab_cols[:, wi * 8 + ci: wi * 8 + ci + 1]), inc=c_s)
            P.wait("y", c_s, c_s.n)

    def absum_finish():
        P.wait("v", c_s, c_s.n)
        for wi in range(3):
            P.op("v", lambda E, wi=wi: E.reduce_sum(
                f_st3[:, 32 + wi:32 + wi + 1], ab_cols[:, wi * 8:wi * 8 + 8], axis=AXX),
                inc=c_v)
        P.op("v", lambda E: E.memset(sc_t[:, 15:16], 1.0), inc=c_v)
        P.wait("t", c_v, c_v.n)
        P.wait("t", dS["const"], const_done)
        for wi in range(3):
            P.op("t", lambda E, wi=wi: E.matmul(
                PF[0][0:1, wi:wi + 1], f_st3[:, 32 + wi:32 + wi + 1], sc_t[:, 15:16],
                start=True, stop=True), inc=c_t)
        P.wait("v", c_t, c_t.n)
        P.op("v", lambda E: E.memset(sc_t[0:1, 0:4], 0.0), flush=c_v)
        P.op("v", lambda E: E.tensor_copy(sc_t[0:1, 0:3], PF[0][0:1, 0:3]), inc=c_v)
        P.wait("y", c_v, c_v.n)
        dma("y", d_arin.ap(), sc_t[0:1, 0:4], dS["ar"])
        P.wait("g", dS["ar"], dS["ar"].n)
        P.op("g", lambda E: E.collective_compute(
            "AllReduce", Alu.add, replica_groups=[list(range(NCORES))],
            ins=[d_arin.ap()], outs=[d_arout.ap()]), inc=cc)
        P.wait("t", c_v, c_v.n)

    def quant_pass2():
        P.wait("g", cc, cc.n)
        P.op("g", lambda E: E.dma_start(out=sc_t[:, 0:4],
                                        in_=d_arout.ap().partition_broadcast(CH)),
             inc=dS["ar"], inck=16)
        P.op("g", lambda E: E.dma_start(out=sc_t[:, 4:8],
                                        in_=d_wninv.partition_broadcast(CH)),
             inc=dS["ar"], inck=16)
        P.wait("v", dS["ar"], dS["ar"].n)
        P.op("v", lambda E: E.tensor_tensor(sc_t[:, 0:4], sc_t[:, 0:4], sc_t[:, 4:8], op=Alu.mult), flush=c_v)
        P.op("v", lambda E: E.tensor_scalar(sc_t[:, 8:12], sc_t[:, 0:4], 1e-8, None, op0=Alu.add), flush=c_v)
        P.op("v", lambda E: E.reciprocal(sc_t[:, 8:12], sc_t[:, 8:12]), flush=c_v)
        P.op("v", lambda E: E.tensor_scalar(sc_t[:, 12:16], sc_t[:, 0:4], -1.0, 1.0,
                                            op0=Alu.mult, op1=Alu.add), flush=c_v)
        P.op("v", lambda E: E.reciprocal(sc_t[:, 12:16], sc_t[:, 12:16]), flush=c_v)
        P.op("v", lambda E: E.tensor_scalar(sc_t[:, 0:4], sc_t[:, 0:4], 1.0, None,
                                            op0=Alu.add), flush=c_v)
        P.op("v", lambda E: E.tensor_tensor(sc_t[:, 0:4], sc_t[:, 0:4], sc_t[:, 12:16],
                                            op=Alu.mult), inc=c_v)
        P.wait("s", c_v, c_v.n)
        P.op("s", lambda E: E.activation(sc_t[:, 12:16], sc_t[:, 0:4], Act.Ln, scale=1.0), flush=c_s)
        P.op("s", lambda E: E.mul(sc_t[:, 12:16], sc_t[:, 12:16], 0.5), inc=c_s)
        meta["a_ready_s"] = c_s.n
        for wi in range(3):
            for ci, chunk in enumerate(shard_in[wi]):
                P.wait("y", c_s, c_s.n)
                dma("y", f_qa[:], chunk, dS["shard"])
                P.wait("s", dS["shard"], dS["shard"].n)
                P.op("s", lambda E, wi=wi: E.activation(
                    f_sq[:], f_qa[:], Act.Tanh, scale=sc_t[:, 8 + wi:9 + wi]), inc=c_s)
                P.wait("v", c_s, c_s.n)
                P.wait("v", dS["tern"], dS["tern"].n)
                P.wait("v", dS["y8"], dS["y8"].n)
                P.op("v", lambda E: E.tensor_scalar(
                    i8a[:], f_sq[:], MAGIC, -MAGIC, op0=Alu.add, op1=Alu.add), inc=c_v)
                P.op("v", lambda E: E.tensor_copy(istg[:], i8a[:]), inc=c_v)
                P.wait("g", c_v, c_v.n)
                P.op("g", lambda E, dst=tern_out[wi][ci]: E.dma_start(out=dst, in_=istg[:]),
                     inc=dS["tern"], inck=16)
        meta["tern_done"] = dS["tern"].n

    dma("y", dmk[:], d_dmaskT, dS["const"])
    dmask_done = dS["const"].n

    v_s = f_vec[:, 0:1]
    v_ss = f_vec[:, 1:2]
    v_m = f_vec[:, 2:3]
    v_var = f_vec[:, 3:4]
    v_vp = f_vec[:, 4:5]
    v_r = f_vec[:, 5:6]
    v_t0 = f_vec[:, 6:7]
    v_t1 = f_vec[:, 7:8]
    v_t2 = f_vec[:, 8:9]
    v_rr = f_vec[:, 9:10]
    v_mx = f_vec[:, 10:11]
    v_mn = f_vec[:, 11:12]
    v_m1 = f_vec[:, 12:13]
    v_d1 = f_vec[:, 13:14]
    v_c1 = f_vec[:, 14:15]
    invs1 = f_vec[:, 16:24]
    invs2 = f_vec[:, 24:32]
    cu_c = f_vec[:, 32:40]
    cv_c = f_vec[:, 40:48]
    cy3_c = f_vec[:, 16:24]
    c3_c = f_vec[:, 48:56]

    def newton_rsqrt(dst, src_plus):
        P.op("v", lambda E: E.reciprocal(v_t0[:], src_plus), inc=c_v)
        P.wait("s", c_v, c_v.n)
        P.op("s", lambda E: E.activation(dst, v_t0[:], Act.Sqrt), inc=c_s)
        P.wait("v", c_s, c_s.n)
        P.op("v", lambda E: E.tensor_tensor(v_t1[:], dst, dst, op=Alu.mult), flush=c_v)
        P.op("v", lambda E: E.tensor_tensor(v_t1[:], v_t1[:], src_plus, op=Alu.mult), flush=c_v)
        P.op("v", lambda E: E.tensor_scalar(v_t1[:], v_t1[:], -0.5, 1.5,
                                            op0=Alu.mult, op1=Alu.add), flush=c_v)
        P.op("v", lambda E: E.tensor_tensor(dst, dst, v_t1[:], op=Alu.mult), flush=c_v)

    xnT = [xnT0, xnT1]

    def phase1_tile(i):
        b = i // 4
        P.wait("y", c_s, c_s.n)
        P.wait("y", c_v, c_v.n)
        dma("y", f_x[:], d_x[i * CH:(i + 1) * CH, :], dS["x"])
        P.wait("s", dS["x"], dS["x"].n)
        P.wait("s", c_t, c_t.n)
        P.wait("s", c_v, c_v.n)
        P.op("s", lambda E: E.activation(f_xn[:], f_x[:],
                                         Act.Square, accum_out=v_ss[:]), inc=c_s)
        P.wait("v", dS["x"], dS["x"].n)
        P.op("v", lambda E: E.reduce_sum(v_s[:], f_x[:], axis=AXX), inc=c_v)
        P.wait("v", c_s, c_s.n)
        P.op("v", lambda E: E.tensor_scalar(v_m[:], v_s[:], 1.0 / D, None, op0=Alu.mult),
             flush=c_v)
        P.op("v", lambda E: E.tensor_tensor(v_t2[:], v_m[:], v_m[:], op=Alu.mult),
             flush=c_v)
        P.op("v", lambda E: E.scalar_tensor_tensor(v_var[:], v_ss[:], 1.0 / D, v_t2[:],
                                                   op0=Alu.mult, op1=Alu.subtract),
             flush=c_v)
        P.op("v", lambda E: E.tensor_scalar(v_vp[:], v_var[:], 1e-5, None, op0=Alu.add),
             inc=c_v)
        P.wait("v", c_v, c_v.n)
        newton_rsqrt(v_r[:], v_vp[:])
        P.op("v", lambda E: E.scalar_tensor_tensor(v_t2[:], v_m[:], -1.0, v_r[:],
                                                   op0=Alu.mult, op1=Alu.mult), inc=c_v)
        P.wait("s", c_v, c_v.n)
        P.op("s", lambda E: E.activation(f_xn[:], f_x[:], Act.Identity,
                                         scale=v_r[:], bias=v_t2[:]), inc=c_s)
        xn_done = c_s.n
        P.wait("v", c_s, xn_done)
        P.op("v", lambda E: E.reciprocal(v_t0[:], v_vp[:]), flush=c_v)
        P.op("v", lambda E: E.tensor_tensor(v_t1[:], v_var[:], v_t0[:], op=Alu.mult),
             flush=c_v)
        P.op("v", lambda E: E.tensor_scalar(v_t1[:], v_t1[:], 1e-8, None, op0=Alu.add),
             flush=c_v)
        P.op("v", lambda E: E.tensor_copy(v_t2[:], v_t1[:]), flush=c_v)
        newton_rsqrt(v_rr[:], v_t2[:])
        P.wait("v", c_s, xn_done)
        P.wait("v", dS["g3b"], g12_done)
        for br in range(2):
            gsl = g12[:, br * D:(br + 1) * D]
            ydram = d_y1 if br == 0 else d_y2
            invsl = (invs1 if br == 0 else invs2)[:, i:i + 1]
            P.op("v", lambda E, gsl=gsl: E.tensor_tensor(f_t1[:], f_xn[:], gsl,
                                                         op=Alu.mult))
            P.op("v", lambda E: E.reduce_max(v_mx[:], f_t1[:], axis=AXX), flush=c_v)
            P.op("v", lambda E: E.tensor_reduce(v_mn[:], f_t1[:], axis=AXX, op=Alu.min),
                 flush=c_v)
            P.op("v", lambda E: E.tensor_scalar(v_mn[:], v_mn[:], -1.0, None,
                                                op0=Alu.mult), flush=c_v)
            P.op("v", lambda E: E.tensor_tensor(v_m1[:], v_mx[:], v_mn[:], op=Alu.max),
                 flush=c_v)
            P.op("v", lambda E: E.tensor_tensor(v_d1[:], v_rr[:], v_m1[:], op=Alu.mult),
                 flush=c_v)
            P.op("v", lambda E: E.tensor_scalar_max(v_d1[:], v_d1[:], 1e-4), flush=c_v)
            P.op("v", lambda E: E.reciprocal(v_t0[:], v_d1[:]), flush=c_v)
            P.op("v", lambda E: E.scalar_tensor_tensor(v_c1[:], v_rr[:], 127.0, v_t0[:],
                                                       op0=Alu.mult, op1=Alu.mult),
                 flush=c_v)
            P.op("v", lambda E, invsl=invsl: E.tensor_scalar(invsl, v_d1[:], 1.0 / 127.0,
                                                             None, op0=Alu.mult), inc=c_v)
            P.wait("v", c_v, c_v.n)
            P.op("v", lambda E: E.tensor_scalar(f_t1[:], f_t1[:], v_c1[:], MAGIC,
                                                op0=Alu.mult, op1=Alu.add))
            P.wait("v", dS["y8"], dS["y8"].n)
            P.op("v", lambda E: E.tensor_scalar(i8a[:], f_t1[:], -MAGIC, None,
                                                op0=Alu.add), inc=c_v)
            P.wait("g", c_v, c_v.n)
            P.op("g", lambda E, d=ydram, i=i: E.dma_start(
                out=d[i * CH:(i + 1) * CH, :], in_=i8a[:]), inc=dS["y8"], inck=16)
        P.wait("t", c_s, xn_done)
        P.wait("t", dS["const"], const_done)
        for gq in range(4):
            pf = PF[4 + (gq % 2)]
            P.wait("t", c_s, c_s.n)
            P.wait("t", c_v, c_v.n)
            for j in range(4):
                kc = gq * 4 + j
                P.op("t", lambda E, pf=pf, j=j, kc=kc: E.transpose(
                    pf[:, j * CH:(j + 1) * CH], f_xn[:, kc * CH:(kc + 1) * CH],
                    identf_t[:]), inc=(c_t if j == 3 else None))
            P.wait("s", c_t, c_t.n)
            dst = xnT[b][:].rearrange("p (kc t) -> p kc t", kc=16)
            P.op("s", lambda E, pf=pf, gq=gq, b=b, i=i, dst=dst: E.activation(
                dst[:, gq * 4:gq * 4 + 4, (i % 4) * CH:(i % 4 + 1) * CH],
                pf[:].rearrange("p (j t) -> p j t", j=4), Act.Copy), inc=c_s)

    qkv_state = {"slot_mm": [0, 0], "bank_ev": [0, 0], "pan_ld": [0, 0]}

    def qkv_block(b, ocs):
        st = qkv_state
        P.wait("t", c_s, c_s.n)
        for oi, oc in enumerate(ocs):
            slot = oi % 2
            pan = wqp[:, slot * 2048:(slot + 1) * 2048]
            P.wait("g", c_t, st["slot_mm"][slot])
            psem = dS["qkvpan0" if slot == 0 else "qkvpan1"]
            src = d_wqkvT[:, oc * CH:(oc + 1) * CH].rearrange("(kc p) j -> p kc j", p=CH)
            P.op("g", lambda E, pan=pan, src=src: E.dma_start(
                out=pan.rearrange("p (kc j) -> p kc j", kc=16), in_=src),
                inc=psem, inck=16)
            pfq = PF[oi % 2]
            P.wait("t", psem, psem.n)
            P.wait("t", c_s, st["bank_ev"][oi % 2])
            for kc in range(16):
                P.op("t", lambda E, pfq=pfq, pan=pan, kc=kc, b=b: E.matmul(
                    pfq[:], pan[:, kc * CH:(kc + 1) * CH],
                    xnT[b][:, kc * 512:(kc + 1) * 512],
                    start=(kc == 0), stop=(kc == 15)),
                    inc=(c_t if kc == 15 else None))
            st["slot_mm"][slot] = c_t.n
            if oc < 16:
                h = oc
                dst = qfm[:, h * TC + b * 512: h * TC + b * 512 + 512]
            elif oc < 32:
                h = oc - 16
                dst = kvfm[:, h * 512: h * 512 + 512]
            else:
                h = oc - 32
                dst = kvfm[:, 8192 + h * 512: 8192 + h * 512 + 512]
            P.wait("s", c_t, c_t.n)
            P.op("s", lambda E, dst=dst, pfq=pfq, oc=oc: E.activation(
                dst, pfq[:], Act.Identity, bias=bq_t[:, oc:oc + 1]), inc=c_s)
            st["bank_ev"][oi % 2] = c_s.n

    KTS = smalls[:, 0:CH]
    VTM0 = smalls[:, CH:2 * CH]
    VTM1 = smalls[:, 2 * CH:3 * CH]
    SCT0 = smalls[:, 3 * CH:4 * CH]
    SCT1 = smalls[:, 4 * CH:5 * CH]
    SBF = smalls[:, 5 * CH:6 * CH]

    def p3a_block(b):
        for c_ in range(4):
            cg = b * 4 + c_
            for h in range(H):
                ksl = kvfm[:, h * 512 + c_ * CH: h * 512 + (c_ + 1) * CH]
                vsl = kvfm[:, 8192 + h * 512 + c_ * CH: 8192 + h * 512 + (c_ + 1) * CH]
                P.wait("t", c_s, c_s.n)
                P.op("t", lambda E, ksl=ksl: E.transpose(PBA[:, 0:CH], ksl, identb_t[:]),
                     inc=c_t)
                P.op("t", lambda E, vsl=vsl: E.transpose(PBB[:, 0:CH], vsl, identb_t[:]),
                     inc=c_t)
                P.wait("s", c_t, c_t.n)
                P.op("s", lambda E, h=h: E.activation(KTS[:], PBA[:, 0:CH], Act.Identity,
                                                      scale=dk_t[:, h:h + 1]), inc=c_s)
                P.op("s", lambda E: E.activation(VTM0[:], PBB[:, 0:CH], Act.Copy), inc=c_s)
                P.wait("t", c_s, c_s.n)
                P.wait("t", c_v, c_v.n)
                P.op("t", lambda E: E.matmul(PF[2][:, 0:CH], KTS[:], VTM0[:],
                                             start=True, stop=True), inc=c_t)
                P.wait("v", c_t, c_t.n)
                csl = cst[:, (cg * H + h) * CH: (cg * H + h + 1) * CH]
                P.op("v", lambda E, csl=csl: E.tensor_copy(csl, PF[2][:, 0:CH]))
                asl = f_qa[:, h * CH:(h + 1) * CH]
                if cg == 0:
                    P.op("v", lambda E, asl=asl: E.tensor_copy(asl, PF[2][:, 0:CH]),
                         inc=c_v)
                else:
                    P.op("v", lambda E, asl=asl, h=h: E.scalar_tensor_tensor(
                        asl, asl, lc_t[:, h:h + 1], PF[2][:, 0:CH],
                        op0=Alu.mult, op1=Alu.add), inc=c_v)

    def p3b_block(b):
        P.wait("t", dS["const"], dmask_done)
        P.wait("v", dS["const"], dmask_done)
        for c_ in range(4):
            cg = b * 4 + c_
            for h in range(H):
                ksl = kvfm[:, h * 512 + c_ * CH: h * 512 + (c_ + 1) * CH]
                vsl = kvfm[:, 8192 + h * 512 + c_ * CH: 8192 + h * 512 + (c_ + 1) * CH]
                qsl = qfm[:, h * TC + cg * CH: h * TC + (cg + 1) * CH]
                vtm = VTM0 if h % 2 == 0 else VTM1
                sct = SCT0 if h % 2 == 0 else SCT1
                P.wait("t", c_s, c_s.n)
                P.op("t", lambda E, vsl=vsl: E.transpose(PBB[:, 0:CH], vsl, identb_t[:]),
                     inc=c_t)
                P.wait("s", c_t, c_t.n)
                P.op("s", lambda E, vtm=vtm: E.activation(vtm, PBB[:, 0:CH], Act.Copy),
                     inc=c_s)
                P.wait("t", c_v, c_v.n)
                P.op("t", lambda E, ksl=ksl, qsl=qsl: E.matmul(
                    PF[3][:, 0:CH], ksl, qsl, start=True, stop=True), inc=c_t)
                P.wait("v", c_t, c_t.n)
                P.op("v", lambda E, h=h, sct=sct: E.tensor_tensor(
                    sct, PF[3][:, 0:CH], dmk[:, h * CH:(h + 1) * CH], op=Alu.mult),
                    inc=c_v)
                pfi = PF[4 + h % 2]
                P.wait("t", c_v, c_v.n)
                P.wait("t", c_s, c_s.n)
                P.op("t", lambda E, pfi=pfi, sct=sct, vtm=vtm: E.matmul(
                    pfi[:, 0:CH], sct, vtm, start=True, stop=True), inc=c_t)
                P.wait("v", c_t, c_t.n)
                P.wait("v", dS["intra"], dS["intra"].n)
                P.wait("v", dS["tern"], dS["tern"].n)
                P.op("v", lambda E, pfi=pfi, h=h: E.tensor_copy(
                    istg[:, h * CH:(h + 1) * CH], pfi[:, 0:CH]), inc=c_v)
            P.wait("g", c_v, c_v.n)
            P.op("g", lambda E, cg=cg: E.dma_start(
                out=d_intra.ap()[cg * CH:(cg + 1) * CH, :], in_=istg[:]),
                inc=dS["intra"], inck=16)

    for i in range(4):
        phase1_tile(i)
        if i < 3:
            absum_chunks(i)
    absum_finish()
    qkv_block(0, list(range(16, 48)))
    quant_pass2()
    p3a_block(0)
    qkv_block(0, list(range(0, 16)))
    p3b_block(0)
    for i in range(4, 8):
        phase1_tile(i)
    qkv_block(1, list(range(16, 48)))
    p3a_block(1)
    P.wait("g", c_v, c_v.n)
    P.op("g", lambda E: E.dma_start(out=d_Aloc.ap(), in_=f_qa[:]), inc=dS["aall"], inck=16)
    P.wait("g", dS["aall"], dS["aall"].n)
    P.op("g", lambda E: E.collective_compute(
        "AllGather", Alu.bypass, replica_groups=[list(range(NCORES))],
        ins=[d_Aloc.ap()], outs=[d_Aall.ap()]), inc=cc)
    aAG_done = cc.n
    P.wait("g", dS["tern"], meta["tern_done"])
    for din, dout in ((d_t1b, d_w1q), (d_t2b, d_w2q), (d_t3b, d_w3q)):
        P.op("g", lambda E, a=din, b=dout: E.collective_compute(
            "AllGather", Alu.bypass, replica_groups=[list(range(NCORES))],
            ins=[a.ap()], outs=[b.ap()]), inc=cc)
    meta["wq_ready"] = cc.n
    qkv_block(1, list(range(0, 16)))
    p3b_block(1)

    P.wait("v", c_s, c_s.n)
    P.op("v", lambda E: E.memset(f_sq[:], 0.0), inc=c_v)
    P.wait("y", cc, aAG_done)
    for j in range(7):
        P.wait("y", c_v, c_v.n)
        if j == 0:
            P.wait("y", dS["aall"], dS["aall"].n)
        dma("y", f_qa[:], d_Aall.ap()[j * CH:(j + 1) * CH, :], dS["aall"])
        P.wait("v", dS["aall"], dS["aall"].n)
        for h in range(H):
            P.op("v", lambda E, j=j, h=h: E.scalar_tensor_tensor(
                f_sq[:, h * CH:(h + 1) * CH], f_qa[:, h * CH:(h + 1) * CH],
                si_t[:, j * H + h: j * H + h + 1], f_sq[:, h * CH:(h + 1) * CH],
                op0=Alu.mult, op1=Alu.add), inc=(c_v if h == H - 1 else None))

    P.wait("g", c_v, c_v.n)
    P.op("g", lambda E: E.dma_start(out=dmk[:], in_=d_boutb), inc=dS["misc"], inck=16)
    boutb_done = dS["misc"].n

    wout_slots = [xnT0, xnT1, kvfm[:, 0:8192], kvfm[:, 8192:16384]]

    def load_wout_panel(oc):
        dstp = wout_slots[oc]
        psem = dS["woutp0"] if oc < 2 else dS["woutp1"]
        src = d_woutT[:, oc * 512:(oc + 1) * 512].rearrange("(kc p) j -> p kc j", p=CH)
        P.op("g", lambda E, dstp=dstp, src=src: E.dma_start(
            out=dstp[:].rearrange("p (kc j) -> p kc j", kc=16), in_=src),
            inc=psem, inck=16)

    P.wait("g", c_t, c_t.n)
    load_wout_panel(0)
    load_wout_panel(1)
    P.wait("g", c_t, c_t.n)
    load_wout_panel(2)
    load_wout_panel(3)
    wout_done = (dS["woutp0"].n, dS["woutp1"].n)

    scan_mm_last = c_t.n
    scan_ev = [c_v.n] * 4
    for j in range(NT):
        P.wait("y", c_v, c_v.n)
        if j == 0:
            P.wait("y", dS["intra"], dS["intra"].n)
        dma("y", istg[:], d_intra.ap()[j * CH:(j + 1) * CH, :], dS["t3r0"])
        P.wait("v", dS["t3r0"], dS["t3r0"].n)
        P.wait("v", dS["outx"], dS["outx"].n)
        SBF4 = smalls[:, 0:4 * CH]
        for hg in range(4):
            hs = [4 * hg + i for i in range(4)]
            P.wait("v", c_t, scan_mm_last)
            for i, h in enumerate(hs):
                P.op("v", lambda E, h=h, i=i: E.tensor_copy(
                    SBF4[:, i * CH:(i + 1) * CH], f_sq[:, h * CH:(h + 1) * CH]),
                    inc=(c_v if i == 3 else None))
            P.wait("t", c_v, c_v.n)
            P.wait("t", c_v, scan_ev[hg])
            for i, h in enumerate(hs):
                P.op("t", lambda E, i=i, h=h, j=j: E.matmul(
                    PF[i][:, 0:CH], qfm[:, h * TC + j * CH: h * TC + (j + 1) * CH],
                    SBF4[:, i * CH:(i + 1) * CH], start=True, stop=True),
                    inc=(c_t if i == 3 else None))
            scan_mm_last = c_t.n
            P.wait("v", c_t, scan_mm_last)
            for i, h in enumerate(hs):
                P.op("v", lambda E, i=i, h=h: E.scalar_tensor_tensor(
                    attnt[:, h * CH:(h + 1) * CH], PF[i][:, 0:CH], dq_t[:, h:h + 1],
                    istg[:, h * CH:(h + 1) * CH], op0=Alu.mult, op1=Alu.add))
                P.op("v", lambda E, h=h, j=j: E.scalar_tensor_tensor(
                    f_sq[:, h * CH:(h + 1) * CH], f_sq[:, h * CH:(h + 1) * CH],
                    lc_t[:, h:h + 1], cst[:, (j * H + h) * CH:(j * H + h + 1) * CH],
                    op0=Alu.mult, op1=Alu.add), inc=c_v)
            scan_ev[hg] = c_v.n
        attn_done = c_v.n
        P.wait("t", c_v, attn_done)
        for gq in range(4):
            P.wait("t", c_s, c_s.n)
            for kk in range(4):
                kc = gq * 4 + kk
                P.op("t", lambda E, kc=kc, kk=kk: E.transpose(
                    PBA[:, kk * CH:(kk + 1) * CH], attnt[:, kc * CH:(kc + 1) * CH],
                    identb_t[:]), inc=(c_t if kk == 3 else None))
            P.wait("s", c_t, c_t.n)
            P.op("s", lambda E, gq=gq: E.activation(
                afm[:].rearrange("p (g t) -> p g t", g=16)[:, gq * 4:gq * 4 + 4, :],
                PBA[:, 0:512].rearrange("p (g t) -> p g t", g=4), Act.Copy), inc=c_s)
        afm_done = c_s.n
        P.wait("t", c_s, afm_done)
        for oc in range(4):
            pfo = PF[2 + oc % 2]
            P.wait("t", c_v, c_v.n)
            if j == 0:
                P.wait("t", dS["woutp0"], wout_done[0])
                P.wait("t", dS["woutp1"], wout_done[1])
            wp_slot = wout_slots[oc]
            for kc in range(16):
                P.op("t", lambda E, pfo=pfo, wp_slot=wp_slot, kc=kc: E.matmul(
                    pfo[:], afm[:, kc * CH:(kc + 1) * CH],
                    wp_slot[:, kc * 512:(kc + 1) * 512],
                    start=(kc == 0), stop=(kc == 15)), inc=(c_t if kc == 15 else None))
            if oc == 0:
                P.wait("y", c_v, c_v.n)
                dma("y", f_x[:], d_x[j * CH:(j + 1) * CH, :], dS["x"])
                P.wait("v", dS["x"], dS["x"].n)
                P.wait("v", dS["misc"], boutb_done)
            P.wait("v", c_t, c_t.n)
            P.wait("v", dS["outx"], dS["outx"].n)
            P.op("v", lambda E, pfo=pfo, oc=oc: E.scalar_tensor_tensor(
                f_t1[:, oc * 512:(oc + 1) * 512], pfo[:], 1.0,
                f_x[:, oc * 512:(oc + 1) * 512], op0=Alu.mult, op1=Alu.add))
            P.op("v", lambda E, pfo=pfo, oc=oc: E.tensor_tensor(
                attnt[:, oc * 512:(oc + 1) * 512], f_t1[:, oc * 512:(oc + 1) * 512],
                dmk[:, oc * 512:(oc + 1) * 512], op=Alu.add), inc=c_v)
        P.wait("g", c_v, c_v.n)
        P.op("g", lambda E, j=j: E.dma_start(
            out=d_outx.ap()[j * CH:(j + 1) * CH, :], in_=attnt[:]),
            inc=dS["outx"], inck=16)

    P.barrier(bar)
    g3t = kvfm[:, 8192:16384]
    P.op("g", lambda E: E.dma_start(out=g3t, in_=d_g3b), inc=dS["g3b"], inck=16)
    g3_done = dS["g3b"].n

    y1T = qfm
    y2T = cst
    for br in range(2):
        ydram = d_y1 if br == 0 else d_y2
        ytgt = y1T if br == 0 else y2T
        for i in range(NT):
            P.wait("y", c_v, c_v.n)
            P.wait("y", dS["y8"], dS["y8"].n)
            dma("y", i8a[:], ydram.ap()[i * CH:(i + 1) * CH, :], dS["y8r"])
            P.wait("v", dS["y8r"], dS["y8r"].n)
            P.wait("v", c_t, c_t.n)
            P.op("v", lambda E: E.tensor_copy(f_t1[:], i8a[:]), inc=c_v)
            P.wait("t", c_v, c_v.n)
            for gq in range(4):
                pf = PF[4 + gq % 2]
                P.wait("t", c_s, c_s.n)
                for kk in range(4):
                    kc = gq * 4 + kk
                    P.op("t", lambda E, pf=pf, kk=kk, kc=kc: E.transpose(
                        pf[:, kk * CH:(kk + 1) * CH], f_t1[:, kc * CH:(kc + 1) * CH],
                        identf_t[:]), inc=(c_t if kk == 3 else None))
                P.wait("s", c_t, c_t.n)
                dstr = ytgt[:].rearrange("p (kc t) -> p kc t", kc=16)
                P.op("s", lambda E, pf=pf, gq=gq, i=i, dstr=dstr: E.activation(
                    dstr[:, gq * 4:gq * 4 + 4, i * CH:(i + 1) * CH],
                    pf[:].rearrange("p (j t) -> p j t", j=4), Act.Copy), inc=c_s)

    P.wait("v", c_s, meta["a_ready_s"])
    P.op("v", lambda E: E.tensor_single_scalar(cu_c[:], invs1[:], sc_t[:, 12:13],
                                               op=Alu.mult), flush=c_v)
    P.op("v", lambda E: E.tensor_single_scalar(cv_c[:], invs2[:], sc_t[:, 13:14],
                                               op=Alu.mult), inc=c_v)

    P.op("v", lambda E: E.memset(f_st3[:], 0.0), inc=c_v)
    P.op("v", lambda E: E.memset(f_st3[:, 128:256], -3.0e38))
    P.op("v", lambda E: E.memset(f_st3[:, 256:384], 3.0e38), inc=c_v)

    pan_slots = [xnT0, xnT1, kvfm[:, 0:8192]]
    HCH = f_xn[:, 0:512]
    SILUS = [f_xn[:, 512:1024], f_xn[:, 1536:2048]]
    T3S0 = attnt[:, 0:512]
    T3S1 = attnt[:, 512:1024]
    P.wait("y", cc, meta["wq_ready"])
    P.wait("t", bar, bar.n)
    P.wait("s", bar, bar.n)
    slot_sem_t = [0, 0, 0]
    ubank_ev = [c_s.n, c_s.n]
    vbank_ev = [c_v.n, c_v.n]
    silu_free = [c_v.n, c_v.n]

    pan_i = 0
    for oc in range(16):
        for w_i, dram in ((0, d_w1q), (1, d_w2q)):
            slot = pan_i % 3
            pan_i += 1
            P.wait("y", c_t, slot_sem_t[slot])
            src = dram.ap()[:, oc * 512:(oc + 1) * 512].rearrange(
                "(kc p) j -> p kc j", p=CH)
            P.op("y", lambda E, slot=slot, src=src: E.dma_start(
                out=pan_slots[slot][:].rearrange("p (kc j) -> p kc j", kc=16), in_=src),
                inc=dS["ffnpan"], inck=16)
        slot_u = (pan_i - 2) % 3
        slot_v = (pan_i - 1) % 3
        P.wait("t", dS["ffnpan"], dS["ffnpan"].n)
        if oc == 0:
            P.wait("s", dS["g3b"], g3_done)
            P.wait("v", dS["g3b"], g3_done)
            P.wait("v", dS["outx"], dS["outx"].n)
        for t_ in range(NT):
            pu = PF[t_ % 2]
            pv = PF[2 + t_ % 2]
            P.wait("t", c_s, ubank_ev[t_ % 2])
            for kc in range(16):
                P.op("t", lambda E, pu=pu, kc=kc, t_=t_, slot_u=slot_u: E.matmul(
                    pu[:], y1T[:, kc * TC + t_ * CH: kc * TC + (t_ + 1) * CH],
                    pan_slots[slot_u][:, kc * 512:(kc + 1) * 512],
                    start=(kc == 0), stop=(kc == 15)), inc=(c_t if kc == 15 else None))
            u_mm = c_t.n
            P.wait("t", c_v, vbank_ev[t_ % 2])
            for kc in range(16):
                P.op("t", lambda E, pv=pv, kc=kc, t_=t_, slot_v=slot_v: E.matmul(
                    pv[:], y2T[:, kc * TC + t_ * CH: kc * TC + (t_ + 1) * CH],
                    pan_slots[slot_v][:, kc * 512:(kc + 1) * 512],
                    start=(kc == 0), stop=(kc == 15)), inc=(c_t if kc == 15 else None))
            v_mm = c_t.n
            SILU = SILUS[t_ % 2]
            P.wait("s", c_t, u_mm)
            P.wait("s", c_v, silu_free[t_ % 2])
            P.op("s", lambda E, pu=pu, t_=t_, SILU=SILU: E.activation(
                SILU, pu[:], Act.Silu, scale=cu_c[:, t_:t_ + 1]), inc=c_s)
            ubank_ev[t_ % 2] = c_s.n
            P.wait("v", c_t, v_mm)
            P.wait("v", c_s, c_s.n)
            P.op("v", lambda E, pv=pv, t_=t_, SILU=SILU: E.scalar_tensor_tensor(
                HCH, pv[:], cv_c[:, t_:t_ + 1], SILU, op0=Alu.mult, op1=Alu.mult),
                inc=c_v)
            vbank_ev[t_ % 2] = c_v.n
            silu_free[t_ % 2] = c_v.n
            P.wait("s", c_v, c_v.n)
            P.op("s", lambda E: E.activation(f_xn[:, 1024:1536], HCH, Act.Square,
                                             accum_out=v_t0[:]), inc=c_s)
            P.wait("v", c_s, c_s.n)
            col = f_st3[:, t_ * 16 + oc: t_ * 16 + oc + 1]
            P.op("v", lambda E, col=col: E.tensor_copy(col, v_t0[:]), flush=c_v)
            t3s = T3S0 if oc % 2 == 0 else T3S1
            P.wait("v", dS["t3w"], dS["t3w"].n)
            P.op("v", lambda E, t3s=t3s, oc=oc: E.tensor_tensor(
                t3s, HCH, g3t[:, oc * 512:(oc + 1) * 512], op=Alu.mult))
            mxc = f_st3[:, 128 + t_ * 16 + oc: 128 + t_ * 16 + oc + 1]
            mnc = f_st3[:, 256 + t_ * 16 + oc: 256 + t_ * 16 + oc + 1]
            P.op("v", lambda E, t3s=t3s, mxc=mxc: E.reduce_max(mxc, t3s, axis=AXX),
                 flush=c_v)
            P.op("v", lambda E, t3s=t3s, mnc=mnc: E.tensor_reduce(mnc, t3s, axis=AXX,
                                                                  op=Alu.min), inc=c_v)
            P.wait("g", c_v, c_v.n)
            P.op("g", lambda E, t3s=t3s, t_=t_, oc=oc: E.dma_start(
                out=d_t3.ap()[t_ * CH:(t_ + 1) * CH, oc * 512:(oc + 1) * 512], in_=t3s),
                inc=dS["t3w"], inck=16)
        slot_sem_t[slot_u] = c_t.n
        slot_sem_t[slot_v] = c_t.n

    P.wait("v", c_s, c_s.n)
    for t_ in range(NT):
        P.op("v", lambda E, t_=t_: E.reduce_sum(v_ss[:], f_st3[:, t_ * 16:t_ * 16 + 16],
                                                axis=AXX), flush=c_v)
        P.op("v", lambda E, t_=t_: E.reduce_max(v_mx[:],
                                                f_st3[:, 128 + t_ * 16:128 + t_ * 16 + 16],
                                                axis=AXX), flush=c_v)
        P.op("v", lambda E, t_=t_: E.tensor_reduce(
            v_mn[:], f_st3[:, 256 + t_ * 16:256 + t_ * 16 + 16], axis=AXX, op=Alu.min),
            flush=c_v)
        P.op("v", lambda E: E.tensor_scalar(v_mn[:], v_mn[:], -1.0, None, op0=Alu.mult),
             flush=c_v)
        P.op("v", lambda E: E.tensor_tensor(v_m1[:], v_mx[:], v_mn[:], op=Alu.max),
             flush=c_v)
        P.op("v", lambda E: E.tensor_scalar(v_t1[:], v_ss[:], 1.0 / HID, 1e-8,
                                            op0=Alu.mult, op1=Alu.add), flush=c_v)
        P.op("v", lambda E: E.tensor_copy(v_t2[:], v_t1[:]), flush=c_v)
        newton_rsqrt(v_rr[:], v_t2[:])
        P.op("v", lambda E: E.tensor_tensor(v_d1[:], v_rr[:], v_m1[:], op=Alu.mult),
             flush=c_v)
        P.op("v", lambda E: E.tensor_scalar_max(v_d1[:], v_d1[:], 1e-4), flush=c_v)
        P.op("v", lambda E: E.reciprocal(v_t0[:], v_d1[:]), flush=c_v)
        P.op("v", lambda E, t_=t_: E.scalar_tensor_tensor(
            c3_c[:, t_:t_ + 1], v_rr[:], 127.0, v_t0[:], op0=Alu.mult, op1=Alu.mult),
            flush=c_v)
        P.op("v", lambda E, t_=t_: E.tensor_scalar(cy3_c[:, t_:t_ + 1], v_d1[:],
                                                   1.0 / 127.0, None, op0=Alu.mult),
             inc=c_v)
    P.wait("v", c_v, c_v.n)
    P.op("v", lambda E: E.tensor_single_scalar(cy3_c[:], cy3_c[:], sc_t[:, 14:15],
                                               op=Alu.mult), inc=c_v)

    P.barrier(bar)
    y3T = cst
    w3p = [wqp[:, 0:2048], wqp[:, 2048:4096]]
    OXB = dmk

    afm_use = [0, 0, 0, 0]
    ft1_use = [0, 0, 0, 0]
    pfq_ev = [0, 0]
    t3r_ld = [0, 0, 0, 0]
    for g_ in range(4):
        tiles = (2 * g_, 2 * g_ + 1)
        for ti, t_ in enumerate(tiles):
            for oc in range(16):
                it = (ti * 16 + oc) % 4
                asl = afm[:, it * 512:(it + 1) * 512]
                fsl = f_t1[:, it * 512:(it + 1) * 512]
                P.wait("y", c_v, afm_use[it])
                psem = dS["t3r%d" % it]
                dma("y", asl,
                    d_t3.ap()[t_ * CH:(t_ + 1) * CH, oc * 512:(oc + 1) * 512],
                    psem)
                P.wait("v", psem, psem.n)
                P.wait("v", c_t, ft1_use[it])
                P.op("v", lambda E, t_=t_, asl=asl, fsl=fsl: E.tensor_scalar(
                    fsl, asl, c3_c[:, t_:t_ + 1], MAGIC,
                    op0=Alu.mult, op1=Alu.add), inc=c_v)
                afm_use[it] = c_v.n
                q_ready = c_v.n
                pf = PF[4 + oc % 2]
                P.wait("t", c_v, q_ready)
                P.wait("t", c_s, pfq_ev[oc % 2])
                for kk in range(4):
                    P.op("t", lambda E, pf=pf, kk=kk, fsl=fsl: E.transpose(
                        pf[:, kk * CH:(kk + 1) * CH], fsl[:, kk * CH:(kk + 1) * CH],
                        identf_t[:]), inc=(c_t if kk == 3 else None))
                ft1_use[it] = c_t.n
                P.wait("s", c_t, ft1_use[it])
                P.op("s", lambda E, pf=pf: E.activation(
                    i16a[:], pf[:], Act.Copy, bias=-MAGIC))
                dstr = y3T[:].rearrange("p (kc t) -> p kc t", kc=64)
                P.op("s", lambda E, oc=oc, ti=ti, dstr=dstr: E.activation(
                    dstr[:, oc * 4:oc * 4 + 4, ti * CH:(ti + 1) * CH],
                    i16a[:].rearrange("p (j t) -> p j t", j=4), Act.Copy), inc=c_s)
                pfq_ev[oc % 2] = c_s.n
        y3r = y3T[:].rearrange("p (kc t) -> p kc t", kc=64)
        w3_slot_mm = [0, 0]
        w3_slot_ld = [0, 0]
        for sweep, ocs in ((0, (0, 1, 2)), (1, (3,))):
            for kc in range(64):
                slot = kc % 2
                P.wait("y", c_t, w3_slot_mm[slot])
                psem = dS["w3p0" if slot == 0 else "w3p1"]
                if sweep == 0:
                    P.wait("g", c_t, w3_slot_mm[slot])
                    src_a = d_w3q.ap()[kc * CH:(kc + 1) * CH, 0:768]
                    src_b = d_w3q.ap()[kc * CH:(kc + 1) * CH, 768:1536]
                    P.op("y", lambda E, slot=slot, src_a=src_a: E.dma_start(
                        out=w3p[slot][:, 0:768], in_=src_a), inc=psem, inck=16)
                    P.op("g", lambda E, slot=slot, src_b=src_b: E.dma_start(
                        out=w3p[slot][:, 768:1536], in_=src_b), inc=psem, inck=16)
                else:
                    src = d_w3q.ap()[kc * CH:(kc + 1) * CH, 1536:2048]
                    P.op("y", lambda E, slot=slot, src=src: E.dma_start(
                        out=w3p[slot][:, 0:512], in_=src), inc=psem, inck=16)
                P.wait("t", psem, psem.n)
                if kc == 0:
                    P.wait("t", c_v, c_v.n)
                    P.wait("t", c_s, c_s.n)
                nmm = 2 * len(ocs)
                mi = 0
                for ti in range(2):
                    for oi, oc in enumerate(ocs):
                        mi += 1
                        pf = PF[ti * 3 + oi] if sweep == 0 else PF[ti]
                        off = oi * 512
                        P.op("t", lambda E, pf=pf, slot=slot, off=off, kc=kc, ti=ti:
                             E.matmul(pf[:], y3r[:, kc, ti * CH:(ti + 1) * CH],
                                      w3p[slot][:, off:off + 512],
                                      start=(kc == 0), stop=(kc == 63)),
                             inc=(c_t if mi == nmm else None))
                w3_slot_mm[slot] = c_t.n
            mm_done = c_t.n
            for ti, t_ in enumerate(tiles):
                P.wait("y", c_v, c_v.n)
                if g_ == 0 and sweep == 0 and ti == 0:
                    P.wait("y", dS["misc"], dS["misc"].n)
                dma("y", OXB[:], d_outx.ap()[t_ * CH:(t_ + 1) * CH, :], dS["outxr"])
                P.wait("v", c_t, mm_done)
                P.wait("v", dS["outxr"], dS["outxr"].n)
                P.wait("v", dS["out"], dS["out"].n)
                for oi, oc in enumerate(ocs):
                    pf = PF[ti * 3 + oi] if sweep == 0 else PF[ti]
                    P.op("v", lambda E, pf=pf, t_=t_, oc=oc: E.scalar_tensor_tensor(
                        f_sq[:, oc * 512:(oc + 1) * 512], pf[:], cy3_c[:, t_:t_ + 1],
                        OXB[:, oc * 512:(oc + 1) * 512], op0=Alu.mult, op1=Alu.add),
                        inc=c_v)
                P.wait("y", c_v, c_v.n)
                for oi, oc in enumerate(ocs):
                    P.op("y", lambda E, t_=t_, oc=oc: E.dma_start(
                        out=d_out[t_ * CH:(t_ + 1) * CH, oc * 512:(oc + 1) * 512],
                        in_=f_sq[:, oc * 512:(oc + 1) * 512]), inc=dS["out"], inck=16)

    P.wait("y", dS["out"], dS["out"].n)
    P.wait("g", dS["out"], dS["out"].n)

    with nc.Block() as block:
        @block.tensor
        def _(E):
            for f in P.q["t"]:
                f(E)

        @block.vector
        def _(E):
            for f in P.q["v"]:
                f(E)

        @block.scalar
        def _(E):
            for f in P.q["s"]:
                f(E)

        @block.gpsimd
        def _(E):
            for f in P.q["g"]:
                f(E)

        @block.sync
        def _(E):
            for f in P.q["y"]:
                f(E)

    return nc


_BUILT = {}


def kernel(L0_attn_x, L0_attn_w_qkv, L0_attn_b_qkv, L0_attn_w_out, L0_attn_b_out,
           L0_ffn_w1, L0_ffn_g1, L0_ffn_w2, L0_ffn_g2, L0_ffn_w3, L0_ffn_g3):
    const = host_constants()
    if "nc" not in _BUILT:
        _BUILT["nc"] = build_nc(const)
    nc = _BUILT["nc"]

    x = np.ascontiguousarray(np.asarray(L0_attn_x, np.float32).reshape(8192, D))
    wqkvT = np.ascontiguousarray(np.asarray(L0_attn_w_qkv, np.float32).T)
    bqkv = np.asarray(L0_attn_b_qkv, np.float32)
    bqkv_fm = np.ascontiguousarray(bqkv.reshape(48, CH).T)
    woutT = np.ascontiguousarray(np.asarray(L0_attn_w_out, np.float32).T)
    boutb = np.ascontiguousarray(
        np.broadcast_to(np.asarray(L0_attn_b_out, np.float32)[None, :], (CH, D)))
    w1T = np.ascontiguousarray(np.asarray(L0_ffn_w1, np.float32).T)
    w2T = np.ascontiguousarray(np.asarray(L0_ffn_w2, np.float32).T)
    w3T = np.ascontiguousarray(np.asarray(L0_ffn_w3, np.float32).T)
    g12b = np.ascontiguousarray(np.broadcast_to(
        np.concatenate([np.asarray(L0_ffn_g1, np.float32),
                        np.asarray(L0_ffn_g2, np.float32)])[None, :], (CH, 2 * D)))
    g3b = np.ascontiguousarray(
        np.broadcast_to(np.asarray(L0_ffn_g3, np.float32)[None, :], (CH, HID)))

    in_maps = []
    for c in range(NCORES):
        m = {
            "x_slice": np.ascontiguousarray(x[c * TC:(c + 1) * TC]),
            "wqkvT": wqkvT,
            "bqkv_fm": bqkv_fm,
            "woutT": woutT,
            "bout_b": boutb,
            "w1T_shard": np.ascontiguousarray(w1T[c * 256:(c + 1) * 256]),
            "w2T_shard": np.ascontiguousarray(w2T[c * 256:(c + 1) * 256]),
            "w3T_shard": np.ascontiguousarray(w3T[c * 1024:(c + 1) * 1024]),
            "g12_b": g12b,
            "g3_b": g3b,
            "dmaskT": const["dmaskT"],
            "decayqcol": const["decayqcol"],
            "decaykb": const["decaykb"],
            "lamcb": const["lamcb"],
            "identf": const["identf"],
            "identb": const["identb"],
            "sinit_w": const["sinit"][c],
            "wnumel_inv": const["wnumel_inv"],
        }
        in_maps.append(m)

    res = run_bass_kernel_spmd(nc, in_maps, core_ids=list(range(NCORES)))
    out = np.concatenate([res.results[c]["out_slice"] for c in range(NCORES)], axis=0)
    return out.reshape(1, 8192, D)
